# version 27
# baseline (speedup 1.0000x reference)
from contextlib import ExitStack

import numpy as np
import ml_dtypes

import concourse.bass as bass
import concourse.mybir as mybir
from concourse.bass_utils import run_bass_kernel_spmd

F32 = mybir.dt.float32
BF16 = mybir.dt.bfloat16
F8 = mybir.dt.float8e4
AF = mybir.ActivationFunctionType
ALU = mybir.AluOpType
AX = mybir.AxisListType

E = 2048
H = 16
D = 128
NSEQ = 4096
OWN = 1024
NCH = 8
INVSQ = float(np.float32(1.0) / np.float32(np.sqrt(np.float32(128.0))))
EPS = 1e-5

ENGS = ("sync", "gpsimd", "vector", "scalar", "tensor")


class Step:
    __slots__ = ("eng", "emit", "waits", "inc_n", "seq", "val", "sem")

    def __init__(self, eng, emit, waits, inc_n, sem=None):
        self.eng = eng
        self.emit = emit
        self.waits = [w for w in waits if w is not None]
        self.inc_n = inc_n
        self.seq = None
        self.val = None
        self.sem = sem if sem is not None else eng


class Phase:
    def __init__(self, name):
        self.name = name
        self.steps = []

    def add(self, eng, emit, waits=(), inc=1, sem=None):
        s = Step(eng, emit, list(waits), inc, sem=sem)
        s.seq = len(self.steps)
        self.steps.append(s)
        return s

    def resolve(self, ctr0):
        ctr = dict(ctr0)
        for s in self.steps:
            for w in s.waits:
                if isinstance(w, Step):
                    assert w.val is not None, f"{self.name}: wait on unresolved step"
                    assert w.inc_n > 0, f"{self.name}: wait on inc-less step"
            ctr[s.sem] = ctr.get(s.sem, 0) + s.inc_n
            s.val = ctr[s.sem]
        return ctr

    def emit_engine(self, engname, eng, sems):
        for s in self.steps:
            if s.eng != engname:
                continue
            needed = {}
            for w in s.waits:
                if isinstance(w, Step):
                    key, val = w.sem, w.val
                else:
                    key, val = w
                if needed.get(key, -1) < val:
                    needed[key] = val
            for key, val in sorted(needed.items()):
                eng.wait_ge(sems[key], val)
            instr = s.emit(eng)
            if s.inc_n:
                assert instr is not None, f"{self.name}: step returned no instr"
                instr.then_inc(sems[s.sem], s.inc_n)


def build_nc(debug=False):
    nc = bass.Bass("TRN2", target_bir_lowering=False, debug=False)

    xp_h = nc.dram_tensor("xp", [NSEQ, E], F32, kind="ExternalInput").ap()
    xo_h = nc.dram_tensor("xo", [OWN, E], F32, kind="ExternalInput").ap()
    wq_h = nc.dram_tensor("w_q", [E, E], F32, kind="ExternalInput").ap()
    wkv_h = nc.dram_tensor("w_kv", [E, 2 * D], F32, kind="ExternalInput").ap()
    bq_h = nc.dram_tensor("b_q", [D, H], F32, kind="ExternalInput").ap()
    bk_h = nc.dram_tensor("b_k", [D, 1], F32, kind="ExternalInput").ap()
    bv_h = nc.dram_tensor("b_v", [D, 1], F32, kind="ExternalInput").ap()
    wfc_h = nc.dram_tensor("w_fc", [E, E], F32, kind="ExternalInput").ap()
    bfc_h = nc.dram_tensor("b_fc", [1, E], F32, kind="ExternalInput").ap()
    wT_h = nc.dram_tensor("w_ffnT", [E, E], BF16, kind="ExternalInput").ap()
    id_h = nc.dram_tensor("ident", [128, 128], F32, kind="ExternalInput").ap()
    idb_h = nc.dram_tensor("identb", [128, 128], BF16, kind="ExternalInput").ap()
    id8_h = nc.dram_tensor("ident8", [128, 128], F8, kind="ExternalInput").ap()
    onesR_h = nc.dram_tensor("onesR", [1, 128], BF16, kind="ExternalInput").ap()
    onesC_h = nc.dram_tensor("onesC", [128, 1], BF16, kind="ExternalInput").ap()
    onesRf_h = nc.dram_tensor("onesRf", [1, 128], F32, kind="ExternalInput").ap()
    ones512_h = nc.dram_tensor("ones512", [1, 512], F32, kind="ExternalInput").ap()
    mask_h = nc.dram_tensor("masks", [128, 4 * 512], BF16, kind="ExternalInput").ap()
    y_h = nc.dram_tensor("y", [OWN, E], F32, kind="ExternalOutput").ap()
    cc_in = nc.dram_tensor("cc_in", [1, 1], F32, kind="Internal").ap()
    rbd_h = nc.dram_tensor("rb_scratch", [4, 512], F32, kind="Internal").ap()
    scb_h = nc.dram_tensor("sc_scratch", [1, 2], F32, kind="Internal").ap()
    cc_out = nc.dram_tensor("cc_out", [1, 1], F32, kind="Internal").ap()

    dbg = {}
    if debug:
        for nm, shp, dt in (("KT", [128, NSEQ], BF16), ("V", [128, NSEQ], BF16),
                            ("qT", [128, H * 1024], BF16),
                            ("attnT", [128, H * 1024], BF16),
                            ("x1", [128, 8 * E], F32),
                            ("stat", [1, 32], F32),
                            ("lnqT", [128, 16 * 1024], BF16)):
            dbg[nm] = nc.dram_tensor("dbg_" + nm, shp, dt,
                                     kind="ExternalOutput").ap()

    with ExitStack() as outer:
        ent = outer.enter_context
        idT = ent(nc.sbuf_tensor("idT", [128, 128], F32))
        idB = ent(nc.sbuf_tensor("idB", [128, 128], BF16))
        id8 = ent(nc.sbuf_tensor("id8", [128, 128], F8))
        onesR = ent(nc.sbuf_tensor("onesRs", [1, 128], BF16))
        onesC = ent(nc.sbuf_tensor("onesCs", [128, 1], BF16))
        onesRf = ent(nc.sbuf_tensor("onesRfs", [1, 128], F32))
        ones512 = ent(nc.sbuf_tensor("ones512s", [1, 512], F32))
        bq = ent(nc.sbuf_tensor("bqs", [128, H], F32))
        bk = ent(nc.sbuf_tensor("bks", [128, 1], F32))
        bvb = ent(nc.sbuf_tensor("bvbs", [128, 1], F32))
        bfcb = ent(nc.sbuf_tensor("bfcbs", [1, E], BF16))
        kmax8 = ent(nc.sbuf_tensor("kmax8", [1, 8], F32))
        pmax = ent(nc.sbuf_tensor("pmax", [1, 8], F32))
        mq2 = ent(nc.sbuf_tensor("mq2", [1, 2], F32))
        s11 = ent(nc.sbuf_tensor("s11", [1, 32], F32))
        bB = ent(nc.sbuf_tensor("bB", [128, 2], F32))
        colv = ent(nc.sbuf_tensor("colv", [128, 4], F32))
        amw = ent(nc.sbuf_tensor("amw", [128, 1], F32))
        amx = ent(nc.sbuf_tensor("amx", [128, 1], F32))
        mus = ent(nc.sbuf_tensor("mus", [128, 16], F32))
        stv = ent(nc.sbuf_tensor("stv", [128, 200], F32))
        sKVQ = ent(ExitStack())
        KT = sKVQ.enter_context(nc.sbuf_tensor("KT", [128, NSEQ], BF16))
        V = sKVQ.enter_context(nc.sbuf_tensor("V", [128, NSEQ], BF16))
        qT = sKVQ.enter_context(nc.sbuf_tensor("qT", [128, H * 1024], BF16))

        DMA_SEMS = ["dconst", "dxr0", "dxr1", "dxr2", "dxr3", "dxr4", "dxr5", "dxr6", "dxr7", "dwkv", "dwq",
                    "dbvb", "dbfcb", "dmask", "dwfc0", "dwfc1",
                    "dxo0", "dxo1", "dxo2", "dxo3", "dwt0", "dwt1", "dwt2", "dwt3",
                    "dwj0", "dwj1", "dos0", "dos1", "dos2", "dos3",
                    "dcci", "dcco", "ddbg", "dbr0", "dbr1", "dbo0", "dbo1",
                    "dwq0", "dwq1", "dwq2", "dev0", "dev1"]
        sems = {n: ent(nc.semaphore("sem_" + n)) for n in list(ENGS) + DMA_SEMS}
        ctr = {n: 0 for n in sems}

        def run_phase(ph, switch=None):
            nonlocal ctr
            end_ctr = ph.resolve(ctr)
            arm_end = None
            if switch is not None:
                for c in range(4):
                    e = switch[c].resolve(end_ctr)
                    if arm_end is None:
                        arm_end = e
                    assert e == arm_end, f"arm {c}: {e} != {arm_end}"
            with nc.Block(ph.name, no_gpsimd_drain=True) as blk:
                def mk(engname):
                    def emitter(eng):
                        ph.emit_engine(engname, eng, sems)
                        if switch is not None and engname in ("tensor", "scalar", "vector", "gpsimd", "sync"):
                            rc = eng.partition_id() % 4
                            for c in nc.Switch(engines=[eng], index=[rc], n=4):
                                switch[c].emit_engine(engname, eng, sems)
                    return emitter
                blk.sync(mk("sync"))
                blk.gpsimd(mk("gpsimd"))
                blk.vector(mk("vector"))
                blk.scalar(mk("scalar"))
                blk.tensor(mk("tensor"))
            ctr = arm_end if switch is not None else end_ctr

        def dma(dst, src):
            return lambda eng: eng.dma_start(out=dst, in_=src)

        with ExitStack() as sA:
            wkv = sA.enter_context(nc.sbuf_tensor("wkv", [128, 16 * 256], BF16, side="right"))
            wqs = sA.enter_context(nc.sbuf_tensor("wqs", [128, 3 * E], BF16, side="right"))
            xch = sA.enter_context(nc.sbuf_tensor("xch", [128, 8 * E], F32, side="right"))
            lnB = sA.enter_context(nc.sbuf_tensor("lnB", [128, 4 * E], BF16, side="right"))
            sqJ = sA.enter_context(nc.sbuf_tensor("sqJ", [128, E], BF16, side="right"))
            lnT = sA.enter_context(nc.sbuf_tensor("lnT", [128, 2 * 16 * 512], BF16, side="right"))
            k2s = sA.enter_context(nc.sbuf_tensor("k2s", [128, 1024], BF16, side="right"))
            VTs = sA.enter_context(nc.sbuf_tensor("VTs", [128, 512], BF16, side="right"))

            p0 = Phase("p0")
            ld_const = []
            for dst, srcap in ((idT[:], id_h[:, :]), (idB[:], idb_h[:, :]),
                               (id8[:], id8_h[:, :]),
                               (onesR[:], onesR_h[:, :]),
                               (onesC[:], onesC_h[:, :]), (onesRf[:], onesRf_h[:, :]),
                               (ones512[:], ones512_h[:, :]),
                               (bq[:], bq_h[:, :]), (bk[:], bk_h[:, :])):
                ld_const.append(p0.add("sync", dma(dst, srcap), inc=16,
                                        sem="dconst"))
            ld_bvb = p0.add("gpsimd", dma(bvb[:], bv_h[:, :]), inc=16,
                            sem="dbvb")
            ld_bfcb = p0.add("gpsimd", dma(bfcb[:], bfc_h[:, :]), inc=16,
                             sem="dbfcb")
            ld_wkv = p0.add("gpsimd", dma(
                wkv[:].rearrange("p (fc j) -> p fc j", fc=16),
                wkv_h.rearrange("(fc p) j -> p fc j", p=128)), inc=16,
                sem="dwkv")
            wq_rj = wq_h.rearrange("(fc p) j -> p fc j", p=128)
            run_phase(p0)

            with nc.psum_tensor("psT0", [128, 1024], BF16) as psT0, \
                 nc.psum_tensor("psT1", [128, 1024], BF16) as psT1, \
                 nc.psum_tensor("psKV0", [128, 512], F32) as psKV0, \
                 nc.psum_tensor("psKV1", [128, 512], F32) as psKV1, \
                 nc.psum_tensor("psK0", [128, 512], F32) as psK0, \
                 nc.psum_tensor("psK1", [128, 512], F32) as psK1, \
                 nc.psum_tensor("psVT", [128, 512], BF16) as psVT, \
                 nc.psum_tensor("psk2", [1, 512], F32) as psk2:
                p1 = Phase("p1")
                st1 = {"psT": (psT0, psT1), "psKVs": (psKV0, psKV1),
                       "psKs": (psK0, psK1), "psVT": psVT, "psk2": psk2,
                       "kev_par": {0: None, 1: None},
                       "vte_par": {0: None, 1: None}}
                wqld = {}
                wq_slot_rd = {}

                def emit_wq_ld(ph, jc):
                    slot = jc % 3
                    wl = [wq_slot_rd[slot]] if wq_slot_rd.get(slot) else []
                    ld = ph.add("gpsimd", dma(
                        wqs[:, slot * E:(slot + 1) * E]
                        .rearrange("p (fc j) -> p fc j", fc=16),
                        wq_rj[:, :, jc * 128:(jc + 1) * 128]),
                        waits=wl, inc=16, sem="dwq%d" % slot)
                    wqld[jc] = ld
                    return ld

                def emit_tailA(ph, info):
                    state = st1
                    par = info["pos"] % 2
                    ch = info["ch"]
                    kev = ph.add("scalar", lambda s, ch=ch, info=info:
                                 s.activation(
                                     KT[:, ch * 512:(ch + 1) * 512],
                                     info["bankK"][:], AF.Identity,
                                     bias=bk[:, 0:1]),
                                 waits=[info["kmm"], ld_const[-1]])
                    state["kev_par"][par] = kev
                    info["kev"] = kev
                    vte_w = [info["vtm"], ld_bvb]
                    if state.get("vtr") is not None:
                        vte_w.append(state["vtr"])
                    vte = ph.add("scalar", lambda s, info=info: s.activation(
                        VTs[:, :], info["bankV"][:], AF.Identity,
                        bias=bvb[:, 0:1]), waits=vte_w)
                    state["vte_par"][par] = vte
                    info["vte"] = vte

                def emit_tailB(ph, info):
                    state = st1
                    ch = info["ch"]
                    vtr_w = [info["vte"], ld_const[-1]]
                    if state.get("vev") is not None:
                        vtr_w.append(state["vev"])
                    def vtr(tn):
                        last = None
                        for q in range(4):
                            last = tn.transpose(
                                state["psVT"][:, q * 128:(q + 1) * 128],
                                VTs[:, q * 128:(q + 1) * 128], idB[:])
                        return last
                    state["vtr"] = ph.add("tensor", vtr, waits=vtr_w)
                    state["vev"] = ph.add("vector", lambda v, ch=ch: v.tensor_copy(
                        V[:, ch * 512:(ch + 1) * 512], state["psVT"][:]),
                        waits=[state["vtr"]])
                    ksqw = [info["kev"]]
                    if state.get("k2red") is not None:
                        ksqw.append(state["k2red"])
                    ksq = ph.add("scalar", lambda s, ch=ch: s.activation(
                        k2s[:, (ch % 2) * 512:(ch % 2 + 1) * 512],
                        KT[:, ch * 512:(ch + 1) * 512], AF.Square), waits=ksqw)
                    k2w = [ksq, ld_const[-1]]
                    if state.get("k2red") is not None:
                        k2w.append(state["k2red"])
                    k2m = ph.add("tensor", lambda tn, ch=ch: tn.matmul(
                        state["psk2"][0:1, 0:512], onesC[:],
                        k2s[:, (ch % 2) * 512:(ch % 2 + 1) * 512],
                        start=True, stop=True), waits=k2w)
                    state["k2red"] = ph.add("vector", lambda v, ch=ch: v.reduce_max(
                        kmax8[0:1, ch:ch + 1], state["psk2"][0:1, 0:512], AX.X),
                        waits=[k2m])

                def emit_p1():
                    state = st1
                    info = {}
                    for pos in range(NCH):
                        state["lnT_consumers"] = (
                            [state["kmm"], state["vtm"]] if pos > 0 else [])
                        emit_one_chunk(p1, xp_h, pos, state, do_kv=True)
                        info[pos] = {"pos": pos, "ch": state["ch"],
                                     "kmm": state["kmm"], "vtm": state["vtm"],
                                     "bankK": state["psKs"][pos % 2],
                                     "bankV": state["psKVs"][pos % 2]}
                        if pos >= 2:
                            emit_tailB(p1, info[pos - 2])
                        if pos >= 1:
                            emit_tailA(p1, info[pos - 1])
                        if pos == 4:
                            for jc in range(3):
                                emit_wq_ld(p1, jc)
                        if pos == 6:
                            st1["sv_mu"] = p1.add(
                                "vector", lambda v: v.tensor_copy(
                                    stv[:, 140:144], stv[:, 8:12]),
                                waits=[state["c11"]])
                            st1["sv_r"] = p1.add(
                                "vector", lambda v: v.tensor_copy(
                                    stv[:, 148:152], stv[:, 12:16]),
                                waits=[state["c11"]])
                            st1["ldc_6"] = list(state["ldc_chunk"])
                    emit_tailB(p1, info[6])
                    emit_tailA(p1, info[7])
                    emit_tailB(p1, info[7])
                    st1["lnT_evs_7"] = list(state["lnT_evs"])

                def emit_one_chunk(ph, src_h, pos, state, do_kv):
                    tr_last = state.setdefault("tr_last", {0: None, 1: None,
                                                           2: None, 3: None})
                    bank_ev = state.setdefault("bank_ev", [None, None])
                    sq_last = state.setdefault("sq_last", [None, None])
                    ring_cons = state.setdefault("ring_cons", [None] * 8)
                    slotdone = state.setdefault("slotdone", [[], []])
                    ch = idx = pos
                    par = idx % 2
                    base = par * 32
                    sums4 = stv[:, base + 0:base + 4]
                    sumsq4 = stv[:, base + 4:base + 8]
                    mu4 = stv[:, base + 8:base + 12]
                    r4 = stv[:, base + 12:base + 16]
                    t4 = stv[:, base + 16:base + 20]
                    u4 = stv[:, base + 20:base + 24]
                    r04 = stv[:, base + 24:base + 28]
                    tmp4 = stv[:, base + 28:base + 32]
                    war = list(slotdone[par])
                    stats_steps = []
                    lnT_evs_chunk = []
                    vmm_chunk = []
                    xins = []
                    ldcs = []
                    for ti in range(4):
                        rs = (idx * 4 + ti) % 8
                        row0 = idx * 512 + ti * 128
                        wl = [ring_cons[rs]] if ring_cons[rs] else []
                        halves = []
                        for hh in range(2):
                            halves.append(ph.add("sync", dma(
                                xch[:, rs * E + hh * 1024:
                                    rs * E + (hh + 1) * 1024],
                                src_h[row0:row0 + 128,
                                      hh * 1024:(hh + 1) * 1024]),
                                waits=wl, inc=16,
                                sem="dxr%d" % ((idx * 4 + ti + 4 * hh) % 8)))
                        ldcs.append(halves)
                        xin = xch[:, rs * E:(rs + 1) * E]
                        xins.append(xin)
                        st_sq = ph.add("scalar", lambda s, xin=xin, ti=ti:
                                       s.activation(
                                           sqJ[:, :], xin, AF.Square,
                                           accum_out=stv[:, base + 4 + ti:
                                                         base + 5 + ti]),
                                       waits=halves + war +
                                             ([sq_last[0]] if sq_last[0] else []))
                        sq_last[0] = st_sq
                        vs = ph.add("vector", lambda v, xin=xin, ti=ti:
                                    v.tensor_reduce(
                                        stv[:, base + ti:base + 1 + ti],
                                        xin, AX.X, ALU.add),
                                    waits=halves + war)
                        stats_steps.append((st_sq, vs))
                    state["ldc_chunk"] = ldcs
                    c1 = ph.add("vector", lambda v: v.tensor_scalar(
                        mu4, sums4, 1.0 / 2048.0, None, ALU.mult),
                        waits=[s[1] for s in stats_steps] + war)
                    c2 = ph.add("vector", lambda v: v.tensor_scalar(
                        t4, sumsq4, 1.0 / 2048.0, None, ALU.mult),
                        waits=[s[0] for s in stats_steps])
                    c3 = ph.add("vector", lambda v: v.tensor_tensor(
                        tmp4, mu4, mu4, ALU.mult), waits=[c1])
                    c4 = ph.add("vector", lambda v: v.tensor_tensor(
                        t4, t4, tmp4, ALU.subtract), waits=[c2, c3])
                    c5 = ph.add("vector", lambda v: v.tensor_scalar(
                        t4, t4, EPS, None, ALU.add), waits=[c4])
                    c6 = ph.add("vector", lambda v: v.reciprocal(u4, t4),
                                waits=[c5])
                    c7 = ph.add("scalar", lambda s: s.activation(
                        r04, u4, AF.Sqrt), waits=[c6])
                    c8 = ph.add("vector", lambda v: v.tensor_tensor(
                        tmp4, r04, r04, ALU.mult), waits=[c7])
                    c9 = ph.add("vector", lambda v: v.tensor_tensor(
                        tmp4, tmp4, t4, ALU.mult), waits=[c8])
                    c10 = ph.add("vector", lambda v: v.tensor_scalar(
                        tmp4, tmp4, -0.5, 1.5, ALU.mult, ALU.add), waits=[c9])
                    c11 = ph.add("vector", lambda v: v.tensor_tensor(
                        r4, r04, tmp4, ALU.mult), waits=[c10])
                    state["c11"] = c11
                    nm1 = ph.add("vector", lambda v: v.tensor_tensor(
                        tmp4, mu4, r4, ALU.mult), waits=[c11])
                    nm2 = ph.add("vector", lambda v: v.tensor_scalar(
                        tmp4, tmp4, -1.0, None, ALU.mult), waits=[nm1])
                    applies = []
                    for ti in range(4):
                        rs = (idx * 4 + ti) % 8
                        xin = xins[ti]
                        apw = [nm2, stats_steps[ti][0]] + ldcs[ti]
                        if tr_last[ti] is not None:
                            apw.append(tr_last[ti])
                        ap = ph.add("scalar", lambda s, xin=xin, ti=ti:
                                    s.activation(
                                        lnB[:, ti * E:(ti + 1) * E], xin,
                                        AF.Identity,
                                        bias=stv[:, base + 28 + ti:
                                                 base + 29 + ti],
                                        scale=stv[:, base + 12 + ti:
                                                  base + 13 + ti]), waits=apw)
                        ring_cons[rs] = ap
                        applies.append(ap)
                        for ss_ in range(2):
                            bank = state["psT"][ss_]
                            deps = [ap, ld_const[-1]]
                            if bank_ev[ss_] is not None:
                                deps.append(bank_ev[ss_])
                            def tre(tn, ti=ti, ss_=ss_, bank=bank):
                                last = None
                                for q in range(8):
                                    fc = ss_ * 8 + q
                                    last = tn.transpose(
                                        bank[:, q * 128:(q + 1) * 128],
                                        lnB[:, ti * E + fc * 128:
                                            ti * E + (fc + 1) * 128], idB[:])
                                return last
                            st = ph.add("tensor", tre, waits=deps)
                            evw = [st]
                            if ti == 0 and ss_ == 0:
                                evw += state.get("lnT_consumers", [])
                            lnbase = state.get("lnT_base", 0)
                            def eva(v, ss_=ss_, ti=ti, bank=bank, lnbase=lnbase):
                                dstv = lnT[:, lnbase:lnbase + 16 * 512].rearrange(
                                    "p (fc t) -> p fc t", fc=16)
                                return v.tensor_copy(
                                    dstv[:, ss_ * 8:(ss_ + 1) * 8,
                                         ti * 128:(ti + 1) * 128],
                                    bank[:].rearrange("p (q t) -> p q t", q=8))
                            e = ph.add("vector", eva, waits=evw)
                            bank_ev[ss_] = e
                            lnT_evs_chunk.append(e)
                            tr_last[ti] = st
                    slotdone[par] = list(applies)
                    state["lnT_evs"] = lnT_evs_chunk
                    state["ch"] = ch
                    if do_kv:
                        bankK = state["psKs"][par]
                        bankV = state["psKVs"][par]
                        kdeps = list(lnT_evs_chunk) + [ld_wkv]
                        if state["kev_par"][par] is not None:
                            kdeps.append(state["kev_par"][par])
                        def kmm(tn, bankK=bankK):
                            last = None
                            for fc in range(16):
                                last = tn.matmul(
                                    bankK[:],
                                    wkv[:, fc * 256: fc * 256 + 128],
                                    lnT[:, fc * 512:(fc + 1) * 512],
                                    start=(fc == 0), stop=(fc == 15))
                            return last
                        state["kmm"] = ph.add("tensor", kmm, waits=kdeps)
                        vdeps = list(lnT_evs_chunk) + [ld_wkv]
                        if state["vte_par"][par] is not None:
                            vdeps.append(state["vte_par"][par])
                        def vtm(tn, bankV=bankV):
                            last = None
                            for fc in range(16):
                                last = tn.matmul(
                                    bankV[:],
                                    wkv[:, fc * 256 + 128: fc * 256 + 256],
                                    lnT[:, fc * 512:(fc + 1) * 512],
                                    start=(fc == 0), stop=(fc == 15))
                            return last
                        state["vtm"] = ph.add("tensor", vtm, waits=vdeps)

                emit_p1()
                run_phase(p1)

                p2 = Phase("p2")
                tr_last = st1["tr_last"]
                bank_ev = st1["bank_ev"]
                och1_evs = []
                qev = {0: None, 1: None}
                qps = (psKV0, psKV1)
                qwar0 = {0: [st1["vte_par"][0]], 1: [st1["vte_par"][1]]}
                lnT_ready = {0: list(st1["lnT_evs_7"]), 1: och1_evs}
                qm_by_jc_och = {}
                qev_h = {}

                def emit_q(jc, och):
                    deps = list(lnT_ready[och]) + [wqld[jc]]
                    if qev[och] is not None:
                        deps.append(qev[och])
                    else:
                        deps += qwar0[och]
                    def qmm(tn, jc=jc, och=och):
                        last = None
                        base_l = och * 8192
                        for fc in range(16):
                            last = tn.matmul(
                                qps[och][:],
                                wqs[:, (jc % 3) * E + fc * 128:
                                    (jc % 3) * E + (fc + 1) * 128],
                                lnT[:, base_l + fc * 512:
                                    base_l + (fc + 1) * 512],
                                start=(fc == 0), stop=(fc == 15))
                        return last
                    qm = p2.add("tensor", qmm, waits=deps)
                    qm_by_jc_och[(jc, och)] = qm
                    qev[och] = p2.add(
                        "scalar", lambda s, jc=jc, och=och: s.activation(
                            qT[:, jc * 1024 + och * 512:
                               jc * 1024 + (och + 1) * 512],
                            qps[och][:], AF.Identity, bias=bq[:, jc:jc + 1]),
                        waits=[qm, ld_const[-1]])
                    qev_h.setdefault(jc, {})[och] = qev[och]
                    return qm

                def post_jc(jc):
                    wq_slot_rd[jc % 3] = qm_by_jc_och[(jc, 1)]
                    if jc + 3 < H:
                        emit_wq_ld(p2, jc + 3)

                emit_q(0, 0)
                emit_q(1, 0)
                for ti in range(4):
                    apw = [st1["sv_mu"], st1["sv_r"]] + st1["ldc_6"][ti]
                    if tr_last[ti] is not None:
                        apw.append(tr_last[ti])
                    ap = p2.add("vector", lambda v, ti=ti: v.tensor_scalar(
                        lnB[:, ti * E:(ti + 1) * E],
                        xch[:, ti * E:(ti + 1) * E],
                        stv[:, 140 + ti:141 + ti],
                        stv[:, 148 + ti:149 + ti],
                        ALU.subtract, ALU.mult), waits=apw)
                    for ss_ in range(2):
                        bank = (psT0, psT1)[ss_]
                        deps = [ap, ld_const[-1]]
                        if bank_ev[ss_] is not None:
                            deps.append(bank_ev[ss_])
                        def tre2(tn, ti=ti, ss_=ss_, bank=bank):
                            last = None
                            for q in range(8):
                                fc = ss_ * 8 + q
                                last = tn.transpose(
                                    bank[:, q * 128:(q + 1) * 128],
                                    lnB[:, ti * E + fc * 128:
                                        ti * E + (fc + 1) * 128], idB[:])
                            return last
                        st = p2.add("tensor", tre2, waits=deps)
                        def eva2(v, ss_=ss_, ti=ti, bank=bank):
                            dstv = lnT[:, 8192:16384].rearrange(
                                "p (fc t) -> p fc t", fc=16)
                            return v.tensor_copy(
                                dstv[:, ss_ * 8:(ss_ + 1) * 8,
                                     ti * 128:(ti + 1) * 128],
                                bank[:].rearrange("p (q t) -> p q t", q=8))
                        e = p2.add("vector", eva2, waits=[st])
                        bank_ev[ss_] = e
                        och1_evs.append(e)
                        tr_last[ti] = st
                q2st = {"mm": None, "red": None, "acc": None}

                def emit_q2(h):
                    sqw = [qev_h[h][0], qev_h[h][1]]
                    if q2st["mm"] is not None:
                        sqw.append(q2st["mm"])
                    q2sq = p2.add("scalar", lambda s, h=h: s.activation(
                        k2s[:, 0:1024], qT[:, h * 1024:(h + 1) * 1024],
                        AF.Square), waits=sqw)
                    for g in range(2):
                        w = [q2sq, ld_const[-1]]
                        if q2st["red"] is not None:
                            w.append(q2st["red"])
                        else:
                            w.append(st1["k2red"])
                        q2mm = p2.add("tensor", lambda tn, g=g: tn.matmul(
                            psk2[:], onesC[:], k2s[:, g * 512:(g + 1) * 512],
                            start=True, stop=True), waits=w)
                        q2st["mm"] = q2mm
                        rw = [q2mm]
                        if q2st["acc"] is not None:
                            rw.append(q2st["acc"])
                        red = p2.add("vector", lambda v: v.reduce_max(
                            s11[0:1, 0:1], psk2[0:1, :], AX.X), waits=rw)
                        if h == 0:
                            acc = p2.add("vector", lambda v, g=g: v.tensor_copy(
                                mq2[0:1, g:g + 1], s11[0:1, 0:1]), waits=[red])
                        else:
                            acc = p2.add("vector", lambda v, g=g: v.tensor_tensor(
                                mq2[0:1, g:g + 1], mq2[0:1, g:g + 1],
                                s11[0:1, 0:1], ALU.max), waits=[red])
                        q2st["red"] = red
                        q2st["acc"] = acc

                emit_q(0, 1)
                post_jc(0)
                emit_q(1, 1)
                post_jc(1)
                emit_q2(0)
                for jc in range(2, H):
                    emit_q(jc, 0)
                    emit_q(jc, 1)
                    post_jc(jc)
                    emit_q2(jc - 1)
                emit_q2(H - 1)
                if debug:
                    d1 = p2.add("sync", dma(dbg["KT"][:, :], KT[:]), inc=16, sem="ddbg")
                    d2 = p2.add("sync", dma(dbg["V"][:, :], V[:]), inc=16, sem="ddbg")
                    d3 = p2.add("sync", dma(dbg["qT"][:, :], qT[:]), inc=16, sem="ddbg")
                    p2.add("sync", lambda s: None, waits=[d3], inc=0)
                run_phase(p2)

        with ExitStack() as sB:
            sB1 = sB.enter_context(ExitStack())
            attnT = sB1.enter_context(nc.sbuf_tensor("attnT", [128, H * 1024], BF16, side="right"))
            wfcj = sB1.enter_context(nc.sbuf_tensor("wfcj", [128, 2 * 16 * 512], BF16, side="right"))
            m3 = sB1.enter_context(ExitStack())
            pT = m3.enter_context(nc.sbuf_tensor("pT", [128, 4 * 1024], BF16, side="right"))
            pacc = m3.enter_context(nc.sbuf_tensor("pacc", [128, 2 * 1024], BF16, side="right"))
            masks = m3.enter_context(nc.sbuf_tensor("masks_s", [128, 4 * 512], BF16, side="right"))
            rcs = m3.enter_context(nc.sbuf_tensor("rcs", [1, 4 * 512], F32, side="right"))
            rcsum = m3.enter_context(nc.sbuf_tensor("rcsum", [1, 4 * 512], F32, side="right"))
            aTu = m3.enter_context(nc.sbuf_tensor("aTu", [128, 4 * 512], BF16, side="right"))
            rbs = m3.enter_context(nc.sbuf_tensor("rbs", [128, 4 * 512], F32, side="right"))

            wfc_r = wfc_h.rearrange("(fc p) j -> p fc j", p=128)
            wfcj_r = wfcj[:].rearrange("p (b fc j) -> p b fc j", b=2, fc=16)

            with nc.psum_tensor("psS0", [128, 1024], F32) as psS0, \
                 nc.psum_tensor("psS1", [128, 1024], F32) as psS1, \
                 nc.psum_tensor("psA0", [128, 512], F32) as psA0, \
                 nc.psum_tensor("psA1", [128, 512], F32) as psA1, \
                 nc.psum_tensor("psR0", [128, 512], F32) as psR0, \
                 nc.psum_tensor("psR1", [128, 512], F32) as psR1:
                psS = (psS0, psS1)
                psA = (psA0, psA1)
                psR = (psR0, psR1)

                p3 = Phase("p3")
                ld_mask = p3.add("sync", dma(masks[:], mask_h[:, :]), inc=16,
                                 sem="dmask")
                ld_wfc01 = []
                for j in range(2):
                    ld_wfc01.append(p3.add("gpsimd", dma(
                        wfcj_r[:, j], wfc_r[:, :, j * 512:(j + 1) * 512]), inc=16,
                        sem="dwfc%d" % j))

                arms = []
                for c in range(4):
                    arm = Phase(f"p3arm{c}")
                    own = (c, 7 - c)
                    perm = [p for p in range(8) if p not in own] + [7 - c, c]
                    inv = {p: s for s, p in enumerate(perm)}
                    extents = (4 * (c + 1), 4 * (8 - c))
                    kc_lists = []
                    for g in range(2):
                        ephys = c if g == 0 else 7 - c
                        slots_nd = sorted(inv[p] for p in range(ephys))
                        kl = [s * 4 + q for s in slots_nd for q in range(4)]
                        kl += [inv[ephys] * 4 + q for q in range(4)]
                        kc_lists.append(kl)
                    bset = []
                    for g in range(2):
                        ephys = c if g == 0 else 7 - c
                        slots = sorted(inv[p] for p in range(ephys + 1))
                        m0 = arm.add("vector", lambda v, s0=slots[0], g=g:
                                     v.tensor_copy(s11[0:1, 14 + g:15 + g],
                                                   kmax8[0:1, s0:s0 + 1]))
                        for s in slots[1:]:
                            m0 = arm.add("vector", lambda v, s=s, g=g:
                                         v.tensor_tensor(
                                             s11[0:1, 14 + g:15 + g],
                                             s11[0:1, 14 + g:15 + g],
                                             kmax8[0:1, s:s + 1], ALU.max),
                                         waits=[m0])
                        b1 = arm.add("vector", lambda v, g=g:
                                     v.tensor_tensor(
                                         s11[0:1, 8 + g:9 + g], mq2[0:1, g:g + 1],
                                         s11[0:1, 14 + g:15 + g], ALU.mult),
                                     waits=[m0])
                        b2 = arm.add("scalar", lambda s, g=g: s.activation(
                            s11[0:1, 10 + g:11 + g], s11[0:1, 8 + g:9 + g],
                            AF.Sqrt, scale=1.0 / 128.0), waits=[b1])
                        b3 = arm.add("vector", lambda v, g=g: v.tensor_scalar(
                            s11[0:1, 12 + g:13 + g], s11[0:1, 10 + g:11 + g],
                            -1.01, -0.25, ALU.mult, ALU.add), waits=[b2])
                        b4 = arm.add("tensor", lambda tn, g=g: tn.matmul(
                            psR0[:, g:g + 1], onesRf[:],
                            s11[0:1, 12 + g:13 + g], start=True, stop=True),
                            waits=[b3, ld_const[-1]])
                        b5 = arm.add("vector", lambda v, g=g: v.tensor_copy(
                            bB[:, g:g + 1], psR0[:, g:g + 1]), waits=[b4])
                        bset.append(b5)

                    prev_eager = None
                    pending_lazy = None
                    lazy_hist = [None, None]
                    acc_war = [None, None]
                    last_accs = None
                    bseq = 0
                    for hp in range(8):
                        h0, h1 = 2 * hp, 2 * hp + 1
                        for g in range(2):
                            Eg = extents[g]
                            kcl = kc_lists[g]
                            exp_steps = [None] * Eg
                            post_steps = [None] * Eg
                            cons_steps = [None] * Eg
                            acc_steps = [None] * Eg
                            s2_steps = [None] * Eg

                            def mk_s2(kc, it, h0=h0, h1=h1, g=g):
                                def f(tn):
                                    b = psS[it % 2]
                                    tn.matmul(
                                        b[:, 0:512],
                                        KT[:, kc * 128:(kc + 1) * 128],
                                        qT[:, h0 * 1024 + g * 512: h0 * 1024 + (g + 1) * 512],
                                        start=True, stop=True)
                                    return tn.matmul(
                                        b[:, 512:1024],
                                        KT[:, kc * 128:(kc + 1) * 128],
                                        qT[:, h1 * 1024 + g * 512: h1 * 1024 + (g + 1) * 512],
                                        start=True, stop=True)
                                return f

                            def mk_cons(kc, it, Eg=Eg):
                                def f(tn):
                                    first = (it == 0)
                                    last = (it == Eg - 1)
                                    buf = it % 4
                                    tn.matmul(psA[0][:],
                                              V[:, kc * 128:(kc + 1) * 128],
                                              pT[:, buf * 1024: buf * 1024 + 512],
                                              start=first, stop=last)
                                    return tn.matmul(
                                        psA[1][:], V[:, kc * 128:(kc + 1) * 128],
                                        pT[:, buf * 1024 + 512: buf * 1024 + 1024],
                                        start=first, stop=last)
                                return f

                            def mk_acc(it, bpar):
                                def f(v):
                                    dst = pacc[:, bpar * 1024:
                                               bpar * 1024 + 1024]
                                    src = pT[:, (it % 4) * 1024:
                                             (it % 4) * 1024 + 1024]
                                    if it == 0:
                                        return v.tensor_copy(dst, src)
                                    return v.tensor_tensor(dst, dst, src,
                                                           ALU.add)
                                return f

                            def mk_rowsum(bpar):
                                def f(tn):
                                    tn.matmul(psR[0][0:1, :], onesC[:],
                                              pacc[:, bpar * 1024:
                                                   bpar * 1024 + 512],
                                              start=True, stop=True)
                                    return tn.matmul(
                                        psR[1][0:1, :], onesC[:],
                                        pacc[:, bpar * 1024 + 512:
                                             bpar * 1024 + 1024],
                                        start=True, stop=True)
                                return f

                            def mk_exp(it, g=g):
                                def f(s):
                                    return s.activation(
                                        pT[:, (it % 4) * 1024:(it % 4) * 1024 + 1024],
                                        psS[it % 2][:, 0:1024], AF.Exp,
                                        bias=bB[:, g:g + 1], scale=INVSQ)
                                return f

                            def mk_mask(it, m, half):
                                def f(v):
                                    base = (it % 4) * 1024 + half * 512
                                    return v.tensor_tensor(
                                        pT[:, base:base + 512],
                                        pT[:, base:base + 512],
                                        masks[:, m * 512:(m + 1) * 512], ALU.mult)
                                return f

                            def emit_eager(rs_last, av_last, lazy_prev2, par):
                                ev = []
                                for hh in (0, 1):
                                    w = [rs_last]
                                    if lazy_prev2 is not None:
                                        w.append(lazy_prev2["rec"][hh])
                                    ev.append(arm.add(
                                        "vector", lambda v, hh=hh, par=par:
                                        v.tensor_copy(
                                            rcsum[0:1, par * 1024 + hh * 512:
                                                  par * 1024 + (hh + 1) * 512],
                                            psR[hh][0:1, :]), waits=w))
                                for hh in (0, 1):
                                    w = [av_last]
                                    if lazy_prev2 is not None:
                                        w.append(lazy_prev2["mul"][hh])
                                    ev.append(arm.add(
                                        "scalar", lambda s, hh=hh, par=par:
                                        s.activation(
                                            aTu[:, par * 1024 + hh * 512:
                                                par * 1024 + (hh + 1) * 512],
                                            psA[hh][:], AF.Identity), waits=w))
                                return ev

                            def lazy_rec(bspec, out, hh):
                                h0b, h1b, gb, ev, lazy_prev2, par = bspec
                                o = par * 1024 + hh * 512
                                rs_ = rcsum[0:1, o:o + 512]
                                rc = rcs[0:1, o:o + 512]
                                rw = [ev[hh]]
                                if lazy_prev2 is not None:
                                    rw.append(lazy_prev2["dm1"][hh])
                                rec = arm.add("vector",
                                              lambda v, rc=rc, rs_=rs_:
                                              v.reciprocal(rc, rs_),
                                              waits=rw)
                                out["rec"].append(rec)

                            def lazy_dm1(bspec, out, hh):
                                h0b, h1b, gb, ev, lazy_prev2, par = bspec
                                o = par * 1024 + hh * 512
                                rc = rcs[0:1, o:o + 512]
                                di = par * 2 + hh
                                dw = [out["rec"][hh]]
                                if lazy_prev2 is not None:
                                    dw.append(lazy_prev2["dma"][hh])
                                dm1 = arm.add("sync",
                                              lambda g_, rc=rc, di=di:
                                              g_.dma_start(
                                                  out=rbd_h[di:di + 1, :],
                                                  in_=rc),
                                              waits=dw, inc=16,
                                              sem="dbo%d" % hh)
                                out["dm1"].append(dm1)

                            def lazy_stage2(bspec, out, hh):
                                h0b, h1b, gb, ev, lazy_prev2, par = bspec
                                o = par * 1024 + hh * 512
                                rb = rbs[:, o:o + 512]
                                di = par * 2 + hh
                                dw2 = [out["dm1"][hh]]
                                if lazy_prev2 is not None:
                                    dw2.append(lazy_prev2["mul"][hh])
                                dmb = arm.add("sync",
                                              lambda g_, rb=rb, di=di:
                                              g_.dma_start(
                                                  out=rb,
                                                  in_=rbd_h[di:di + 1, :]
                                                  .broadcast_to((128, 512))),
                                              waits=dw2, inc=16,
                                              sem="dbr%d" % hh)
                                out["dma"].append(dmb)

                            def lazy_stage3(bspec, out, hh):
                                h0b, h1b, gb, ev, lazy_prev2, par = bspec
                                hid = h0b if hh == 0 else h1b
                                o = par * 1024 + hh * 512
                                rb = rbs[:, o:o + 512]
                                au = aTu[:, o:o + 512]
                                mul = arm.add(
                                    "vector",
                                    lambda v, rb=rb, au=au, hid=hid, gb=gb:
                                    v.tensor_tensor(
                                        attnT[:, hid * 1024 + gb * 512:
                                              hid * 1024 + (gb + 1) * 512],
                                        au, rb, ALU.mult),
                                    waits=[out["dma"][hh], ev[2 + hh]])
                                out["mul"].append(mul)

                            def emit_lazy(bspec):
                                out = {"rec": [], "dm1": [], "dma": [], "mul": []}
                                for hh in (0, 1):
                                    lazy_rec(bspec, out, hh)
                                    lazy_dm1(bspec, out, hh)
                                for hh in (0, 1):
                                    lazy_stage2(bspec, out, hh)
                                for hh in (0, 1):
                                    lazy_stage3(bspec, out, hh)
                                return out

                            def mk_av_last(kc, it):
                                def f(tn):
                                    buf = it % 4
                                    first = (it == 0)
                                    tn.matmul(psA[0][:],
                                              V[:, kc * 128:(kc + 1) * 128],
                                              pT[:, buf * 1024: buf * 1024 + 512],
                                              start=first, stop=True)
                                    return tn.matmul(
                                        psA[1][:], V[:, kc * 128:(kc + 1) * 128],
                                        pT[:, buf * 1024 + 512: buf * 1024 + 1024],
                                        start=first, stop=True)
                                return f

                            for it in range(Eg):
                                kc = kcl[it]
                                deps = list(bset) if (hp == 0 and g == 0 and it == 0) else []
                                if it >= 2:
                                    deps.append(exp_steps[it - 2])
                                elif hp > 0 or g > 0:
                                    deps.append(last_exps[it - 2])
                                s2_steps[it] = arm.add("tensor", mk_s2(kc, it),
                                                       waits=deps)
                                edeps = [s2_steps[it]]
                                if it >= 4:
                                    edeps.append(cons_steps[it - 4])
                                    edeps.append(acc_steps[it - 4])
                                elif hp > 0 or g > 0:
                                    edeps.append(last_cons[it - 4])
                                    edeps.append(last_accs[it - 4])
                                if hp == 0 and g == 0 and it < 2:
                                    edeps += list(bset)
                                exp_steps[it] = arm.add("scalar", mk_exp(it),
                                                        waits=edeps)
                                post = exp_steps[it]
                                if it >= Eg - 4:
                                    m = it - (Eg - 4)
                                    arm.add("vector", mk_mask(it, m, 0),
                                            waits=[exp_steps[it], ld_mask])
                                    post = arm.add("vector", mk_mask(it, m, 1),
                                                   waits=[exp_steps[it], ld_mask])
                                post_steps[it] = post
                                aw = [post_steps[it]]
                                if it == 0:
                                    if acc_war[bseq % 2] is not None:
                                        aw.append(acc_war[bseq % 2])
                                else:
                                    aw.append(acc_steps[it - 1])
                                acc_steps[it] = arm.add(
                                    "vector", mk_acc(it, bseq % 2), waits=aw)
                                if pending_lazy is not None:
                                    if it == 1:
                                        lazy_out = {"rec": [], "dm1": [],
                                                    "dma": [], "mul": []}
                                        lazy_rec(pending_lazy, lazy_out, 0)
                                        lazy_dm1(pending_lazy, lazy_out, 0)
                                    elif it == 2:
                                        lazy_rec(pending_lazy, lazy_out, 1)
                                        lazy_dm1(pending_lazy, lazy_out, 1)
                                        lazy_stage2(pending_lazy, lazy_out, 0)
                                    elif it == 3:
                                        lazy_stage2(pending_lazy, lazy_out, 1)
                                        lazy_stage3(pending_lazy, lazy_out, 0)
                                        lazy_stage3(pending_lazy, lazy_out, 1)
                                        lazy_hist[pending_lazy[5]] = lazy_out
                                        pending_lazy = None
                                if it >= 1:
                                    cdeps = [post_steps[it - 1]]
                                    if it == 1 and prev_eager is not None:
                                        cdeps += prev_eager
                                    cons_steps[it - 1] = arm.add(
                                        "tensor", mk_cons(kcl[it - 1], it - 1),
                                        waits=cdeps)
                            av_last = arm.add("tensor", mk_av_last(kcl[Eg - 1],
                                                                   Eg - 1),
                                              waits=[post_steps[Eg - 1]])
                            cons_steps[Eg - 1] = av_last
                            rsw = [acc_steps[Eg - 1], ld_const[-1]]
                            if prev_eager is not None:
                                rsw += prev_eager[0:2]
                            rsm = arm.add("tensor", mk_rowsum(bseq % 2),
                                          waits=rsw)
                            acc_war[bseq % 2] = rsm
                            last_exps = exp_steps
                            last_cons = cons_steps
                            last_accs = acc_steps
                            par = bseq % 2
                            eager_ev = emit_eager(rsm, av_last,
                                                  lazy_hist[par], par)
                            prev_eager = eager_ev
                            pending_lazy = (h0, h1, g, eager_ev,
                                            lazy_hist[par], par)
                            bseq += 1
                    lazy_hist[pending_lazy[5]] = emit_lazy(pending_lazy)
                    pending_lazy = None
                    arms.append(arm)
                run_phase(p3, switch={c: arms[c] for c in range(4)})

                if debug:
                    pd = Phase("p3d")
                    dd = pd.add("sync", dma(dbg["attnT"][:, :], attnT[:]), inc=16,
                                sem="ddbg")
                    pd.add("sync", lambda s: None, waits=[dd], inc=0)
                    run_phase(pd)

            m3.close()
            sKVQ.close()
            x1 = sB.enter_context(nc.sbuf_tensor("x1", [128, 8 * E], F32))
            fst = sB.enter_context(nc.sbuf_tensor("fst", [128, 2 * 512], BF16))
            ost = sB.enter_context(nc.sbuf_tensor("ost", [128, 4 * 512], F32))
            m4 = sB1.enter_context(ExitStack())
            xos = m4.enter_context(nc.sbuf_tensor("xos", [128, 4 * 512], F32, side="right"))
            sqJ4 = m4.enter_context(nc.sbuf_tensor("sqJ4", [128, 512], F32, side="right"))
            wTs = m4.enter_context(nc.sbuf_tensor("wTs", [128, 4 * E], BF16, side="right"))
            xqb = sB.enter_context(nc.sbuf_tensor("xqb", [128, 2 * E], F32))
            with nc.psum_tensor("psF0", [128, 512], F32) as psF0, \
                 nc.psum_tensor("psF1", [128, 512], F32) as psF1, \
                 nc.psum_tensor("psF2", [128, 512], F32) as psF2, \
                 nc.psum_tensor("psF3", [128, 512], F32) as psF3, \
                 nc.psum_tensor("psW", [1, 128], F32) as psW:
                psF = (psF0, psF1, psF2, psF3)
                p4 = Phase("p4")
                ld_wfc = {0: ld_wfc01[0], 1: ld_wfc01[1]}
                evac = {}
                bank_prev = [None, None, None, None]
                xos_prev = [None, None, None, None]
                mm_by_j = {0: [], 1: [], 2: [], 3: []}
                stat_tail = {}
                stwar = [None] * 32
                sq4_prev = [None]
                wT_rc = wT_h.rearrange("(fc p) j -> fc p j", p=128)
                wam = {"buf": [None] * 4, "acc": None}

                def emit_wamax(fc):
                    wl = [wam["buf"][fc % 4]] if wam["buf"][fc % 4] else []
                    ldw = p4.add("sync", dma(
                        wTs[:, (fc % 4) * E:(fc % 4 + 1) * E], wT_rc[fc]),
                        waits=wl, inc=16, sem="dwt%d" % (fc % 4))
                    red = p4.add("vector", lambda v, fc=fc: v.tensor_reduce(
                        stv[:, 64:65], wTs[:, (fc % 4) * E:(fc % 4 + 1) * E],
                        AX.X, ALU.max, apply_absolute_value=True),
                        waits=[ldw] + ([wam["acc"]] if wam["acc"] else []))
                    if fc == 0:
                        acc = p4.add("vector", lambda v: v.tensor_copy(
                            amw[:, 0:1], stv[:, 64:65]), waits=[red])
                    else:
                        acc = p4.add("vector", lambda v: v.tensor_tensor(
                            amw[:, 0:1], amw[:, 0:1], stv[:, 64:65], ALU.max),
                            waits=[red, wam["acc"]])
                    wam["buf"][fc % 4] = red
                    wam["acc"] = acc

                n = 0
                for j in range(4):
                    if j >= 2:
                        ld_wfc[j] = p4.add("gpsimd", dma(
                            wfcj_r[:, j % 2],
                            wfc_r[:, :, j * 512:(j + 1) * 512]),
                            waits=list(mm_by_j[j - 2]), inc=16,
                            sem="dwfc%d" % (j % 2))
                    for tt in range(8):
                        xw = [xos_prev[n % 4]] if xos_prev[n % 4] else []
                        xd = p4.add("sync", dma(
                            xos[:, (n % 4) * 512:(n % 4 + 1) * 512],
                            xo_h[tt * 128:(tt + 1) * 128, j * 512:(j + 1) * 512]),
                            waits=xw, inc=16, sem="dxo%d" % (n % 4))
                        deps = [ld_wfc[j], xd]
                        if bank_prev[n % 4] is not None:
                            deps.append(bank_prev[n % 4])
                        def fcmm(tn, j=j, tt=tt, bank=psF[n % 4]):
                            last = tn.matmul(bank[:], onesR[:],
                                             bfcb[0:1, j * 512:(j + 1) * 512],
                                             start=True, stop=False)
                            for ec in range(16):
                                last = tn.matmul(
                                    bank[:],
                                    attnT[:, ec * 1024 + tt * 128: ec * 1024 + (tt + 1) * 128],
                                    wfcj[:, (j % 2) * 8192 + ec * 512:
                                         (j % 2) * 8192 + (ec + 1) * 512],
                                    start=False, stop=(ec == 15))
                            return last
                        mm = p4.add("tensor", fcmm,
                                    waits=deps + [ld_const[-1], ld_bfcb])
                        mm_by_j[j].append(mm)
                        ev = p4.add("vector",
                                    lambda v, j=j, tt=tt, n=n, bank=psF[n % 4]:
                                    v.scalar_tensor_tensor(
                                        x1[:, tt * E + j * 512: tt * E + (j + 1) * 512],
                                        bank[:], 1.0,
                                        xos[:, (n % 4) * 512:(n % 4 + 1) * 512],
                                        ALU.mult, ALU.add,
                                        accum_out=stv[:, tt * 4 + j:tt * 4 + j + 1]),
                                    waits=[mm, xd] +
                                          ([stwar[tt * 4 + j]] if stwar[tt * 4 + j]
                                           else []))
                        evac[(j, tt)] = ev
                        x1sl = x1[:, tt * E + j * 512: tt * E + (j + 1) * 512]
                        sq_p = p4.add("scalar", lambda s, x1sl=x1sl, j=j, tt=tt:
                                      s.activation(sqJ4[:, :], x1sl, AF.Square,
                                                   accum_out=stv[:, 32 + tt * 4 + j:
                                                                 33 + tt * 4 + j]),
                                      waits=[ev] + ([sq4_prev[0]] if sq4_prev[0]
                                                    else []))
                        sq4_prev[0] = sq_p
                        rmx = p4.add("vector", lambda v, x1sl=x1sl, j=j, tt=tt:
                                     v.tensor_reduce(
                                         stv[:, 136 + tt * 4 + j:137 + tt * 4 + j],
                                         x1sl, AX.X, ALU.max), waits=[ev])
                        rmn = p4.add("vector", lambda v, x1sl=x1sl, j=j, tt=tt:
                                     v.tensor_reduce(
                                         stv[:, 168 + tt * 4 + j:169 + tt * 4 + j],
                                         x1sl, AX.X, ALU.min), waits=[ev])
                        stat_tail[(j, tt)] = (sq_p, rmx, rmn)
                        stwar[tt * 4 + j] = None
                        bank_prev[n % 4] = ev
                        xos_prev[n % 4] = ev
                        if n < 16:
                            emit_wamax(n)
                        n += 1
                trw = p4.add("tensor", lambda tn: tn.transpose(
                    psW[0:1, :], amw[:, 0:1], idT[:]),
                    waits=[wam["acc"], ld_const[-1]])
                redw = p4.add("vector", lambda v: v.reduce_max(
                    s11[0:1, 1:2], psW[0:1, :], AX.X), waits=[trw])
                swst = p4.add("vector", lambda v: v.tensor_scalar(
                    s11[0:1, 2:3], s11[0:1, 1:2], 1.0 / 448.0, 1e-12,
                    ALU.mult, ALU.max), waits=[redw])
                rsw = p4.add("vector", lambda v: v.reciprocal(
                    s11[0:1, 3:4], s11[0:1, 2:3]), waits=[swst])
                i2sw = p4.add("vector", lambda v: v.tensor_scalar(
                    s11[0:1, 4:5], s11[0:1, 3:4], 0.5, None, ALU.mult),
                    waits=[rsw])
                bw = p4.add("tensor", lambda tn: tn.matmul(
                    psF0[:, 0:1], onesRf[:], s11[0:1, 4:5],
                    start=True, stop=True),
                    waits=[i2sw, evac[(0, 0)], bank_prev[0], ld_const[-1]])
                bwc = p4.add("vector", lambda v: v.tensor_copy(
                    colv[:, 1:2], psF0[:, 0:1]), waits=[bw])
                sums8 = stv[:, 72:80]
                sumsq8 = stv[:, 80:88]
                mx8 = stv[:, 88:96]
                mn8 = stv[:, 96:104]
                t8 = stv[:, 104:112]
                u8 = stv[:, 112:120]
                r08 = stv[:, 120:128]
                tmp8 = stv[:, 128:136]
                mu8 = mus[:, 0:8]
                r8 = mus[:, 8:16]
                combs = []
                for tt in range(8):
                    wsum = [evac[(j, tt)] for j in range(4)]
                    wtail = [stat_tail[(j, tt)] for j in range(4)]
                    combs.append(p4.add("vector", lambda v, tt=tt: v.tensor_reduce(
                        sums8[:, tt:tt + 1], stv[:, tt * 4:tt * 4 + 4], AX.X,
                        ALU.add), waits=wsum))
                    combs.append(p4.add("vector", lambda v, tt=tt: v.tensor_reduce(
                        sumsq8[:, tt:tt + 1], stv[:, 32 + tt * 4:36 + tt * 4],
                        AX.X, ALU.add), waits=[w[0] for w in wtail]))
                    combs.append(p4.add("vector", lambda v, tt=tt: v.tensor_reduce(
                        mx8[:, tt:tt + 1], stv[:, 136 + tt * 4:140 + tt * 4],
                        AX.X, ALU.max), waits=[w[1] for w in wtail]))
                    combs.append(p4.add("vector", lambda v, tt=tt: v.tensor_reduce(
                        mn8[:, tt:tt + 1], stv[:, 168 + tt * 4:172 + tt * 4],
                        AX.X, ALU.min), waits=[w[2] for w in wtail]))
                c1 = p4.add("vector", lambda v: v.tensor_scalar(
                    mu8, sums8, 1.0 / 2048.0, None, ALU.mult), waits=combs)
                c2 = p4.add("vector", lambda v: v.tensor_scalar(
                    t8, sumsq8, 1.0 / 2048.0, None, ALU.mult), waits=combs)
                c3 = p4.add("vector", lambda v: v.tensor_tensor(
                    tmp8, mu8, mu8, ALU.mult), waits=[c1])
                c4 = p4.add("vector", lambda v: v.tensor_tensor(
                    t8, t8, tmp8, ALU.subtract), waits=[c2, c3])
                c5 = p4.add("vector", lambda v: v.tensor_scalar(
                    t8, t8, EPS, None, ALU.add), waits=[c4])
                c6 = p4.add("vector", lambda v: v.reciprocal(u8, t8), waits=[c5])
                c7 = p4.add("scalar", lambda s: s.activation(
                    r08, u8, AF.Sqrt), waits=[c6])
                c8 = p4.add("vector", lambda v: v.tensor_tensor(
                    tmp8, r08, r08, ALU.mult), waits=[c7])
                c9 = p4.add("vector", lambda v: v.tensor_tensor(
                    tmp8, tmp8, t8, ALU.mult), waits=[c8])
                c10 = p4.add("vector", lambda v: v.tensor_scalar(
                    tmp8, tmp8, -0.5, 1.5, ALU.mult, ALU.add), waits=[c9])
                c11 = p4.add("vector", lambda v: v.tensor_tensor(
                    r8, r08, tmp8, ALU.mult), waits=[c10])
                a1 = p4.add("vector", lambda v: v.tensor_tensor(
                    mx8, mx8, mu8, ALU.subtract), waits=[c11])
                a2 = p4.add("vector", lambda v: v.tensor_tensor(
                    mn8, mu8, mn8, ALU.subtract), waits=[c11, a1])
                a3 = p4.add("vector", lambda v: v.tensor_tensor(
                    mx8, mx8, mn8, ALU.max), waits=[a2])
                a4 = p4.add("vector", lambda v: v.tensor_tensor(
                    mx8, mx8, r8, ALU.mult), waits=[a3])
                a5 = p4.add("vector", lambda v: v.tensor_reduce(
                    amx[:, 0:1], mx8, AX.X, ALU.max), waits=[a4])
                trx = p4.add("tensor", lambda tn: tn.transpose(
                    psW[0:1, :], amx[:, 0:1], idT[:]),
                    waits=[a5, redw, ld_const[-1]])
                redx = p4.add("vector", lambda v: v.reduce_max(
                    s11[0:1, 5:6], psW[0:1, :], AX.X), waits=[trx])
                nm8a = p4.add("vector", lambda v: v.tensor_tensor(
                    stv[:, 72:80], mu8, r8, ALU.mult), waits=[c11])
                nm8b = p4.add("vector", lambda v: v.tensor_scalar(
                    stv[:, 72:80], stv[:, 72:80], -1.0, None, ALU.mult),
                    waits=[nm8a])
                q1_steps = {}
                for tt in range(2):
                    q1_steps[tt] = p4.add(
                        "scalar", lambda s, tt=tt: s.activation(
                            xqb[:, (tt % 2) * E:(tt % 2 + 1) * E],
                            x1[:, tt * E:(tt + 1) * E], AF.Identity,
                            bias=stv[:, 72 + tt:73 + tt],
                            scale=mus[:, 8 + tt:9 + tt]), waits=[nm8b])
                if debug:
                    dd4 = p4.add("sync", dma(dbg["x1"][:, :], x1[:]),
                                 waits=[evac[(j, tt)] for j in range(4)
                                        for tt in range(8)], inc=16, sem="ddbg")
                    p4.add("sync", lambda s: None, waits=[dd4], inc=0)
                run_phase(p4)

            sB1.close()
            with ExitStack() as sC:
                lnqT = sC.enter_context(nc.sbuf_tensor("lnqT", [128, 16 * 1024], F8, side="right"))
                wq8 = sC.enter_context(nc.sbuf_tensor("wq8", [128, 4 * 16 * 512], F8, side="right"))
                wqj = sC.enter_context(nc.sbuf_tensor("wqj", [128, 2 * 16 * 512], BF16, side="right"))
                with nc.psum_tensor("psQ0", [128, 1024], F32) as psQ0, \
                     nc.psum_tensor("psQ1", [128, 1024], F32) as psQ1, \
                     nc.psum_tensor("psN0", [128, 512], F32) as psN0, \
                     nc.psum_tensor("psN1", [128, 512], F32) as psN1, \
                     nc.psum_tensor("psN2", [128, 512], F32) as psN2, \
                     nc.psum_tensor("psN3", [128, 512], F32) as psN3:
                    psQ = (psQ0, psQ1)
                    psN = (psN0, psN1, psN2, psN3)
                    p5 = Phase("p5")
                    wT_rj = wT_h.rearrange("(fc p) j -> p fc j", p=128)
                    wqj_r = wqj[:].rearrange("p (b fc j) -> p b fc j", b=2, fc=16)
                    ld_wTj = {}
                    wq_steps = {}
                    for j in range(4):
                        wl = [] if j < 2 else [wq_steps[j - 2]]
                        ld_wTj[j] = p5.add("sync", dma(
                            wqj_r[:, j % 2], wT_rj[:, :, j * 512:(j + 1) * 512]),
                            waits=wl, inc=16, sem="dwj%d" % (j % 2))
                        if j % 2 == 0:
                            wq1 = p5.add("scalar", lambda s, j=j: s.activation(
                                wq8[:, j * 8192:(j + 1) * 8192],
                                wqj[:, (j % 2) * 8192:(j % 2) * 8192 + 8192],
                                AF.Identity, scale=colv[:, 1:2]),
                                waits=[ld_wTj[j], bwc])
                        else:
                            wq1 = p5.add("vector", lambda v, j=j: v.tensor_scalar(
                                wq8[:, j * 8192:(j + 1) * 8192],
                                wqj[:, (j % 2) * 8192:(j % 2) * 8192 + 8192],
                                colv[:, 1:2], None, ALU.mult),
                                waits=[ld_wTj[j]])
                        wq_steps[j] = wq1
                    tox = p5.add("gpsimd", dma(cc_in[:, :], s11[0:1, 5:6]),
                                 waits=[redx], inc=16, sem="dcci")
                    cc = p5.add("gpsimd", lambda gp: gp.collective_compute(
                        "AllReduce", ALU.max, replica_groups=[list(range(8))],
                        ins=[cc_in[:, :]], outs=[cc_out[:, :]]), waits=[tox])
                    fromx = p5.add("gpsimd", dma(s11[0:1, 6:7], cc_out[:, :]),
                                   waits=[cc], inc=16, sem="dcco")
                    sxst = p5.add("vector", lambda v: v.tensor_scalar(
                        s11[0:1, 7:8], s11[0:1, 6:7], 1.0 / 448.0, 1e-12,
                        ALU.mult, ALU.max), waits=[fromx])
                    rsx = p5.add("vector", lambda v: v.reciprocal(
                        s11[0:1, 8:9], s11[0:1, 7:8]), waits=[sxst])
                    i2sx = p5.add("vector", lambda v: v.tensor_scalar(
                        s11[0:1, 9:10], s11[0:1, 8:9], 0.5, None, ALU.mult),
                        waits=[rsx])
                    sc4a = p5.add("vector", lambda v: v.tensor_tensor(
                        s11[0:1, 10:11], s11[0:1, 7:8], s11[0:1, 2:3], ALU.mult),
                        waits=[sxst, swst])
                    sc4b = p5.add("vector", lambda v: v.tensor_scalar(
                        s11[0:1, 11:12], s11[0:1, 10:11], 4.0, None, ALU.mult),
                        waits=[sc4a])
                    dmW1 = p5.add("sync", lambda g_: g_.dma_start(
                        out=scb_h[0:1, 0:1], in_=s11[0:1, 9:10]),
                        waits=[i2sx], inc=16, sem="dbo0")
                    dmW2 = p5.add("sync", lambda g_: g_.dma_start(
                        out=scb_h[0:1, 1:2], in_=s11[0:1, 11:12]),
                        waits=[sc4b], inc=16, sem="dbo1")
                    bxc = p5.add("sync", lambda g_: g_.dma_start(
                        out=colv[:, 0:1], in_=scb_h[0:1, 0:1]
                        .broadcast_to((128, 1))), waits=[dmW1], inc=16,
                        sem="dbr0")
                    bxc2 = p5.add("sync", lambda g_: g_.dma_start(
                        out=colv[:, 2:3], in_=scb_h[0:1, 1:2]
                        .broadcast_to((128, 1))), waits=[dmW2], inc=16,
                        sem="dbr1")
                    lnq_done = {}
                    tr_by_tt = {}
                    bank_ev5 = [None, None]
                    for tt in range(8):
                        if tt < 2:
                            q1 = q1_steps[tt]
                        else:
                            q1 = p5.add("scalar", lambda s, tt=tt:
                                        s.activation(
                                            xqb[:, (tt % 2) * E:
                                                (tt % 2 + 1) * E],
                                            x1[:, tt * E:(tt + 1) * E],
                                            AF.Identity,
                                            bias=stv[:, 72 + tt:73 + tt],
                                            scale=mus[:, 8 + tt:9 + tt]),
                                        waits=[nm8b, tr_by_tt[tt - 2]])
                        evs = []
                        for ss_ in range(2):
                            bank = psQ[ss_]
                            deps = [q1, ld_const[-1]]
                            if bank_ev5[ss_] is not None:
                                deps.append(bank_ev5[ss_])
                            def tre5(tn, ss_=ss_, tt=tt, bank=bank):
                                last = None
                                for q in range(8):
                                    fc = ss_ * 8 + q
                                    last = tn.transpose(
                                        bank[:, q * 128:(q + 1) * 128],
                                        xqb[:, (tt % 2) * E + fc * 128:
                                            (tt % 2) * E + (fc + 1) * 128],
                                        idT[:])
                                return last
                            st = p5.add("tensor", tre5, waits=deps)
                            tr_by_tt[tt] = st
                            def eva5(v, ss_=ss_, tt=tt, bank=bank):
                                dstv = lnqT[:].rearrange("p (fc t) -> p fc t",
                                                         fc=16)
                                return v.tensor_scalar(
                                    dstv[:, ss_ * 8:(ss_ + 1) * 8,
                                         tt * 128:(tt + 1) * 128],
                                    bank[:].rearrange("p (q t) -> p q t", q=8),
                                    colv[:, 0:1], None, ALU.mult)
                            e = p5.add("vector", eva5, waits=[st, bxc])
                            bank_ev5[ss_] = e
                            evs.append(e)
                        lnq_done[tt] = evs
                    out_dmas = []
                    ffn_ev = {}
                    mm_lastj = {}
                    nn = 0
                    for j in range(4):
                        for tt in range(8):
                            deps = [wq_steps[j]] + list(lnq_done[tt])
                            if nn >= 4:
                                deps.append(ffn_ev[nn - 4])
                            def fmm(tn, j=j, tt=tt, bank=psN[nn % 4]):
                                last = None
                                lq = lnqT[:].rearrange("p (fc t) -> p fc t",
                                                       fc=16)
                                wv_ = wq8[:].rearrange("p (b fc jj) -> p b fc jj",
                                                       b=4, fc=16)
                                for fp in range(8):
                                    last = tn.matmul(
                                        bank[:],
                                        lq[:, 2 * fp:2 * fp + 2,
                                           tt * 128:(tt + 1) * 128],
                                        wv_[:, j, 2 * fp:2 * fp + 2, :],
                                        start=(fp == 0), stop=(fp == 7),
                                        perf_mode=mybir.MatmulPerfMode.DoubleRow)
                                return last
                            mm = p5.add("tensor", fmm, waits=deps)
                            if tt == 7:
                                mm_lastj[j] = mm
                            f1 = p5.add("vector",
                                        lambda v, nn=nn, bank=psN[nn % 4]:
                                        v.tensor_scalar(
                                            fst[:, (nn % 2) * 512:(nn % 2 + 1) * 512],
                                            bank[:], colv[:, 2:3], None, ALU.mult),
                                        waits=[mm, bxc2] +
                                              ([ffn_ev[nn - 2]] if nn >= 2 else []))
                            f2 = p5.add("vector", lambda v, nn=nn, j=j, tt=tt:
                                        v.tensor_tensor(
                                            ost[:, (nn % 4) * 512:(nn % 4 + 1) * 512],
                                            fst[:, (nn % 2) * 512:(nn % 2 + 1) * 512],
                                            x1[:, tt * E + j * 512: tt * E + (j + 1) * 512],
                                            ALU.add),
                                        waits=[f1] + ([out_dmas[nn - 4]]
                                                      if nn >= 4 else []))
                            ffn_ev[nn] = f2
                            od = p5.add("sync", dma(
                                y_h[tt * 128:(tt + 1) * 128, j * 512:(j + 1) * 512],
                                ost[:, (nn % 4) * 512:(nn % 4 + 1) * 512]),
                                waits=[f2], inc=16, sem="dos%d" % (nn % 4))
                            out_dmas.append(od)
                            nn += 1
                    p5.add("sync", lambda s: None,
                           waits=out_dmas[-4:], inc=0)
                    run_phase(p5)

    return nc


_NC_CACHE = {}


def _get_nc(debug=False):
    key = bool(debug)
    if key not in _NC_CACHE:
        _NC_CACHE[key] = build_nc(debug=key)
    return _NC_CACHE[key]


def make_in_maps(inputs):
    x = np.asarray(inputs["L0_attn_x"])
    w_qkv = np.ascontiguousarray(np.asarray(inputs["L0_attn_w_qkv"]))
    b_qkv = np.asarray(inputs["L0_attn_b_qkv"])
    w_fc = np.ascontiguousarray(np.asarray(inputs["L0_attn_w_fc"]))
    b_fc = np.asarray(inputs["L0_attn_b_fc"])
    w_ffn = np.asarray(inputs["L0_ffn_weight"])
    if w_ffn.dtype == np.uint16:
        w_ffn = w_ffn.view(ml_dtypes.bfloat16)

    ident = np.eye(128, dtype=np.float32)
    identb = np.eye(128, dtype=ml_dtypes.bfloat16)
    ident8 = np.eye(128, dtype=ml_dtypes.float8_e4m3)
    onesR = np.ones((1, 128), dtype=ml_dtypes.bfloat16)
    onesC = np.ones((128, 1), dtype=ml_dtypes.bfloat16)
    onesRf = np.ones((1, 128), dtype=np.float32)
    ones512f = np.ones((1, 512), dtype=np.float32)
    masks = np.zeros((128, 4, 512), dtype=ml_dtypes.bfloat16)
    ii = np.arange(128)[:, None]
    jj = np.arange(512)[None, :]
    for m in range(4):
        masks[:, m, :] = ((m * 128 + ii) <= jj).astype(np.float32)
    masks = np.ascontiguousarray(masks.reshape(128, 2048))

    w_q = np.ascontiguousarray(w_qkv[:, :E])
    w_kv = np.ascontiguousarray(w_qkv[:, E:])
    b_q = np.ascontiguousarray(b_qkv[:E].reshape(H, D).T)
    b_k = np.ascontiguousarray(b_qkv[E:E + D].reshape(D, 1))
    b_v = np.ascontiguousarray(b_qkv[E + D:].reshape(D, 1))
    b_fc2 = np.ascontiguousarray(b_fc.reshape(1, E))
    wT = np.ascontiguousarray(w_ffn.T)

    in_maps = []
    for pid in range(8):
        bi, c = divmod(pid, 4)
        xo = np.concatenate([x[bi, c * 512:(c + 1) * 512],
                             x[bi, (7 - c) * 512:(8 - c) * 512]], axis=0)
        perm = [p for p in range(8) if p not in (c, 7 - c)] + [7 - c, c]
        xp = x[bi].reshape(8, 512, E)[perm].reshape(4096, E)
        in_maps.append({
            "xp": np.ascontiguousarray(xp),
            "xo": np.ascontiguousarray(xo),
            "w_q": w_q, "w_kv": w_kv, "b_q": b_q, "b_k": b_k, "b_v": b_v,
            "w_fc": w_fc, "b_fc": b_fc2, "w_ffnT": wT,
            "ident": ident, "identb": identb, "ident8": ident8,
            "onesR": onesR, "onesC": onesC, "onesRf": onesRf,
            "ones512": ones512f, "masks": masks,
        })
    return in_maps


def kernel(**inputs):
    in_maps = make_in_maps(inputs)
    nc = _get_nc(debug=False)
    res = run_bass_kernel_spmd(nc, in_maps, core_ids=list(range(8)))
    out = np.empty((2, 4096, 2048), dtype=np.float32)
    for pid in range(8):
        bi, c = divmod(pid, 4)
        y = res.results[pid]["y"]
        out[bi, c * 512:(c + 1) * 512] = y[:512]
        out[bi, (7 - c) * 512:(8 - c) * 512] = y[512:]
    return out


# version 28
# speedup vs baseline: 1.0160x; 1.0160x over previous
from contextlib import ExitStack

import numpy as np
import ml_dtypes

import concourse.bass as bass
import concourse.mybir as mybir
from concourse.bass_utils import run_bass_kernel_spmd

F32 = mybir.dt.float32
BF16 = mybir.dt.bfloat16
F8 = mybir.dt.float8e4
AF = mybir.ActivationFunctionType
ALU = mybir.AluOpType
AX = mybir.AxisListType

E = 2048
H = 16
D = 128
NSEQ = 4096
OWN = 1024
NCH = 8
INVSQ = float(np.float32(1.0) / np.float32(np.sqrt(np.float32(128.0))))
EPS = 1e-5

ENGS = ("sync", "gpsimd", "vector", "scalar", "tensor")


class Step:
    __slots__ = ("eng", "emit", "waits", "inc_n", "seq", "val", "sem")

    def __init__(self, eng, emit, waits, inc_n, sem=None):
        self.eng = eng
        self.emit = emit
        self.waits = [w for w in waits if w is not None]
        self.inc_n = inc_n
        self.seq = None
        self.val = None
        self.sem = sem if sem is not None else eng


class Phase:
    def __init__(self, name):
        self.name = name
        self.steps = []

    def add(self, eng, emit, waits=(), inc=1, sem=None):
        s = Step(eng, emit, list(waits), inc, sem=sem)
        s.seq = len(self.steps)
        self.steps.append(s)
        return s

    def resolve(self, ctr0):
        ctr = dict(ctr0)
        for s in self.steps:
            for w in s.waits:
                if isinstance(w, Step):
                    assert w.val is not None, f"{self.name}: wait on unresolved step"
                    assert w.inc_n > 0, f"{self.name}: wait on inc-less step"
            ctr[s.sem] = ctr.get(s.sem, 0) + s.inc_n
            s.val = ctr[s.sem]
        return ctr

    def emit_engine(self, engname, eng, sems):
        for s in self.steps:
            if s.eng != engname:
                continue
            needed = {}
            for w in s.waits:
                if isinstance(w, Step):
                    key, val = w.sem, w.val
                else:
                    key, val = w
                if needed.get(key, -1) < val:
                    needed[key] = val
            for key, val in sorted(needed.items()):
                eng.wait_ge(sems[key], val)
            instr = s.emit(eng)
            if s.inc_n:
                assert instr is not None, f"{self.name}: step returned no instr"
                instr.then_inc(sems[s.sem], s.inc_n)


def build_nc(debug=False):
    nc = bass.Bass("TRN2", target_bir_lowering=False, debug=False)

    xp_h = nc.dram_tensor("xp", [NSEQ, E], F32, kind="ExternalInput").ap()
    xo_h = nc.dram_tensor("xo", [OWN, E], F32, kind="ExternalInput").ap()
    wq_h = nc.dram_tensor("w_q", [E, E], F32, kind="ExternalInput").ap()
    wkv_h = nc.dram_tensor("w_kv", [E, 2 * D], F32, kind="ExternalInput").ap()
    bq_h = nc.dram_tensor("b_q", [D, H], F32, kind="ExternalInput").ap()
    bk_h = nc.dram_tensor("b_k", [D, 1], F32, kind="ExternalInput").ap()
    bv_h = nc.dram_tensor("b_v", [D, 1], F32, kind="ExternalInput").ap()
    wfc_h = nc.dram_tensor("w_fc", [E, E], F32, kind="ExternalInput").ap()
    bfc_h = nc.dram_tensor("b_fc", [1, E], F32, kind="ExternalInput").ap()
    wT_h = nc.dram_tensor("w_ffnT", [E, E], BF16, kind="ExternalInput").ap()
    id_h = nc.dram_tensor("ident", [128, 128], F32, kind="ExternalInput").ap()
    idb_h = nc.dram_tensor("identb", [128, 128], BF16, kind="ExternalInput").ap()
    id8_h = nc.dram_tensor("ident8", [128, 128], F8, kind="ExternalInput").ap()
    onesR_h = nc.dram_tensor("onesR", [1, 128], BF16, kind="ExternalInput").ap()
    onesC_h = nc.dram_tensor("onesC", [128, 1], BF16, kind="ExternalInput").ap()
    onesRf_h = nc.dram_tensor("onesRf", [1, 128], F32, kind="ExternalInput").ap()
    mask_h = nc.dram_tensor("masks", [128, 4 * 512], BF16, kind="ExternalInput").ap()
    y_h = nc.dram_tensor("y", [OWN, E], F32, kind="ExternalOutput").ap()
    cc_in = nc.dram_tensor("cc_in", [1, 1], F32, kind="Internal").ap()
    rbd_h = nc.dram_tensor("rb_scratch", [4, 512], F32, kind="Internal").ap()
    scb_h = nc.dram_tensor("sc_scratch", [1, 2], F32, kind="Internal").ap()
    cc_out = nc.dram_tensor("cc_out", [1, 1], F32, kind="Internal").ap()

    dbg = {}
    if debug:
        for nm, shp, dt in (("KT", [128, NSEQ], BF16), ("V", [128, NSEQ], BF16),
                            ("qT", [128, H * 1024], BF16),
                            ("attnT", [128, H * 1024], BF16),
                            ("x1", [128, 8 * E], F32),
                            ("stat", [1, 32], F32),
                            ("lnqT", [128, 16 * 1024], BF16)):
            dbg[nm] = nc.dram_tensor("dbg_" + nm, shp, dt,
                                     kind="ExternalOutput").ap()

    with ExitStack() as outer:
        ent = outer.enter_context
        idT = ent(nc.sbuf_tensor("idT", [128, 128], F32))
        idB = ent(nc.sbuf_tensor("idB", [128, 128], BF16))
        id8 = ent(nc.sbuf_tensor("id8", [128, 128], F8))
        onesR = ent(nc.sbuf_tensor("onesRs", [1, 128], BF16))
        onesC = ent(nc.sbuf_tensor("onesCs", [128, 1], BF16))
        onesRf = ent(nc.sbuf_tensor("onesRfs", [1, 128], F32))
        bq = ent(nc.sbuf_tensor("bqs", [128, H], F32))
        bk = ent(nc.sbuf_tensor("bks", [128, 1], F32))
        bvb = ent(nc.sbuf_tensor("bvbs", [128, 1], F32))
        bfcb = ent(nc.sbuf_tensor("bfcbs", [1, E], BF16))
        kmax8 = ent(nc.sbuf_tensor("kmax8", [1, 8], F32))
        pmax = ent(nc.sbuf_tensor("pmax", [1, 8], F32))
        mq2 = ent(nc.sbuf_tensor("mq2", [1, 2], F32))
        s11 = ent(nc.sbuf_tensor("s11", [1, 32], F32))
        bB = ent(nc.sbuf_tensor("bB", [128, 2], F32))
        colv = ent(nc.sbuf_tensor("colv", [128, 4], F32))
        amw = ent(nc.sbuf_tensor("amw", [128, 1], F32))
        amx = ent(nc.sbuf_tensor("amx", [128, 1], F32))
        mus = ent(nc.sbuf_tensor("mus", [128, 16], F32))
        stv = ent(nc.sbuf_tensor("stv", [128, 200], F32))
        sKVQ = ent(ExitStack())
        KT = sKVQ.enter_context(nc.sbuf_tensor("KT", [128, NSEQ], BF16))
        V = sKVQ.enter_context(nc.sbuf_tensor("V", [128, NSEQ], BF16))
        qT = sKVQ.enter_context(nc.sbuf_tensor("qT", [128, H * 1024], BF16))

        DMA_SEMS = ["dconst", "dxr0", "dxr1", "dxr2", "dxr3", "dxr4", "dxr5", "dxr6", "dxr7", "dwkv", "dwq",
                    "dbvb", "dbfcb", "dmask", "dwfc0", "dwfc1",
                    "dxo0", "dxo1", "dxo2", "dxo3", "dwt0", "dwt1", "dwt2", "dwt3",
                    "dwj0", "dwj1", "dos0", "dos1", "dos2", "dos3",
                    "dcci", "dcco", "ddbg", "dbr0", "dbr1", "dbo0", "dbo1",
                    "dwq0", "dwq1", "dwq2", "dev0", "dev1"]
        sems = {n: ent(nc.semaphore("sem_" + n)) for n in list(ENGS) + DMA_SEMS}
        ctr = {n: 0 for n in sems}

        def run_phase(ph, switch=None):
            nonlocal ctr
            end_ctr = ph.resolve(ctr)
            arm_end = None
            if switch is not None:
                for c in range(4):
                    e = switch[c].resolve(end_ctr)
                    if arm_end is None:
                        arm_end = e
                    assert e == arm_end, f"arm {c}: {e} != {arm_end}"
            with nc.Block(ph.name, no_gpsimd_drain=True) as blk:
                def mk(engname):
                    def emitter(eng):
                        ph.emit_engine(engname, eng, sems)
                        if switch is not None and engname in ("tensor", "scalar", "vector", "gpsimd", "sync"):
                            rc = eng.partition_id() % 4
                            for c in nc.Switch(engines=[eng], index=[rc], n=4):
                                switch[c].emit_engine(engname, eng, sems)
                    return emitter
                blk.sync(mk("sync"))
                blk.gpsimd(mk("gpsimd"))
                blk.vector(mk("vector"))
                blk.scalar(mk("scalar"))
                blk.tensor(mk("tensor"))
            ctr = arm_end if switch is not None else end_ctr

        def dma(dst, src):
            return lambda eng: eng.dma_start(out=dst, in_=src)

        with ExitStack() as sA:
            wkv = sA.enter_context(nc.sbuf_tensor("wkv", [128, 16 * 256], BF16, side="right"))
            wqs = sA.enter_context(nc.sbuf_tensor("wqs", [128, 3 * E], BF16, side="right"))
            xch = sA.enter_context(nc.sbuf_tensor("xch", [128, 8 * E], F32, side="right"))
            lnB = sA.enter_context(nc.sbuf_tensor("lnB", [128, 4 * E], BF16, side="right"))
            sqJ = sA.enter_context(nc.sbuf_tensor("sqJ", [128, E], BF16, side="right"))
            lnT = sA.enter_context(nc.sbuf_tensor("lnT", [128, 2 * 16 * 512], BF16, side="right"))
            k2s = sA.enter_context(nc.sbuf_tensor("k2s", [128, 1024], BF16, side="right"))
            VTs = sA.enter_context(nc.sbuf_tensor("VTs", [128, 512], BF16, side="right"))

            p0 = Phase("p0")
            ld_const = []
            for dst, srcap in ((idT[:], id_h[:, :]), (idB[:], idb_h[:, :]),
                               (id8[:], id8_h[:, :]),
                               (onesR[:], onesR_h[:, :]),
                               (onesC[:], onesC_h[:, :]), (onesRf[:], onesRf_h[:, :]),
                               (bq[:], bq_h[:, :]), (bk[:], bk_h[:, :])):
                ld_const.append(p0.add("sync", dma(dst, srcap), inc=16,
                                        sem="dconst"))
            ld_bvb = p0.add("gpsimd", dma(bvb[:], bv_h[:, :]), inc=16,
                            sem="dbvb")
            ld_bfcb = p0.add("gpsimd", dma(bfcb[:], bfc_h[:, :]), inc=16,
                             sem="dbfcb")
            ld_wkv = p0.add("gpsimd", dma(
                wkv[:].rearrange("p (fc j) -> p fc j", fc=16),
                wkv_h.rearrange("(fc p) j -> p fc j", p=128)), inc=16,
                sem="dwkv")
            wq_rj = wq_h.rearrange("(fc p) j -> p fc j", p=128)
            run_phase(p0)

            with nc.psum_tensor("psT0", [128, 1024], BF16) as psT0, \
                 nc.psum_tensor("psT1", [128, 1024], BF16) as psT1, \
                 nc.psum_tensor("psKV0", [128, 512], F32) as psKV0, \
                 nc.psum_tensor("psKV1", [128, 512], F32) as psKV1, \
                 nc.psum_tensor("psK0", [128, 512], F32) as psK0, \
                 nc.psum_tensor("psK1", [128, 512], F32) as psK1, \
                 nc.psum_tensor("psVT", [128, 512], BF16) as psVT, \
                 nc.psum_tensor("psk2", [1, 512], F32) as psk2:
                p1 = Phase("p1")
                st1 = {"psT": (psT0, psT1), "psKVs": (psKV0, psKV1),
                       "psKs": (psK0, psK1), "psVT": psVT, "psk2": psk2,
                       "kev_par": {0: None, 1: None},
                       "vte_par": {0: None, 1: None}}
                wqld = {}
                wq_slot_rd = {}

                def emit_wq_ld(ph, jc):
                    slot = jc % 3
                    wl = [wq_slot_rd[slot]] if wq_slot_rd.get(slot) else []
                    ld = ph.add("gpsimd", dma(
                        wqs[:, slot * E:(slot + 1) * E]
                        .rearrange("p (fc j) -> p fc j", fc=16),
                        wq_rj[:, :, jc * 128:(jc + 1) * 128]),
                        waits=wl, inc=16, sem="dwq%d" % slot)
                    wqld[jc] = ld
                    return ld

                def emit_tailA(ph, info):
                    state = st1
                    par = info["pos"] % 2
                    ch = info["ch"]
                    kev = ph.add("scalar", lambda s, ch=ch, info=info:
                                 s.activation(
                                     KT[:, ch * 512:(ch + 1) * 512],
                                     info["bankK"][:], AF.Identity,
                                     bias=bk[:, 0:1]),
                                 waits=[info["kmm"], ld_const[-1]])
                    state["kev_par"][par] = kev
                    info["kev"] = kev
                    vte_w = [info["vtm"], ld_bvb]
                    if state.get("vtr") is not None:
                        vte_w.append(state["vtr"])
                    vte = ph.add("scalar", lambda s, info=info: s.activation(
                        VTs[:, :], info["bankV"][:], AF.Identity,
                        bias=bvb[:, 0:1]), waits=vte_w)
                    state["vte_par"][par] = vte
                    info["vte"] = vte

                def emit_tailB(ph, info):
                    state = st1
                    ch = info["ch"]
                    vtr_w = [info["vte"], ld_const[-1]]
                    if state.get("vev") is not None:
                        vtr_w.append(state["vev"])
                    def vtr(tn):
                        last = None
                        for q in range(4):
                            last = tn.transpose(
                                state["psVT"][:, q * 128:(q + 1) * 128],
                                VTs[:, q * 128:(q + 1) * 128], idB[:])
                        return last
                    state["vtr"] = ph.add("tensor", vtr, waits=vtr_w)
                    state["vev"] = ph.add("vector", lambda v, ch=ch: v.tensor_copy(
                        V[:, ch * 512:(ch + 1) * 512], state["psVT"][:]),
                        waits=[state["vtr"]])
                    ksqw = [info["kev"]]
                    if state.get("k2red") is not None:
                        ksqw.append(state["k2red"])
                    ksq = ph.add("scalar", lambda s, ch=ch: s.activation(
                        k2s[:, (ch % 2) * 512:(ch % 2 + 1) * 512],
                        KT[:, ch * 512:(ch + 1) * 512], AF.Square), waits=ksqw)
                    k2w = [ksq, ld_const[-1]]
                    if state.get("k2red") is not None:
                        k2w.append(state["k2red"])
                    k2m = ph.add("tensor", lambda tn, ch=ch: tn.matmul(
                        state["psk2"][0:1, 0:512], onesC[:],
                        k2s[:, (ch % 2) * 512:(ch % 2 + 1) * 512],
                        start=True, stop=True), waits=k2w)
                    state["k2red"] = ph.add("vector", lambda v, ch=ch: v.reduce_max(
                        kmax8[0:1, ch:ch + 1], state["psk2"][0:1, 0:512], AX.X),
                        waits=[k2m])

                def emit_p1():
                    state = st1
                    info = {}
                    for pos in range(NCH):
                        state["lnT_consumers"] = (
                            [state["kmm"], state["vtm"]] if pos > 0 else [])
                        emit_one_chunk(p1, xp_h, pos, state, do_kv=True)
                        info[pos] = {"pos": pos, "ch": state["ch"],
                                     "kmm": state["kmm"], "vtm": state["vtm"],
                                     "bankK": state["psKs"][pos % 2],
                                     "bankV": state["psKVs"][pos % 2]}
                        if pos >= 2:
                            emit_tailB(p1, info[pos - 2])
                        if pos >= 1:
                            emit_tailA(p1, info[pos - 1])
                        if pos == 4:
                            for jc in range(3):
                                emit_wq_ld(p1, jc)
                        if pos == 6:
                            st1["sv_mu"] = p1.add(
                                "vector", lambda v: v.tensor_copy(
                                    stv[:, 140:144], stv[:, 8:12]),
                                waits=[state["c11"]])
                            st1["sv_r"] = p1.add(
                                "vector", lambda v: v.tensor_copy(
                                    stv[:, 148:152], stv[:, 12:16]),
                                waits=[state["c11"]])
                            st1["ldc_6"] = list(state["ldc_chunk"])
                    emit_tailB(p1, info[6])
                    emit_tailA(p1, info[7])
                    emit_tailB(p1, info[7])
                    st1["lnT_evs_7"] = list(state["lnT_evs"])

                def emit_one_chunk(ph, src_h, pos, state, do_kv):
                    tr_last = state.setdefault("tr_last", {0: None, 1: None,
                                                           2: None, 3: None})
                    bank_ev = state.setdefault("bank_ev", [None, None])
                    sq_last = state.setdefault("sq_last", [None, None])
                    ring_cons = state.setdefault("ring_cons", [None] * 8)
                    slotdone = state.setdefault("slotdone", [[], []])
                    ch = idx = pos
                    par = idx % 2
                    base = par * 32
                    sums4 = stv[:, base + 0:base + 4]
                    sumsq4 = stv[:, base + 4:base + 8]
                    mu4 = stv[:, base + 8:base + 12]
                    r4 = stv[:, base + 12:base + 16]
                    t4 = stv[:, base + 16:base + 20]
                    u4 = stv[:, base + 20:base + 24]
                    r04 = stv[:, base + 24:base + 28]
                    tmp4 = stv[:, base + 28:base + 32]
                    war = list(slotdone[par])
                    stats_steps = []
                    lnT_evs_chunk = []
                    vmm_chunk = []
                    xins = []
                    ldcs = []
                    for ti in range(4):
                        rs = (idx * 4 + ti) % 8
                        row0 = idx * 512 + ti * 128
                        wl = [ring_cons[rs]] if ring_cons[rs] else []
                        halves = []
                        for hh in range(2):
                            halves.append(ph.add("sync", dma(
                                xch[:, rs * E + hh * 1024:
                                    rs * E + (hh + 1) * 1024],
                                src_h[row0:row0 + 128,
                                      hh * 1024:(hh + 1) * 1024]),
                                waits=wl, inc=16,
                                sem="dxr%d" % ((idx * 4 + ti + 4 * hh) % 8)))
                        ldcs.append(halves)
                        xin = xch[:, rs * E:(rs + 1) * E]
                        xins.append(xin)
                        st_sq = ph.add("scalar", lambda s, xin=xin, ti=ti:
                                       s.activation(
                                           sqJ[:, :], xin, AF.Square,
                                           accum_out=stv[:, base + 4 + ti:
                                                         base + 5 + ti]),
                                       waits=halves + war +
                                             ([sq_last[0]] if sq_last[0] else []))
                        sq_last[0] = st_sq
                        vs = ph.add("vector", lambda v, xin=xin, ti=ti:
                                    v.tensor_reduce(
                                        stv[:, base + ti:base + 1 + ti],
                                        xin, AX.X, ALU.add),
                                    waits=halves + war)
                        stats_steps.append((st_sq, vs))
                    state["ldc_chunk"] = ldcs
                    groups = ([(0, 1), (1, 2), (2, 3), (3, 4)]
                              if pos == 0 else [(0, 4)])
                    chain_of = {}
                    for (g0, g1) in groups:
                        def sl(off, g0=g0, g1=g1):
                            return stv[:, base + off + g0:base + off + g1]
                        c1 = ph.add("vector", lambda v, sl=sl: v.tensor_scalar(
                            sl(8), sl(0), 1.0 / 2048.0, None, ALU.mult),
                            waits=[stats_steps[t][1]
                                   for t in range(g0, g1)] + war)
                        c2 = ph.add("vector", lambda v, sl=sl: v.tensor_scalar(
                            sl(16), sl(4), 1.0 / 2048.0, None, ALU.mult),
                            waits=[stats_steps[t][0] for t in range(g0, g1)])
                        c3 = ph.add("vector", lambda v, sl=sl: v.tensor_tensor(
                            sl(28), sl(8), sl(8), ALU.mult), waits=[c1])
                        c4 = ph.add("vector", lambda v, sl=sl: v.tensor_tensor(
                            sl(16), sl(16), sl(28), ALU.subtract),
                            waits=[c2, c3])
                        c5 = ph.add("vector", lambda v, sl=sl: v.tensor_scalar(
                            sl(16), sl(16), EPS, None, ALU.add), waits=[c4])
                        c6 = ph.add("vector", lambda v, sl=sl: v.reciprocal(
                            sl(20), sl(16)), waits=[c5])
                        c7 = ph.add("scalar", lambda s, sl=sl: s.activation(
                            sl(24), sl(20), AF.Sqrt), waits=[c6])
                        c8 = ph.add("vector", lambda v, sl=sl: v.tensor_tensor(
                            sl(28), sl(24), sl(24), ALU.mult), waits=[c7])
                        c9 = ph.add("vector", lambda v, sl=sl: v.tensor_tensor(
                            sl(28), sl(28), sl(16), ALU.mult), waits=[c8])
                        c10 = ph.add("vector", lambda v, sl=sl: v.tensor_scalar(
                            sl(28), sl(28), -0.5, 1.5, ALU.mult, ALU.add),
                            waits=[c9])
                        c11 = ph.add("vector", lambda v, sl=sl: v.tensor_tensor(
                            sl(12), sl(24), sl(28), ALU.mult), waits=[c10])
                        state["c11"] = c11
                        nm1 = ph.add("vector", lambda v, sl=sl: v.tensor_tensor(
                            sl(28), sl(8), sl(12), ALU.mult), waits=[c11])
                        nm2 = ph.add("vector", lambda v, sl=sl: v.tensor_scalar(
                            sl(28), sl(28), -1.0, None, ALU.mult), waits=[nm1])
                        for t in range(g0, g1):
                            chain_of[t] = nm2
                    applies = []
                    for ti in range(4):
                        rs = (idx * 4 + ti) % 8
                        xin = xins[ti]
                        apw = [chain_of[ti], stats_steps[ti][0]] + ldcs[ti]
                        if tr_last[ti] is not None:
                            apw.append(tr_last[ti])
                        ap = ph.add("scalar", lambda s, xin=xin, ti=ti:
                                    s.activation(
                                        lnB[:, ti * E:(ti + 1) * E], xin,
                                        AF.Identity,
                                        bias=stv[:, base + 28 + ti:
                                                 base + 29 + ti],
                                        scale=stv[:, base + 12 + ti:
                                                  base + 13 + ti]), waits=apw)
                        ring_cons[rs] = ap
                        applies.append(ap)
                        for ss_ in range(2):
                            bank = state["psT"][ss_]
                            deps = [ap, ld_const[-1]]
                            if bank_ev[ss_] is not None:
                                deps.append(bank_ev[ss_])
                            def tre(tn, ti=ti, ss_=ss_, bank=bank):
                                last = None
                                for q in range(8):
                                    fc = ss_ * 8 + q
                                    last = tn.transpose(
                                        bank[:, q * 128:(q + 1) * 128],
                                        lnB[:, ti * E + fc * 128:
                                            ti * E + (fc + 1) * 128], idB[:])
                                return last
                            st = ph.add("tensor", tre, waits=deps)
                            evw = [st]
                            if ti == 0 and ss_ == 0:
                                evw += state.get("lnT_consumers", [])
                            lnbase = state.get("lnT_base", 0)
                            def eva(v, ss_=ss_, ti=ti, bank=bank, lnbase=lnbase):
                                dstv = lnT[:, lnbase:lnbase + 16 * 512].rearrange(
                                    "p (fc t) -> p fc t", fc=16)
                                return v.tensor_copy(
                                    dstv[:, ss_ * 8:(ss_ + 1) * 8,
                                         ti * 128:(ti + 1) * 128],
                                    bank[:].rearrange("p (q t) -> p q t", q=8))
                            e = ph.add("vector", eva, waits=evw)
                            bank_ev[ss_] = e
                            lnT_evs_chunk.append(e)
                            tr_last[ti] = st
                    slotdone[par] = list(applies)
                    state["lnT_evs"] = lnT_evs_chunk
                    state["ch"] = ch
                    if do_kv:
                        bankK = state["psKs"][par]
                        bankV = state["psKVs"][par]
                        kdeps = list(lnT_evs_chunk) + [ld_wkv]
                        if state["kev_par"][par] is not None:
                            kdeps.append(state["kev_par"][par])
                        def kmm(tn, bankK=bankK):
                            last = None
                            for fc in range(16):
                                last = tn.matmul(
                                    bankK[:],
                                    wkv[:, fc * 256: fc * 256 + 128],
                                    lnT[:, fc * 512:(fc + 1) * 512],
                                    start=(fc == 0), stop=(fc == 15))
                            return last
                        state["kmm"] = ph.add("tensor", kmm, waits=kdeps)
                        vdeps = list(lnT_evs_chunk) + [ld_wkv]
                        if state["vte_par"][par] is not None:
                            vdeps.append(state["vte_par"][par])
                        def vtm(tn, bankV=bankV):
                            last = None
                            for fc in range(16):
                                last = tn.matmul(
                                    bankV[:],
                                    wkv[:, fc * 256 + 128: fc * 256 + 256],
                                    lnT[:, fc * 512:(fc + 1) * 512],
                                    start=(fc == 0), stop=(fc == 15))
                            return last
                        state["vtm"] = ph.add("tensor", vtm, waits=vdeps)

                emit_p1()
                run_phase(p1)

                p2 = Phase("p2")
                tr_last = st1["tr_last"]
                bank_ev = st1["bank_ev"]
                och1_evs = []
                qev = {0: None, 1: None}
                qps = (psKV0, psKV1)
                qwar0 = {0: [st1["vte_par"][0]], 1: [st1["vte_par"][1]]}
                lnT_ready = {0: list(st1["lnT_evs_7"]), 1: och1_evs}
                qm_by_jc_och = {}
                qev_h = {}

                def emit_q(jc, och):
                    deps = list(lnT_ready[och]) + [wqld[jc]]
                    if qev[och] is not None:
                        deps.append(qev[och])
                    else:
                        deps += qwar0[och]
                    def qmm(tn, jc=jc, och=och):
                        last = None
                        base_l = och * 8192
                        for fc in range(16):
                            last = tn.matmul(
                                qps[och][:],
                                wqs[:, (jc % 3) * E + fc * 128:
                                    (jc % 3) * E + (fc + 1) * 128],
                                lnT[:, base_l + fc * 512:
                                    base_l + (fc + 1) * 512],
                                start=(fc == 0), stop=(fc == 15))
                        return last
                    qm = p2.add("tensor", qmm, waits=deps)
                    qm_by_jc_och[(jc, och)] = qm
                    qev[och] = p2.add(
                        "scalar", lambda s, jc=jc, och=och: s.activation(
                            qT[:, jc * 1024 + och * 512:
                               jc * 1024 + (och + 1) * 512],
                            qps[och][:], AF.Identity, bias=bq[:, jc:jc + 1]),
                        waits=[qm, ld_const[-1]])
                    qev_h.setdefault(jc, {})[och] = qev[och]
                    return qm

                def post_jc(jc):
                    wq_slot_rd[jc % 3] = qm_by_jc_och[(jc, 1)]
                    if jc + 3 < H:
                        emit_wq_ld(p2, jc + 3)

                emit_q(0, 0)
                emit_q(1, 0)
                for ti in range(4):
                    apw = [st1["sv_mu"], st1["sv_r"]] + st1["ldc_6"][ti]
                    if tr_last[ti] is not None:
                        apw.append(tr_last[ti])
                    ap = p2.add("vector", lambda v, ti=ti: v.tensor_scalar(
                        lnB[:, ti * E:(ti + 1) * E],
                        xch[:, ti * E:(ti + 1) * E],
                        stv[:, 140 + ti:141 + ti],
                        stv[:, 148 + ti:149 + ti],
                        ALU.subtract, ALU.mult), waits=apw)
                    for ss_ in range(2):
                        bank = (psT0, psT1)[ss_]
                        deps = [ap, ld_const[-1]]
                        if bank_ev[ss_] is not None:
                            deps.append(bank_ev[ss_])
                        def tre2(tn, ti=ti, ss_=ss_, bank=bank):
                            last = None
                            for q in range(8):
                                fc = ss_ * 8 + q
                                last = tn.transpose(
                                    bank[:, q * 128:(q + 1) * 128],
                                    lnB[:, ti * E + fc * 128:
                                        ti * E + (fc + 1) * 128], idB[:])
                            return last
                        st = p2.add("tensor", tre2, waits=deps)
                        def eva2(v, ss_=ss_, ti=ti, bank=bank):
                            dstv = lnT[:, 8192:16384].rearrange(
                                "p (fc t) -> p fc t", fc=16)
                            return v.tensor_copy(
                                dstv[:, ss_ * 8:(ss_ + 1) * 8,
                                     ti * 128:(ti + 1) * 128],
                                bank[:].rearrange("p (q t) -> p q t", q=8))
                        e = p2.add("vector", eva2, waits=[st])
                        bank_ev[ss_] = e
                        och1_evs.append(e)
                        tr_last[ti] = st
                emit_q(0, 1)
                post_jc(0)
                emit_q(1, 1)
                post_jc(1)
                for jc in range(2, H):
                    emit_q(jc, 0)
                    emit_q(jc, 1)
                    post_jc(jc)
                q2mm_prev = None
                red_prev = None
                acc_prev = None
                for h in range(H):
                    sqw = [qev_h[h][0], qev_h[h][1]]
                    if q2mm_prev is not None:
                        sqw.append(q2mm_prev)
                    q2sq = p2.add("scalar", lambda s, h=h: s.activation(
                        k2s[:, 0:1024], qT[:, h * 1024:(h + 1) * 1024], AF.Square),
                        waits=sqw)
                    for g in range(2):
                        w = [q2sq, ld_const[-1]]
                        if red_prev is not None:
                            w.append(red_prev)
                        else:
                            w.append(st1["k2red"])
                        q2mm = p2.add("tensor", lambda tn, g=g: tn.matmul(
                            psk2[:], onesC[:], k2s[:, g * 512:(g + 1) * 512],
                            start=True, stop=True), waits=w)
                        q2mm_prev = q2mm
                        rw = [q2mm]
                        if acc_prev is not None:
                            rw.append(acc_prev)
                        red = p2.add("vector", lambda v: v.reduce_max(
                            s11[0:1, 0:1], psk2[0:1, :], AX.X), waits=rw)
                        if h == 0:
                            acc = p2.add("vector", lambda v, g=g: v.tensor_copy(
                                mq2[0:1, g:g + 1], s11[0:1, 0:1]), waits=[red])
                        else:
                            acc = p2.add("vector", lambda v, g=g: v.tensor_tensor(
                                mq2[0:1, g:g + 1], mq2[0:1, g:g + 1],
                                s11[0:1, 0:1], ALU.max), waits=[red])
                        red_prev = red
                        acc_prev = acc
                if debug:
                    d1 = p2.add("sync", dma(dbg["KT"][:, :], KT[:]), inc=16, sem="ddbg")
                    d2 = p2.add("sync", dma(dbg["V"][:, :], V[:]), inc=16, sem="ddbg")
                    d3 = p2.add("sync", dma(dbg["qT"][:, :], qT[:]), inc=16, sem="ddbg")
                    p2.add("sync", lambda s: None, waits=[d3], inc=0)
                run_phase(p2)

        with ExitStack() as sB:
            sB1 = sB.enter_context(ExitStack())
            attnT = sB1.enter_context(nc.sbuf_tensor("attnT", [128, H * 1024], BF16, side="right"))
            wfcj = sB1.enter_context(nc.sbuf_tensor("wfcj", [128, 2 * 16 * 512], BF16, side="right"))
            m3 = sB1.enter_context(ExitStack())
            pT = m3.enter_context(nc.sbuf_tensor("pT", [128, 4 * 1024], BF16, side="right"))
            pacc = m3.enter_context(nc.sbuf_tensor("pacc", [128, 2 * 1024], BF16, side="right"))
            masks = m3.enter_context(nc.sbuf_tensor("masks_s", [128, 4 * 512], BF16, side="right"))
            rcs = m3.enter_context(nc.sbuf_tensor("rcs", [1, 4 * 512], F32, side="right"))
            rcsum = m3.enter_context(nc.sbuf_tensor("rcsum", [1, 4 * 512], F32, side="right"))
            aTu = m3.enter_context(nc.sbuf_tensor("aTu", [128, 4 * 512], BF16, side="right"))
            rbs = m3.enter_context(nc.sbuf_tensor("rbs", [128, 4 * 512], F32, side="right"))

            wfc_r = wfc_h.rearrange("(fc p) j -> p fc j", p=128)
            wfcj_r = wfcj[:].rearrange("p (b fc j) -> p b fc j", b=2, fc=16)

            with nc.psum_tensor("psS0", [128, 1024], F32) as psS0, \
                 nc.psum_tensor("psS1", [128, 1024], F32) as psS1, \
                 nc.psum_tensor("psA0", [128, 512], F32) as psA0, \
                 nc.psum_tensor("psA1", [128, 512], F32) as psA1, \
                 nc.psum_tensor("psR0", [128, 512], F32) as psR0, \
                 nc.psum_tensor("psR1", [128, 512], F32) as psR1:
                psS = (psS0, psS1)
                psA = (psA0, psA1)
                psR = (psR0, psR1)

                p3 = Phase("p3")
                ld_mask = p3.add("sync", dma(masks[:], mask_h[:, :]), inc=16,
                                 sem="dmask")
                ld_wfc01 = []
                for j in range(2):
                    ld_wfc01.append(p3.add("gpsimd", dma(
                        wfcj_r[:, j], wfc_r[:, :, j * 512:(j + 1) * 512]), inc=16,
                        sem="dwfc%d" % j))

                arms = []
                for c in range(4):
                    arm = Phase(f"p3arm{c}")
                    own = (c, 7 - c)
                    perm = [p for p in range(8) if p not in own] + [7 - c, c]
                    inv = {p: s for s, p in enumerate(perm)}
                    extents = (4 * (c + 1), 4 * (8 - c))
                    kc_lists = []
                    for g in range(2):
                        ephys = c if g == 0 else 7 - c
                        slots_nd = sorted(inv[p] for p in range(ephys))
                        kl = [s * 4 + q for s in slots_nd for q in range(4)]
                        kl += [inv[ephys] * 4 + q for q in range(4)]
                        kc_lists.append(kl)
                    bset = []
                    for g in range(2):
                        ephys = c if g == 0 else 7 - c
                        slots = sorted(inv[p] for p in range(ephys + 1))
                        m0 = arm.add("vector", lambda v, s0=slots[0], g=g:
                                     v.tensor_copy(s11[0:1, 14 + g:15 + g],
                                                   kmax8[0:1, s0:s0 + 1]))
                        for s in slots[1:]:
                            m0 = arm.add("vector", lambda v, s=s, g=g:
                                         v.tensor_tensor(
                                             s11[0:1, 14 + g:15 + g],
                                             s11[0:1, 14 + g:15 + g],
                                             kmax8[0:1, s:s + 1], ALU.max),
                                         waits=[m0])
                        b1 = arm.add("vector", lambda v, g=g:
                                     v.tensor_tensor(
                                         s11[0:1, 8 + g:9 + g], mq2[0:1, g:g + 1],
                                         s11[0:1, 14 + g:15 + g], ALU.mult),
                                     waits=[m0])
                        b2 = arm.add("scalar", lambda s, g=g: s.activation(
                            s11[0:1, 10 + g:11 + g], s11[0:1, 8 + g:9 + g],
                            AF.Sqrt, scale=1.0 / 128.0), waits=[b1])
                        b3 = arm.add("vector", lambda v, g=g: v.tensor_scalar(
                            s11[0:1, 12 + g:13 + g], s11[0:1, 10 + g:11 + g],
                            -1.01, -0.25, ALU.mult, ALU.add), waits=[b2])
                        b4 = arm.add("tensor", lambda tn, g=g: tn.matmul(
                            psR0[:, g:g + 1], onesRf[:],
                            s11[0:1, 12 + g:13 + g], start=True, stop=True),
                            waits=[b3, ld_const[-1]])
                        b5 = arm.add("vector", lambda v, g=g: v.tensor_copy(
                            bB[:, g:g + 1], psR0[:, g:g + 1]), waits=[b4])
                        bset.append(b5)

                    prev_eager = None
                    pending_lazy = None
                    lazy_hist = [None, None]
                    acc_war = [None, None]
                    last_accs = None
                    bseq = 0
                    for hp in range(8):
                        h0, h1 = 2 * hp, 2 * hp + 1
                        for g in range(2):
                            Eg = extents[g]
                            kcl = kc_lists[g]
                            exp_steps = [None] * Eg
                            post_steps = [None] * Eg
                            cons_steps = [None] * Eg
                            acc_steps = [None] * Eg
                            s2_steps = [None] * Eg

                            def mk_s2(kc, it, h0=h0, h1=h1, g=g):
                                def f(tn):
                                    b = psS[it % 2]
                                    tn.matmul(
                                        b[:, 0:512],
                                        KT[:, kc * 128:(kc + 1) * 128],
                                        qT[:, h0 * 1024 + g * 512: h0 * 1024 + (g + 1) * 512],
                                        start=True, stop=True)
                                    return tn.matmul(
                                        b[:, 512:1024],
                                        KT[:, kc * 128:(kc + 1) * 128],
                                        qT[:, h1 * 1024 + g * 512: h1 * 1024 + (g + 1) * 512],
                                        start=True, stop=True)
                                return f

                            def mk_cons(kc, it, Eg=Eg):
                                def f(tn):
                                    first = (it == 0)
                                    last = (it == Eg - 1)
                                    buf = it % 4
                                    tn.matmul(psA[0][:],
                                              V[:, kc * 128:(kc + 1) * 128],
                                              pT[:, buf * 1024: buf * 1024 + 512],
                                              start=first, stop=last)
                                    return tn.matmul(
                                        psA[1][:], V[:, kc * 128:(kc + 1) * 128],
                                        pT[:, buf * 1024 + 512: buf * 1024 + 1024],
                                        start=first, stop=last)
                                return f

                            def mk_acc(it, bpar):
                                def f(v):
                                    dst = pacc[:, bpar * 1024:
                                               bpar * 1024 + 1024]
                                    src = pT[:, (it % 4) * 1024:
                                             (it % 4) * 1024 + 1024]
                                    if it == 0:
                                        return v.tensor_copy(dst, src)
                                    return v.tensor_tensor(dst, dst, src,
                                                           ALU.add)
                                return f

                            def mk_rowsum(bpar):
                                def f(tn):
                                    tn.matmul(psR[0][0:1, :], onesC[:],
                                              pacc[:, bpar * 1024:
                                                   bpar * 1024 + 512],
                                              start=True, stop=True)
                                    return tn.matmul(
                                        psR[1][0:1, :], onesC[:],
                                        pacc[:, bpar * 1024 + 512:
                                             bpar * 1024 + 1024],
                                        start=True, stop=True)
                                return f

                            def mk_exp(it, g=g):
                                def f(s):
                                    return s.activation(
                                        pT[:, (it % 4) * 1024:(it % 4) * 1024 + 1024],
                                        psS[it % 2][:, 0:1024], AF.Exp,
                                        bias=bB[:, g:g + 1], scale=INVSQ)
                                return f

                            def mk_mask(it, m, half):
                                def f(v):
                                    base = (it % 4) * 1024 + half * 512
                                    return v.tensor_tensor(
                                        pT[:, base:base + 512],
                                        pT[:, base:base + 512],
                                        masks[:, m * 512:(m + 1) * 512], ALU.mult)
                                return f

                            def emit_eager(rs_last, av_last, lazy_prev2, par):
                                ev = []
                                for hh in (0, 1):
                                    w = [rs_last]
                                    if lazy_prev2 is not None:
                                        w.append(lazy_prev2["dm1"][hh])
                                    ev.append(arm.add(
                                        "vector", lambda v, hh=hh, par=par:
                                        v.tensor_copy(
                                            rcsum[0:1, par * 1024 + hh * 512:
                                                  par * 1024 + (hh + 1) * 512],
                                            psR[hh][0:1, :]), waits=w))
                                for hh in (0, 1):
                                    w = [av_last]
                                    if lazy_prev2 is not None:
                                        w.append(lazy_prev2["mul"][hh])
                                    ev.append(arm.add(
                                        "scalar", lambda s, hh=hh, par=par:
                                        s.activation(
                                            aTu[:, par * 1024 + hh * 512:
                                                par * 1024 + (hh + 1) * 512],
                                            psA[hh][:], AF.Identity), waits=w))
                                return ev

                            def lazy_stage1(bspec, out):
                                h0b, h1b, gb, ev, lazy_prev2, par = bspec
                                for hh in (0, 1):
                                    o = par * 1024 + hh * 512
                                    rs_ = rcsum[0:1, o:o + 512]
                                    di = par * 2 + hh
                                    rw = [ev[hh]]
                                    if lazy_prev2 is not None:
                                        rw.append(lazy_prev2["dma"][hh])
                                    dm1 = arm.add("sync",
                                                  lambda g_, rs_=rs_, di=di:
                                                  g_.dma_start(
                                                      out=rbd_h[di:di + 1, :],
                                                      in_=rs_),
                                                  waits=rw, inc=16,
                                                  sem="dbo%d" % hh)
                                    out["dm1"].append(dm1)

                            def lazy_stage2(bspec, out):
                                h0b, h1b, gb, ev, lazy_prev2, par = bspec
                                for hh in (0, 1):
                                    o = par * 1024 + hh * 512
                                    rb = rbs[:, o:o + 512]
                                    di = par * 2 + hh
                                    dw2 = [out["dm1"][hh]]
                                    if lazy_prev2 is not None:
                                        dw2.append(lazy_prev2["mul"][hh])
                                    dmb = arm.add("sync",
                                                  lambda g_, rb=rb, di=di:
                                                  g_.dma_start(
                                                      out=rb,
                                                      in_=rbd_h[di:di + 1, :]
                                                      .broadcast_to((128, 512))),
                                                  waits=dw2, inc=16,
                                                  sem="dbr%d" % hh)
                                    out["dma"].append(dmb)

                            def lazy_stage3(bspec, out):
                                h0b, h1b, gb, ev, lazy_prev2, par = bspec
                                for hh, hid in ((0, h0b), (1, h1b)):
                                    o = par * 1024 + hh * 512
                                    rb = rbs[:, o:o + 512]
                                    au = aTu[:, o:o + 512]
                                    rec = arm.add(
                                        "vector", lambda v, rb=rb:
                                        v.reciprocal(rb, rb),
                                        waits=[out["dma"][hh]])
                                    mul = arm.add(
                                        "vector",
                                        lambda v, rb=rb, au=au, hid=hid, gb=gb:
                                        v.tensor_tensor(
                                            attnT[:, hid * 1024 + gb * 512:
                                                  hid * 1024 + (gb + 1) * 512],
                                            au, rb, ALU.mult),
                                        waits=[rec, ev[2 + hh]])
                                    out["mul"].append(mul)

                            def emit_lazy(bspec):
                                out = {"rec": [], "dm1": [], "dma": [], "mul": []}
                                lazy_stage1(bspec, out)
                                lazy_stage2(bspec, out)
                                lazy_stage3(bspec, out)
                                return out

                            def mk_av_last(kc, it):
                                def f(tn):
                                    buf = it % 4
                                    first = (it == 0)
                                    tn.matmul(psA[0][:],
                                              V[:, kc * 128:(kc + 1) * 128],
                                              pT[:, buf * 1024: buf * 1024 + 512],
                                              start=first, stop=True)
                                    return tn.matmul(
                                        psA[1][:], V[:, kc * 128:(kc + 1) * 128],
                                        pT[:, buf * 1024 + 512: buf * 1024 + 1024],
                                        start=first, stop=True)
                                return f

                            for it in range(Eg):
                                kc = kcl[it]
                                deps = list(bset) if (hp == 0 and g == 0 and it == 0) else []
                                if it >= 2:
                                    deps.append(exp_steps[it - 2])
                                elif hp > 0 or g > 0:
                                    deps.append(last_exps[it - 2])
                                s2_steps[it] = arm.add("tensor", mk_s2(kc, it),
                                                       waits=deps)
                                edeps = [s2_steps[it]]
                                if it >= 4:
                                    edeps.append(cons_steps[it - 4])
                                    edeps.append(acc_steps[it - 4])
                                elif hp > 0 or g > 0:
                                    edeps.append(last_cons[it - 4])
                                    edeps.append(last_accs[it - 4])
                                if hp == 0 and g == 0 and it < 2:
                                    edeps += list(bset)
                                exp_steps[it] = arm.add("scalar", mk_exp(it),
                                                        waits=edeps)
                                post = exp_steps[it]
                                if it >= Eg - 4:
                                    m = it - (Eg - 4)
                                    arm.add("vector", mk_mask(it, m, 0),
                                            waits=[exp_steps[it], ld_mask])
                                    post = arm.add("vector", mk_mask(it, m, 1),
                                                   waits=[exp_steps[it], ld_mask])
                                post_steps[it] = post
                                aw = [post_steps[it]]
                                if it == 0:
                                    if acc_war[bseq % 2] is not None:
                                        aw.append(acc_war[bseq % 2])
                                else:
                                    aw.append(acc_steps[it - 1])
                                acc_steps[it] = arm.add(
                                    "vector", mk_acc(it, bseq % 2), waits=aw)
                                if pending_lazy is not None:
                                    if it == 1:
                                        lazy_out = {"rec": [], "dm1": [],
                                                    "dma": [], "mul": []}
                                        lazy_stage1(pending_lazy, lazy_out)
                                    elif it == 2:
                                        lazy_stage2(pending_lazy, lazy_out)
                                    elif it == 3:
                                        lazy_stage3(pending_lazy, lazy_out)
                                        lazy_hist[pending_lazy[5]] = lazy_out
                                        pending_lazy = None
                                if it >= 1:
                                    cdeps = [post_steps[it - 1]]
                                    if it == 1 and prev_eager is not None:
                                        cdeps += prev_eager
                                    cons_steps[it - 1] = arm.add(
                                        "tensor", mk_cons(kcl[it - 1], it - 1),
                                        waits=cdeps)
                            av_last = arm.add("tensor", mk_av_last(kcl[Eg - 1],
                                                                   Eg - 1),
                                              waits=[post_steps[Eg - 1]])
                            cons_steps[Eg - 1] = av_last
                            rsw = [acc_steps[Eg - 1], ld_const[-1]]
                            if prev_eager is not None:
                                rsw += prev_eager[0:2]
                            rsm = arm.add("tensor", mk_rowsum(bseq % 2),
                                          waits=rsw)
                            acc_war[bseq % 2] = rsm
                            last_exps = exp_steps
                            last_cons = cons_steps
                            last_accs = acc_steps
                            par = bseq % 2
                            eager_ev = emit_eager(rsm, av_last,
                                                  lazy_hist[par], par)
                            prev_eager = eager_ev
                            pending_lazy = (h0, h1, g, eager_ev,
                                            lazy_hist[par], par)
                            bseq += 1
                    lazy_hist[pending_lazy[5]] = emit_lazy(pending_lazy)
                    pending_lazy = None
                    arms.append(arm)
                run_phase(p3, switch={c: arms[c] for c in range(4)})

                if debug:
                    pd = Phase("p3d")
                    dd = pd.add("sync", dma(dbg["attnT"][:, :], attnT[:]), inc=16,
                                sem="ddbg")
                    pd.add("sync", lambda s: None, waits=[dd], inc=0)
                    run_phase(pd)

            m3.close()
            sKVQ.close()
            x1 = sB.enter_context(nc.sbuf_tensor("x1", [128, 8 * E], F32))
            fst = sB.enter_context(nc.sbuf_tensor("fst", [128, 2 * 512], BF16))
            ost = sB.enter_context(nc.sbuf_tensor("ost", [128, 4 * 512], F32))
            m4 = sB1.enter_context(ExitStack())
            xos = m4.enter_context(nc.sbuf_tensor("xos", [128, 4 * 512], F32, side="right"))
            sqJ4 = m4.enter_context(nc.sbuf_tensor("sqJ4", [128, 512], F32, side="right"))
            wTs = m4.enter_context(nc.sbuf_tensor("wTs", [128, 4 * E], BF16, side="right"))
            xqb = sB.enter_context(nc.sbuf_tensor("xqb", [128, 2 * E], F32))
            with nc.psum_tensor("psF0", [128, 512], F32) as psF0, \
                 nc.psum_tensor("psF1", [128, 512], F32) as psF1, \
                 nc.psum_tensor("psF2", [128, 512], F32) as psF2, \
                 nc.psum_tensor("psF3", [128, 512], F32) as psF3, \
                 nc.psum_tensor("psW", [1, 128], F32) as psW:
                psF = (psF0, psF1, psF2, psF3)
                p4 = Phase("p4")
                ld_wfc = {0: ld_wfc01[0], 1: ld_wfc01[1]}
                evac = {}
                bank_prev = [None, None, None, None]
                xos_prev = [None, None, None, None]
                mm_by_j = {0: [], 1: [], 2: [], 3: []}
                stat_tail = {}
                stwar = [None] * 32
                sq4_prev = [None]
                wT_rc = wT_h.rearrange("(fc p) j -> fc p j", p=128)
                wam = {"buf": [None] * 4, "acc": None}

                def emit_wamax(fc):
                    wl = [wam["buf"][fc % 4]] if wam["buf"][fc % 4] else []
                    ldw = p4.add("sync", dma(
                        wTs[:, (fc % 4) * E:(fc % 4 + 1) * E], wT_rc[fc]),
                        waits=wl, inc=16, sem="dwt%d" % (fc % 4))
                    red = p4.add("vector", lambda v, fc=fc: v.tensor_reduce(
                        stv[:, 64:65], wTs[:, (fc % 4) * E:(fc % 4 + 1) * E],
                        AX.X, ALU.max, apply_absolute_value=True),
                        waits=[ldw] + ([wam["acc"]] if wam["acc"] else []))
                    if fc == 0:
                        acc = p4.add("vector", lambda v: v.tensor_copy(
                            amw[:, 0:1], stv[:, 64:65]), waits=[red])
                    else:
                        acc = p4.add("vector", lambda v: v.tensor_tensor(
                            amw[:, 0:1], amw[:, 0:1], stv[:, 64:65], ALU.max),
                            waits=[red, wam["acc"]])
                    wam["buf"][fc % 4] = red
                    wam["acc"] = acc

                n = 0
                for j in range(4):
                    if j >= 2:
                        ld_wfc[j] = p4.add("gpsimd", dma(
                            wfcj_r[:, j % 2],
                            wfc_r[:, :, j * 512:(j + 1) * 512]),
                            waits=list(mm_by_j[j - 2]), inc=16,
                            sem="dwfc%d" % (j % 2))
                    for tt in range(8):
                        xw = [xos_prev[n % 4]] if xos_prev[n % 4] else []
                        xd = p4.add("sync", dma(
                            xos[:, (n % 4) * 512:(n % 4 + 1) * 512],
                            xo_h[tt * 128:(tt + 1) * 128, j * 512:(j + 1) * 512]),
                            waits=xw, inc=16, sem="dxo%d" % (n % 4))
                        deps = [ld_wfc[j], xd]
                        if bank_prev[n % 4] is not None:
                            deps.append(bank_prev[n % 4])
                        def fcmm(tn, j=j, tt=tt, bank=psF[n % 4]):
                            last = tn.matmul(bank[:], onesR[:],
                                             bfcb[0:1, j * 512:(j + 1) * 512],
                                             start=True, stop=False)
                            for ec in range(16):
                                last = tn.matmul(
                                    bank[:],
                                    attnT[:, ec * 1024 + tt * 128: ec * 1024 + (tt + 1) * 128],
                                    wfcj[:, (j % 2) * 8192 + ec * 512:
                                         (j % 2) * 8192 + (ec + 1) * 512],
                                    start=False, stop=(ec == 15))
                            return last
                        mm = p4.add("tensor", fcmm,
                                    waits=deps + [ld_const[-1], ld_bfcb])
                        mm_by_j[j].append(mm)
                        ev = p4.add("vector",
                                    lambda v, j=j, tt=tt, n=n, bank=psF[n % 4]:
                                    v.scalar_tensor_tensor(
                                        x1[:, tt * E + j * 512: tt * E + (j + 1) * 512],
                                        bank[:], 1.0,
                                        xos[:, (n % 4) * 512:(n % 4 + 1) * 512],
                                        ALU.mult, ALU.add,
                                        accum_out=stv[:, tt * 4 + j:tt * 4 + j + 1]),
                                    waits=[mm, xd] +
                                          ([stwar[tt * 4 + j]] if stwar[tt * 4 + j]
                                           else []))
                        evac[(j, tt)] = ev
                        x1sl = x1[:, tt * E + j * 512: tt * E + (j + 1) * 512]
                        sq_p = p4.add("scalar", lambda s, x1sl=x1sl, j=j, tt=tt:
                                      s.activation(sqJ4[:, :], x1sl, AF.Square,
                                                   accum_out=stv[:, 32 + tt * 4 + j:
                                                                 33 + tt * 4 + j]),
                                      waits=[ev] + ([sq4_prev[0]] if sq4_prev[0]
                                                    else []))
                        sq4_prev[0] = sq_p
                        rmx = p4.add("vector", lambda v, x1sl=x1sl, j=j, tt=tt:
                                     v.tensor_reduce(
                                         stv[:, 136 + tt * 4 + j:137 + tt * 4 + j],
                                         x1sl, AX.X, ALU.max), waits=[ev])
                        rmn = p4.add("vector", lambda v, x1sl=x1sl, j=j, tt=tt:
                                     v.tensor_reduce(
                                         stv[:, 168 + tt * 4 + j:169 + tt * 4 + j],
                                         x1sl, AX.X, ALU.min), waits=[ev])
                        stat_tail[(j, tt)] = (sq_p, rmx, rmn)
                        stwar[tt * 4 + j] = None
                        bank_prev[n % 4] = ev
                        xos_prev[n % 4] = ev
                        if n < 16:
                            emit_wamax(n)
                        n += 1
                trw = p4.add("tensor", lambda tn: tn.transpose(
                    psW[0:1, :], amw[:, 0:1], idT[:]),
                    waits=[wam["acc"], ld_const[-1]])
                redw = p4.add("vector", lambda v: v.reduce_max(
                    s11[0:1, 1:2], psW[0:1, :], AX.X), waits=[trw])
                swst = p4.add("vector", lambda v: v.tensor_scalar(
                    s11[0:1, 2:3], s11[0:1, 1:2], 1.0 / 448.0, 1e-12,
                    ALU.mult, ALU.max), waits=[redw])
                rsw = p4.add("vector", lambda v: v.reciprocal(
                    s11[0:1, 3:4], s11[0:1, 2:3]), waits=[swst])
                i2sw = p4.add("vector", lambda v: v.tensor_scalar(
                    s11[0:1, 4:5], s11[0:1, 3:4], 0.5, None, ALU.mult),
                    waits=[rsw])
                bw = p4.add("tensor", lambda tn: tn.matmul(
                    psF0[:, 0:1], onesRf[:], s11[0:1, 4:5],
                    start=True, stop=True),
                    waits=[i2sw, evac[(0, 0)], bank_prev[0], ld_const[-1]])
                bwc = p4.add("vector", lambda v: v.tensor_copy(
                    colv[:, 1:2], psF0[:, 0:1]), waits=[bw])
                sums8 = stv[:, 72:80]
                sumsq8 = stv[:, 80:88]
                mx8 = stv[:, 88:96]
                mn8 = stv[:, 96:104]
                t8 = stv[:, 104:112]
                u8 = stv[:, 112:120]
                r08 = stv[:, 120:128]
                tmp8 = stv[:, 128:136]
                mu8 = mus[:, 0:8]
                r8 = mus[:, 8:16]
                combs = []
                for tt in range(8):
                    wsum = [evac[(j, tt)] for j in range(4)]
                    wtail = [stat_tail[(j, tt)] for j in range(4)]
                    combs.append(p4.add("vector", lambda v, tt=tt: v.tensor_reduce(
                        sums8[:, tt:tt + 1], stv[:, tt * 4:tt * 4 + 4], AX.X,
                        ALU.add), waits=wsum))
                    combs.append(p4.add("vector", lambda v, tt=tt: v.tensor_reduce(
                        sumsq8[:, tt:tt + 1], stv[:, 32 + tt * 4:36 + tt * 4],
                        AX.X, ALU.add), waits=[w[0] for w in wtail]))
                    combs.append(p4.add("vector", lambda v, tt=tt: v.tensor_reduce(
                        mx8[:, tt:tt + 1], stv[:, 136 + tt * 4:140 + tt * 4],
                        AX.X, ALU.max), waits=[w[1] for w in wtail]))
                    combs.append(p4.add("vector", lambda v, tt=tt: v.tensor_reduce(
                        mn8[:, tt:tt + 1], stv[:, 168 + tt * 4:172 + tt * 4],
                        AX.X, ALU.min), waits=[w[2] for w in wtail]))
                c1 = p4.add("vector", lambda v: v.tensor_scalar(
                    mu8, sums8, 1.0 / 2048.0, None, ALU.mult), waits=combs)
                c2 = p4.add("vector", lambda v: v.tensor_scalar(
                    t8, sumsq8, 1.0 / 2048.0, None, ALU.mult), waits=combs)
                c3 = p4.add("vector", lambda v: v.tensor_tensor(
                    tmp8, mu8, mu8, ALU.mult), waits=[c1])
                c4 = p4.add("vector", lambda v: v.tensor_tensor(
                    t8, t8, tmp8, ALU.subtract), waits=[c2, c3])
                c5 = p4.add("vector", lambda v: v.tensor_scalar(
                    t8, t8, EPS, None, ALU.add), waits=[c4])
                c6 = p4.add("vector", lambda v: v.reciprocal(u8, t8), waits=[c5])
                c7 = p4.add("scalar", lambda s: s.activation(
                    r08, u8, AF.Sqrt), waits=[c6])
                c8 = p4.add("vector", lambda v: v.tensor_tensor(
                    tmp8, r08, r08, ALU.mult), waits=[c7])
                c9 = p4.add("vector", lambda v: v.tensor_tensor(
                    tmp8, tmp8, t8, ALU.mult), waits=[c8])
                c10 = p4.add("vector", lambda v: v.tensor_scalar(
                    tmp8, tmp8, -0.5, 1.5, ALU.mult, ALU.add), waits=[c9])
                c11 = p4.add("vector", lambda v: v.tensor_tensor(
                    r8, r08, tmp8, ALU.mult), waits=[c10])
                a1 = p4.add("vector", lambda v: v.tensor_tensor(
                    mx8, mx8, mu8, ALU.subtract), waits=[c11])
                a2 = p4.add("vector", lambda v: v.tensor_tensor(
                    mn8, mu8, mn8, ALU.subtract), waits=[c11, a1])
                a3 = p4.add("vector", lambda v: v.tensor_tensor(
                    mx8, mx8, mn8, ALU.max), waits=[a2])
                a4 = p4.add("vector", lambda v: v.tensor_tensor(
                    mx8, mx8, r8, ALU.mult), waits=[a3])
                a5 = p4.add("vector", lambda v: v.tensor_reduce(
                    amx[:, 0:1], mx8, AX.X, ALU.max), waits=[a4])
                trx = p4.add("tensor", lambda tn: tn.transpose(
                    psW[0:1, :], amx[:, 0:1], idT[:]),
                    waits=[a5, redw, ld_const[-1]])
                redx = p4.add("vector", lambda v: v.reduce_max(
                    s11[0:1, 5:6], psW[0:1, :], AX.X), waits=[trx])
                nm8a = p4.add("vector", lambda v: v.tensor_tensor(
                    stv[:, 72:80], mu8, r8, ALU.mult), waits=[c11])
                nm8b = p4.add("vector", lambda v: v.tensor_scalar(
                    stv[:, 72:80], stv[:, 72:80], -1.0, None, ALU.mult),
                    waits=[nm8a])
                q1_steps = {}
                for tt in range(2):
                    q1_steps[tt] = p4.add(
                        "scalar", lambda s, tt=tt: s.activation(
                            xqb[:, (tt % 2) * E:(tt % 2 + 1) * E],
                            x1[:, tt * E:(tt + 1) * E], AF.Identity,
                            bias=stv[:, 72 + tt:73 + tt],
                            scale=mus[:, 8 + tt:9 + tt]), waits=[nm8b])
                if debug:
                    dd4 = p4.add("sync", dma(dbg["x1"][:, :], x1[:]),
                                 waits=[evac[(j, tt)] for j in range(4)
                                        for tt in range(8)], inc=16, sem="ddbg")
                    p4.add("sync", lambda s: None, waits=[dd4], inc=0)
                run_phase(p4)

            sB1.close()
            with ExitStack() as sC:
                lnqT = sC.enter_context(nc.sbuf_tensor("lnqT", [128, 16 * 1024], F8, side="right"))
                wq8 = sC.enter_context(nc.sbuf_tensor("wq8", [128, 4 * 16 * 512], F8, side="right"))
                wqj = sC.enter_context(nc.sbuf_tensor("wqj", [128, 2 * 16 * 512], BF16, side="right"))
                with nc.psum_tensor("psQ0", [128, 1024], F32) as psQ0, \
                     nc.psum_tensor("psQ1", [128, 1024], F32) as psQ1, \
                     nc.psum_tensor("psN0", [128, 512], F32) as psN0, \
                     nc.psum_tensor("psN1", [128, 512], F32) as psN1, \
                     nc.psum_tensor("psN2", [128, 512], F32) as psN2, \
                     nc.psum_tensor("psN3", [128, 512], F32) as psN3:
                    psQ = (psQ0, psQ1)
                    psN = (psN0, psN1, psN2, psN3)
                    p5 = Phase("p5")
                    wT_rj = wT_h.rearrange("(fc p) j -> p fc j", p=128)
                    wqj_r = wqj[:].rearrange("p (b fc j) -> p b fc j", b=2, fc=16)
                    ld_wTj = {}
                    wq_steps = {}
                    for j in range(4):
                        wl = [] if j < 2 else [wq_steps[j - 2]]
                        ld_wTj[j] = p5.add("sync", dma(
                            wqj_r[:, j % 2], wT_rj[:, :, j * 512:(j + 1) * 512]),
                            waits=wl, inc=16, sem="dwj%d" % (j % 2))
                        if j % 2 == 0:
                            wq1 = p5.add("scalar", lambda s, j=j: s.activation(
                                wq8[:, j * 8192:(j + 1) * 8192],
                                wqj[:, (j % 2) * 8192:(j % 2) * 8192 + 8192],
                                AF.Identity, scale=colv[:, 1:2]),
                                waits=[ld_wTj[j], bwc])
                        else:
                            wq1 = p5.add("vector", lambda v, j=j: v.tensor_scalar(
                                wq8[:, j * 8192:(j + 1) * 8192],
                                wqj[:, (j % 2) * 8192:(j % 2) * 8192 + 8192],
                                colv[:, 1:2], None, ALU.mult),
                                waits=[ld_wTj[j]])
                        wq_steps[j] = wq1
                    tox = p5.add("gpsimd", dma(cc_in[:, :], s11[0:1, 5:6]),
                                 waits=[redx], inc=16, sem="dcci")
                    cc = p5.add("gpsimd", lambda gp: gp.collective_compute(
                        "AllReduce", ALU.max, replica_groups=[list(range(8))],
                        ins=[cc_in[:, :]], outs=[cc_out[:, :]]), waits=[tox])
                    fromx = p5.add("gpsimd", dma(s11[0:1, 6:7], cc_out[:, :]),
                                   waits=[cc], inc=16, sem="dcco")
                    sxst = p5.add("vector", lambda v: v.tensor_scalar(
                        s11[0:1, 7:8], s11[0:1, 6:7], 1.0 / 448.0, 1e-12,
                        ALU.mult, ALU.max), waits=[fromx])
                    rsx = p5.add("vector", lambda v: v.reciprocal(
                        s11[0:1, 8:9], s11[0:1, 7:8]), waits=[sxst])
                    i2sx = p5.add("vector", lambda v: v.tensor_scalar(
                        s11[0:1, 9:10], s11[0:1, 8:9], 0.5, None, ALU.mult),
                        waits=[rsx])
                    sc4a = p5.add("vector", lambda v: v.tensor_tensor(
                        s11[0:1, 10:11], s11[0:1, 7:8], s11[0:1, 2:3], ALU.mult),
                        waits=[sxst, swst])
                    sc4b = p5.add("vector", lambda v: v.tensor_scalar(
                        s11[0:1, 11:12], s11[0:1, 10:11], 4.0, None, ALU.mult),
                        waits=[sc4a])
                    dmW1 = p5.add("sync", lambda g_: g_.dma_start(
                        out=scb_h[0:1, 0:1], in_=s11[0:1, 9:10]),
                        waits=[i2sx], inc=16, sem="dbo0")
                    dmW2 = p5.add("sync", lambda g_: g_.dma_start(
                        out=scb_h[0:1, 1:2], in_=s11[0:1, 11:12]),
                        waits=[sc4b], inc=16, sem="dbo1")
                    bxc = p5.add("sync", lambda g_: g_.dma_start(
                        out=colv[:, 0:1], in_=scb_h[0:1, 0:1]
                        .broadcast_to((128, 1))), waits=[dmW1], inc=16,
                        sem="dbr0")
                    bxc2 = p5.add("sync", lambda g_: g_.dma_start(
                        out=colv[:, 2:3], in_=scb_h[0:1, 1:2]
                        .broadcast_to((128, 1))), waits=[dmW2], inc=16,
                        sem="dbr1")
                    lnq_done = {}
                    tr_by_tt = {}
                    bank_ev5 = [None, None]
                    for tt in range(8):
                        if tt < 2:
                            q1 = q1_steps[tt]
                        else:
                            q1 = p5.add("scalar", lambda s, tt=tt:
                                        s.activation(
                                            xqb[:, (tt % 2) * E:
                                                (tt % 2 + 1) * E],
                                            x1[:, tt * E:(tt + 1) * E],
                                            AF.Identity,
                                            bias=stv[:, 72 + tt:73 + tt],
                                            scale=mus[:, 8 + tt:9 + tt]),
                                        waits=[nm8b, tr_by_tt[tt - 2]])
                        evs = []
                        for ss_ in range(2):
                            bank = psQ[ss_]
                            deps = [q1, ld_const[-1]]
                            if bank_ev5[ss_] is not None:
                                deps.append(bank_ev5[ss_])
                            def tre5(tn, ss_=ss_, tt=tt, bank=bank):
                                last = None
                                for q in range(8):
                                    fc = ss_ * 8 + q
                                    last = tn.transpose(
                                        bank[:, q * 128:(q + 1) * 128],
                                        xqb[:, (tt % 2) * E + fc * 128:
                                            (tt % 2) * E + (fc + 1) * 128],
                                        idT[:])
                                return last
                            st = p5.add("tensor", tre5, waits=deps)
                            tr_by_tt[tt] = st
                            def eva5(v, ss_=ss_, tt=tt, bank=bank):
                                dstv = lnqT[:].rearrange("p (fc t) -> p fc t",
                                                         fc=16)
                                return v.tensor_scalar(
                                    dstv[:, ss_ * 8:(ss_ + 1) * 8,
                                         tt * 128:(tt + 1) * 128],
                                    bank[:].rearrange("p (q t) -> p q t", q=8),
                                    colv[:, 0:1], None, ALU.mult)
                            e = p5.add("vector", eva5, waits=[st, bxc])
                            bank_ev5[ss_] = e
                            evs.append(e)
                        lnq_done[tt] = evs
                    out_dmas = []
                    ffn_ev = {}
                    mm_lastj = {}
                    nn = 0
                    for j in range(4):
                        for tt in range(8):
                            deps = [wq_steps[j]] + list(lnq_done[tt])
                            if nn >= 4:
                                deps.append(ffn_ev[nn - 4])
                            def fmm(tn, j=j, tt=tt, bank=psN[nn % 4]):
                                last = None
                                lq = lnqT[:].rearrange("p (fc t) -> p fc t",
                                                       fc=16)
                                wv_ = wq8[:].rearrange("p (b fc jj) -> p b fc jj",
                                                       b=4, fc=16)
                                for fp in range(8):
                                    last = tn.matmul(
                                        bank[:],
                                        lq[:, 2 * fp:2 * fp + 2,
                                           tt * 128:(tt + 1) * 128],
                                        wv_[:, j, 2 * fp:2 * fp + 2, :],
                                        start=(fp == 0), stop=(fp == 7),
                                        perf_mode=mybir.MatmulPerfMode.DoubleRow)
                                return last
                            mm = p5.add("tensor", fmm, waits=deps)
                            if tt == 7:
                                mm_lastj[j] = mm
                            f1 = p5.add("vector",
                                        lambda v, nn=nn, bank=psN[nn % 4]:
                                        v.tensor_scalar(
                                            fst[:, (nn % 2) * 512:(nn % 2 + 1) * 512],
                                            bank[:], colv[:, 2:3], None, ALU.mult),
                                        waits=[mm, bxc2] +
                                              ([ffn_ev[nn - 2]] if nn >= 2 else []))
                            f2 = p5.add("vector", lambda v, nn=nn, j=j, tt=tt:
                                        v.tensor_tensor(
                                            ost[:, (nn % 4) * 512:(nn % 4 + 1) * 512],
                                            fst[:, (nn % 2) * 512:(nn % 2 + 1) * 512],
                                            x1[:, tt * E + j * 512: tt * E + (j + 1) * 512],
                                            ALU.add),
                                        waits=[f1] + ([out_dmas[nn - 4]]
                                                      if nn >= 4 else []))
                            ffn_ev[nn] = f2
                            od = p5.add("sync", dma(
                                y_h[tt * 128:(tt + 1) * 128, j * 512:(j + 1) * 512],
                                ost[:, (nn % 4) * 512:(nn % 4 + 1) * 512]),
                                waits=[f2], inc=16, sem="dos%d" % (nn % 4))
                            out_dmas.append(od)
                            nn += 1
                    p5.add("sync", lambda s: None,
                           waits=out_dmas[-4:], inc=0)
                    run_phase(p5)

    return nc


_NC_CACHE = {}


def _get_nc(debug=False):
    key = bool(debug)
    if key not in _NC_CACHE:
        _NC_CACHE[key] = build_nc(debug=key)
    return _NC_CACHE[key]


def make_in_maps(inputs):
    x = np.asarray(inputs["L0_attn_x"])
    w_qkv = np.ascontiguousarray(np.asarray(inputs["L0_attn_w_qkv"]))
    b_qkv = np.asarray(inputs["L0_attn_b_qkv"])
    w_fc = np.ascontiguousarray(np.asarray(inputs["L0_attn_w_fc"]))
    b_fc = np.asarray(inputs["L0_attn_b_fc"])
    w_ffn = np.asarray(inputs["L0_ffn_weight"])
    if w_ffn.dtype == np.uint16:
        w_ffn = w_ffn.view(ml_dtypes.bfloat16)

    ident = np.eye(128, dtype=np.float32)
    identb = np.eye(128, dtype=ml_dtypes.bfloat16)
    ident8 = np.eye(128, dtype=ml_dtypes.float8_e4m3)
    onesR = np.ones((1, 128), dtype=ml_dtypes.bfloat16)
    onesC = np.ones((128, 1), dtype=ml_dtypes.bfloat16)
    onesRf = np.ones((1, 128), dtype=np.float32)
    masks = np.zeros((128, 4, 512), dtype=ml_dtypes.bfloat16)
    ii = np.arange(128)[:, None]
    jj = np.arange(512)[None, :]
    for m in range(4):
        masks[:, m, :] = ((m * 128 + ii) <= jj).astype(np.float32)
    masks = np.ascontiguousarray(masks.reshape(128, 2048))

    w_q = np.ascontiguousarray(w_qkv[:, :E])
    w_kv = np.ascontiguousarray(w_qkv[:, E:])
    b_q = np.ascontiguousarray(b_qkv[:E].reshape(H, D).T)
    b_k = np.ascontiguousarray(b_qkv[E:E + D].reshape(D, 1))
    b_v = np.ascontiguousarray(b_qkv[E + D:].reshape(D, 1))
    b_fc2 = np.ascontiguousarray(b_fc.reshape(1, E))
    wT = np.ascontiguousarray(w_ffn.T)

    in_maps = []
    for pid in range(8):
        bi, c = divmod(pid, 4)
        xo = np.concatenate([x[bi, c * 512:(c + 1) * 512],
                             x[bi, (7 - c) * 512:(8 - c) * 512]], axis=0)
        perm = [p for p in range(8) if p not in (c, 7 - c)] + [7 - c, c]
        xp = x[bi].reshape(8, 512, E)[perm].reshape(4096, E)
        in_maps.append({
            "xp": np.ascontiguousarray(xp),
            "xo": np.ascontiguousarray(xo),
            "w_q": w_q, "w_kv": w_kv, "b_q": b_q, "b_k": b_k, "b_v": b_v,
            "w_fc": w_fc, "b_fc": b_fc2, "w_ffnT": wT,
            "ident": ident, "identb": identb, "ident8": ident8,
            "onesR": onesR, "onesC": onesC, "onesRf": onesRf,
            "masks": masks,
        })
    return in_maps


def kernel(**inputs):
    in_maps = make_in_maps(inputs)
    nc = _get_nc(debug=False)
    res = run_bass_kernel_spmd(nc, in_maps, core_ids=list(range(8)))
    out = np.empty((2, 4096, 2048), dtype=np.float32)
    for pid in range(8):
        bi, c = divmod(pid, 4)
        y = res.results[pid]["y"]
        out[bi, c * 512:(c + 1) * 512] = y[:512]
        out[bi, (7 - c) * 512:(8 - c) * 512] = y[512:]
    return out


# version 30
# speedup vs baseline: 1.0541x; 1.0375x over previous
from contextlib import ExitStack

import numpy as np
import ml_dtypes

import concourse.bass as bass
import concourse.mybir as mybir
from concourse.bass_utils import run_bass_kernel_spmd

F32 = mybir.dt.float32
BF16 = mybir.dt.bfloat16
F8 = mybir.dt.float8e4
AF = mybir.ActivationFunctionType
ALU = mybir.AluOpType
AX = mybir.AxisListType

E = 2048
H = 16
D = 128
NSEQ = 4096
OWN = 1024
NCH = 8
INVSQ = float(np.float32(1.0) / np.float32(np.sqrt(np.float32(128.0))))
EPS = 1e-5

ENGS = ("sync", "gpsimd", "vector", "scalar", "tensor")


class Step:
    __slots__ = ("eng", "emit", "waits", "inc_n", "seq", "val", "sem")

    def __init__(self, eng, emit, waits, inc_n, sem=None):
        self.eng = eng
        self.emit = emit
        self.waits = [w for w in waits if w is not None]
        self.inc_n = inc_n
        self.seq = None
        self.val = None
        self.sem = sem if sem is not None else eng


class Phase:
    def __init__(self, name):
        self.name = name
        self.steps = []

    def add(self, eng, emit, waits=(), inc=1, sem=None):
        s = Step(eng, emit, list(waits), inc, sem=sem)
        s.seq = len(self.steps)
        self.steps.append(s)
        return s

    def resolve(self, ctr0):
        ctr = dict(ctr0)
        for s in self.steps:
            for w in s.waits:
                if isinstance(w, Step):
                    assert w.val is not None, f"{self.name}: wait on unresolved step"
                    assert w.inc_n > 0, f"{self.name}: wait on inc-less step"
            ctr[s.sem] = ctr.get(s.sem, 0) + s.inc_n
            s.val = ctr[s.sem]
        return ctr

    def emit_engine(self, engname, eng, sems):
        for s in self.steps:
            if s.eng != engname:
                continue
            needed = {}
            for w in s.waits:
                if isinstance(w, Step):
                    key, val = w.sem, w.val
                else:
                    key, val = w
                if needed.get(key, -1) < val:
                    needed[key] = val
            for key, val in sorted(needed.items()):
                eng.wait_ge(sems[key], val)
            instr = s.emit(eng)
            if s.inc_n:
                assert instr is not None, f"{self.name}: step returned no instr"
                instr.then_inc(sems[s.sem], s.inc_n)


def build_nc(debug=False):
    nc = bass.Bass("TRN2", target_bir_lowering=False, debug=False)

    xp_h = nc.dram_tensor("xp", [NSEQ, E], F32, kind="ExternalInput").ap()
    xo_h = nc.dram_tensor("xo", [OWN, E], F32, kind="ExternalInput").ap()
    wq_h = nc.dram_tensor("w_q", [E, E], F32, kind="ExternalInput").ap()
    wkv_h = nc.dram_tensor("w_kv", [E, 2 * D], F32, kind="ExternalInput").ap()
    bq_h = nc.dram_tensor("b_q", [D, H], F32, kind="ExternalInput").ap()
    bk_h = nc.dram_tensor("b_k", [D, 1], F32, kind="ExternalInput").ap()
    bv_h = nc.dram_tensor("b_v", [D, 1], F32, kind="ExternalInput").ap()
    wfc_h = nc.dram_tensor("w_fc", [E, E], F32, kind="ExternalInput").ap()
    bfc_h = nc.dram_tensor("b_fc", [1, E], F32, kind="ExternalInput").ap()
    wT_h = nc.dram_tensor("w_ffnT", [E, E], BF16, kind="ExternalInput").ap()
    id_h = nc.dram_tensor("ident", [128, 128], F32, kind="ExternalInput").ap()
    idb_h = nc.dram_tensor("identb", [128, 128], BF16, kind="ExternalInput").ap()
    id8_h = nc.dram_tensor("ident8", [128, 128], F8, kind="ExternalInput").ap()
    onesR_h = nc.dram_tensor("onesR", [1, 128], BF16, kind="ExternalInput").ap()
    onesC_h = nc.dram_tensor("onesC", [128, 1], BF16, kind="ExternalInput").ap()
    onesRf_h = nc.dram_tensor("onesRf", [1, 128], F32, kind="ExternalInput").ap()
    mask_h = nc.dram_tensor("masks", [128, 4 * 512], BF16, kind="ExternalInput").ap()
    y_h = nc.dram_tensor("y", [OWN, E], F32, kind="ExternalOutput").ap()
    cc_in = nc.dram_tensor("cc_in", [1, 1], F32, kind="Internal").ap()
    rbd_h = nc.dram_tensor("rb_scratch", [4, 512], F32, kind="Internal").ap()
    scb_h = nc.dram_tensor("sc_scratch", [1, 2], F32, kind="Internal").ap()
    cc_out = nc.dram_tensor("cc_out", [1, 1], F32, kind="Internal").ap()

    dbg = {}
    if debug:
        for nm, shp, dt in (("KT", [128, NSEQ], BF16), ("V", [128, NSEQ], BF16),
                            ("qT", [128, H * 1024], BF16),
                            ("attnT", [128, H * 1024], BF16),
                            ("x1", [128, 8 * E], F32),
                            ("stat", [1, 32], F32),
                            ("lnqT", [128, 16 * 1024], BF16)):
            dbg[nm] = nc.dram_tensor("dbg_" + nm, shp, dt,
                                     kind="ExternalOutput").ap()

    with ExitStack() as outer:
        ent = outer.enter_context
        idT = ent(nc.sbuf_tensor("idT", [128, 128], F32))
        idB = ent(nc.sbuf_tensor("idB", [128, 128], BF16))
        id8 = ent(nc.sbuf_tensor("id8", [128, 128], F8))
        onesR = ent(nc.sbuf_tensor("onesRs", [1, 128], BF16))
        onesC = ent(nc.sbuf_tensor("onesCs", [128, 1], BF16))
        onesRf = ent(nc.sbuf_tensor("onesRfs", [1, 128], F32))
        bq = ent(nc.sbuf_tensor("bqs", [128, H], F32))
        bk = ent(nc.sbuf_tensor("bks", [128, 1], F32))
        bvb = ent(nc.sbuf_tensor("bvbs", [128, 1], F32))
        bfcb = ent(nc.sbuf_tensor("bfcbs", [1, E], BF16))
        kmax8 = ent(nc.sbuf_tensor("kmax8", [1, 8], F32))
        pmax = ent(nc.sbuf_tensor("pmax", [1, 8], F32))
        mq2 = ent(nc.sbuf_tensor("mq2", [1, 2], F32))
        s11 = ent(nc.sbuf_tensor("s11", [1, 32], F32))
        bB = ent(nc.sbuf_tensor("bB", [128, 2], F32))
        colv = ent(nc.sbuf_tensor("colv", [128, 4], F32))
        amw = ent(nc.sbuf_tensor("amw", [128, 1], F32))
        amx = ent(nc.sbuf_tensor("amx", [128, 1], F32))
        mus = ent(nc.sbuf_tensor("mus", [128, 16], F32))
        stv = ent(nc.sbuf_tensor("stv", [128, 200], F32))
        sKVQ = ent(ExitStack())
        KT = sKVQ.enter_context(nc.sbuf_tensor("KT", [128, NSEQ], BF16))
        V = sKVQ.enter_context(nc.sbuf_tensor("V", [128, NSEQ], BF16))
        qT = sKVQ.enter_context(nc.sbuf_tensor("qT", [128, H * 1024], BF16))

        DMA_SEMS = ["dconst", "dxr0", "dxr1", "dxr2", "dxr3", "dxr4", "dxr5", "dxr6", "dxr7", "dwkv", "dwq",
                    "dbvb", "dbfcb", "dmask", "dwfc0", "dwfc1",
                    "dxo0", "dxo1", "dxo2", "dxo3", "dwt0", "dwt1", "dwt2", "dwt3",
                    "dwj0", "dwj1", "dos0", "dos1", "dos2", "dos3",
                    "dcci", "dcco", "ddbg", "dbr0", "dbr1", "dbo0", "dbo1",
                    "dwq0", "dwq1", "dwq2", "dev0", "dev1"]
        sems = {n: ent(nc.semaphore("sem_" + n)) for n in list(ENGS) + DMA_SEMS}
        ctr = {n: 0 for n in sems}

        def run_phase(ph, switch=None):
            nonlocal ctr
            end_ctr = ph.resolve(ctr)
            arm_end = None
            if switch is not None:
                for c in range(4):
                    e = switch[c].resolve(end_ctr)
                    if arm_end is None:
                        arm_end = e
                    assert e == arm_end, f"arm {c}: {e} != {arm_end}"
            with nc.Block(ph.name, no_gpsimd_drain=True) as blk:
                def mk(engname):
                    def emitter(eng):
                        ph.emit_engine(engname, eng, sems)
                        if switch is not None and engname in ("tensor", "scalar", "vector", "gpsimd", "sync"):
                            rc = eng.partition_id() % 4
                            for c in nc.Switch(engines=[eng], index=[rc], n=4):
                                switch[c].emit_engine(engname, eng, sems)
                    return emitter
                blk.sync(mk("sync"))
                blk.gpsimd(mk("gpsimd"))
                blk.vector(mk("vector"))
                blk.scalar(mk("scalar"))
                blk.tensor(mk("tensor"))
            ctr = arm_end if switch is not None else end_ctr

        def dma(dst, src):
            return lambda eng: eng.dma_start(out=dst, in_=src)

        with ExitStack() as sA:
            wkv = sA.enter_context(nc.sbuf_tensor("wkv", [128, 16 * 256], BF16, side="right"))
            wqs = sA.enter_context(nc.sbuf_tensor("wqs", [128, 3 * E], BF16, side="right"))
            xch = sA.enter_context(nc.sbuf_tensor("xch", [128, 8 * E], F32, side="right"))
            lnB = sA.enter_context(nc.sbuf_tensor("lnB", [128, 4 * E], BF16, side="right"))
            sqJ = sA.enter_context(nc.sbuf_tensor("sqJ", [128, E], BF16, side="right"))
            lnT = sA.enter_context(nc.sbuf_tensor("lnT", [128, 2 * 16 * 512], BF16, side="right"))
            k2s = sA.enter_context(nc.sbuf_tensor("k2s", [128, 1024], BF16, side="right"))
            VTs = sA.enter_context(nc.sbuf_tensor("VTs", [128, 512], BF16, side="right"))

            p0 = Phase("p0")
            ld_const = []
            for dst, srcap in ((idT[:], id_h[:, :]), (idB[:], idb_h[:, :]),
                               (id8[:], id8_h[:, :]),
                               (onesR[:], onesR_h[:, :]),
                               (onesC[:], onesC_h[:, :]), (onesRf[:], onesRf_h[:, :]),
                               (bq[:], bq_h[:, :]), (bk[:], bk_h[:, :])):
                ld_const.append(p0.add("sync", dma(dst, srcap), inc=16,
                                        sem="dconst"))
            ld_bvb = p0.add("gpsimd", dma(bvb[:], bv_h[:, :]), inc=16,
                            sem="dbvb")
            ld_bfcb = p0.add("gpsimd", dma(bfcb[:], bfc_h[:, :]), inc=16,
                             sem="dbfcb")
            ld_wkv = p0.add("gpsimd", dma(
                wkv[:].rearrange("p (fc j) -> p fc j", fc=16),
                wkv_h.rearrange("(fc p) j -> p fc j", p=128)), inc=16,
                sem="dwkv")
            wq_rj = wq_h.rearrange("(fc p) j -> p fc j", p=128)
            run_phase(p0)

            with nc.psum_tensor("psT0", [128, 1024], BF16) as psT0, \
                 nc.psum_tensor("psT1", [128, 1024], BF16) as psT1, \
                 nc.psum_tensor("psKV0", [128, 512], F32) as psKV0, \
                 nc.psum_tensor("psKV1", [128, 512], F32) as psKV1, \
                 nc.psum_tensor("psK0", [128, 512], F32) as psK0, \
                 nc.psum_tensor("psK1", [128, 512], F32) as psK1, \
                 nc.psum_tensor("psVT", [128, 512], BF16) as psVT, \
                 nc.psum_tensor("psk2", [1, 512], F32) as psk2:
                p1 = Phase("p1")
                st1 = {"psT": (psT0, psT1), "psKVs": (psKV0, psKV1),
                       "psKs": (psK0, psK1), "psVT": psVT, "psk2": psk2,
                       "kev_par": {0: None, 1: None},
                       "vte_par": {0: None, 1: None}}
                wqld = {}
                wq_slot_rd = {}

                def emit_wq_ld(ph, jc):
                    slot = jc % 3
                    wl = [wq_slot_rd[slot]] if wq_slot_rd.get(slot) else []
                    ld = ph.add("gpsimd", dma(
                        wqs[:, slot * E:(slot + 1) * E]
                        .rearrange("p (fc j) -> p fc j", fc=16),
                        wq_rj[:, :, jc * 128:(jc + 1) * 128]),
                        waits=wl, inc=16, sem="dwq%d" % slot)
                    wqld[jc] = ld
                    return ld

                def emit_tailA(ph, info):
                    state = st1
                    par = info["pos"] % 2
                    ch = info["ch"]
                    kev = ph.add("scalar", lambda s, ch=ch, info=info:
                                 s.activation(
                                     KT[:, ch * 512:(ch + 1) * 512],
                                     info["bankK"][:], AF.Identity,
                                     bias=bk[:, 0:1]),
                                 waits=[info["kmm"], ld_const[-1]])
                    state["kev_par"][par] = kev
                    info["kev"] = kev
                    vte_w = [info["vtm"], ld_bvb]
                    if state.get("vtr") is not None:
                        vte_w.append(state["vtr"])
                    vte = ph.add("scalar", lambda s, info=info: s.activation(
                        VTs[:, :], info["bankV"][:], AF.Identity,
                        bias=bvb[:, 0:1]), waits=vte_w)
                    state["vte_par"][par] = vte
                    info["vte"] = vte

                def emit_tailB(ph, info):
                    state = st1
                    ch = info["ch"]
                    vtr_w = [info["vte"], ld_const[-1]]
                    if state.get("vev") is not None:
                        vtr_w.append(state["vev"])
                    def vtr(tn):
                        last = None
                        for q in range(4):
                            last = tn.transpose(
                                state["psVT"][:, q * 128:(q + 1) * 128],
                                VTs[:, q * 128:(q + 1) * 128], idB[:])
                        return last
                    state["vtr"] = ph.add("tensor", vtr, waits=vtr_w)
                    state["vev"] = ph.add("vector", lambda v, ch=ch: v.tensor_copy(
                        V[:, ch * 512:(ch + 1) * 512], state["psVT"][:]),
                        waits=[state["vtr"]])
                    ksqw = [info["kev"]]
                    if state.get("k2red") is not None:
                        ksqw.append(state["k2red"])
                    ksq = ph.add("scalar", lambda s, ch=ch: s.activation(
                        k2s[:, (ch % 2) * 512:(ch % 2 + 1) * 512],
                        KT[:, ch * 512:(ch + 1) * 512], AF.Square), waits=ksqw)
                    k2w = [ksq, ld_const[-1]]
                    if state.get("k2red") is not None:
                        k2w.append(state["k2red"])
                    k2m = ph.add("tensor", lambda tn, ch=ch: tn.matmul(
                        state["psk2"][0:1, 0:512], onesC[:],
                        k2s[:, (ch % 2) * 512:(ch % 2 + 1) * 512],
                        start=True, stop=True), waits=k2w)
                    state["k2red"] = ph.add("vector", lambda v, ch=ch: v.reduce_max(
                        kmax8[0:1, ch:ch + 1], state["psk2"][0:1, 0:512], AX.X),
                        waits=[k2m])

                def emit_p1():
                    state = st1
                    info = {}
                    for pos in range(NCH):
                        state["lnT_consumers"] = (
                            [state["kmm"], state["vtm"]] if pos > 0 else [])
                        emit_one_chunk(p1, xp_h, pos, state, do_kv=True)
                        info[pos] = {"pos": pos, "ch": state["ch"],
                                     "kmm": state["kmm"], "vtm": state["vtm"],
                                     "bankK": state["psKs"][pos % 2],
                                     "bankV": state["psKVs"][pos % 2]}
                        if pos >= 2:
                            emit_tailB(p1, info[pos - 2])
                        if pos >= 1:
                            emit_tailA(p1, info[pos - 1])
                        if pos == 4:
                            for jc in range(3):
                                emit_wq_ld(p1, jc)
                        if pos == 6:
                            st1["sv_mu"] = p1.add(
                                "vector", lambda v: v.tensor_copy(
                                    stv[:, 140:144], stv[:, 8:12]),
                                waits=[state["c11"]])
                            st1["sv_r"] = p1.add(
                                "vector", lambda v: v.tensor_copy(
                                    stv[:, 148:152], stv[:, 12:16]),
                                waits=[state["c11"]])
                            st1["ldc_6"] = list(state["ldc_chunk"])
                    emit_tailB(p1, info[6])
                    emit_tailA(p1, info[7])
                    emit_tailB(p1, info[7])
                    st1["lnT_evs_7"] = list(state["lnT_evs"])

                def emit_one_chunk(ph, src_h, pos, state, do_kv):
                    tr_last = state.setdefault("tr_last", {0: None, 1: None,
                                                           2: None, 3: None})
                    bank_ev = state.setdefault("bank_ev", [None, None])
                    sq_last = state.setdefault("sq_last", [None, None])
                    ring_cons = state.setdefault("ring_cons", [None] * 8)
                    slotdone = state.setdefault("slotdone", [[], []])
                    ch = idx = pos
                    par = idx % 2
                    base = par * 32
                    sums4 = stv[:, base + 0:base + 4]
                    sumsq4 = stv[:, base + 4:base + 8]
                    mu4 = stv[:, base + 8:base + 12]
                    r4 = stv[:, base + 12:base + 16]
                    t4 = stv[:, base + 16:base + 20]
                    u4 = stv[:, base + 20:base + 24]
                    r04 = stv[:, base + 24:base + 28]
                    tmp4 = stv[:, base + 28:base + 32]
                    war = list(slotdone[par])
                    stats_steps = []
                    lnT_evs_chunk = []
                    vmm_chunk = []
                    xins = []
                    ldcs = []
                    for ti in range(4):
                        rs = (idx * 4 + ti) % 8
                        row0 = idx * 512 + ti * 128
                        wl = [ring_cons[rs]] if ring_cons[rs] else []
                        halves = []
                        for hh in range(2):
                            halves.append(ph.add("sync", dma(
                                xch[:, rs * E + hh * 1024:
                                    rs * E + (hh + 1) * 1024],
                                src_h[row0:row0 + 128,
                                      hh * 1024:(hh + 1) * 1024]),
                                waits=wl, inc=16,
                                sem="dxr%d" % ((idx * 4 + ti + 4 * hh) % 8)))
                        ldcs.append(halves)
                        xin = xch[:, rs * E:(rs + 1) * E]
                        xins.append(xin)
                        st_sq = ph.add("scalar", lambda s, xin=xin, ti=ti:
                                       s.activation(
                                           sqJ[:, :], xin, AF.Square,
                                           accum_out=stv[:, base + 4 + ti:
                                                         base + 5 + ti]),
                                       waits=halves + war +
                                             ([sq_last[0]] if sq_last[0] else []))
                        sq_last[0] = st_sq
                        vs = ph.add("vector", lambda v, xin=xin, ti=ti:
                                    v.tensor_reduce(
                                        stv[:, base + ti:base + 1 + ti],
                                        xin, AX.X, ALU.add),
                                    waits=halves + war)
                        stats_steps.append((st_sq, vs))
                    state["ldc_chunk"] = ldcs
                    c1 = ph.add("vector", lambda v: v.tensor_scalar(
                        mu4, sums4, 1.0 / 2048.0, None, ALU.mult),
                        waits=[s[1] for s in stats_steps] + war)
                    c2 = ph.add("vector", lambda v: v.tensor_scalar(
                        t4, sumsq4, 1.0 / 2048.0, None, ALU.mult),
                        waits=[s[0] for s in stats_steps])
                    c3 = ph.add("vector", lambda v: v.tensor_tensor(
                        tmp4, mu4, mu4, ALU.mult), waits=[c1])
                    c4 = ph.add("vector", lambda v: v.tensor_tensor(
                        t4, t4, tmp4, ALU.subtract), waits=[c2, c3])
                    c5 = ph.add("vector", lambda v: v.tensor_scalar(
                        t4, t4, EPS, None, ALU.add), waits=[c4])
                    c6 = ph.add("vector", lambda v: v.reciprocal(u4, t4),
                                waits=[c5])
                    c7 = ph.add("scalar", lambda s: s.activation(
                        r04, u4, AF.Sqrt), waits=[c6])
                    c8 = ph.add("vector", lambda v: v.tensor_tensor(
                        tmp4, r04, r04, ALU.mult), waits=[c7])
                    c9 = ph.add("vector", lambda v: v.tensor_tensor(
                        tmp4, tmp4, t4, ALU.mult), waits=[c8])
                    c10 = ph.add("vector", lambda v: v.tensor_scalar(
                        tmp4, tmp4, -0.5, 1.5, ALU.mult, ALU.add), waits=[c9])
                    c11 = ph.add("vector", lambda v: v.tensor_tensor(
                        r4, r04, tmp4, ALU.mult), waits=[c10])
                    state["c11"] = c11
                    nm1 = ph.add("vector", lambda v: v.tensor_tensor(
                        tmp4, mu4, r4, ALU.mult), waits=[c11])
                    nm2 = ph.add("vector", lambda v: v.tensor_scalar(
                        tmp4, tmp4, -1.0, None, ALU.mult), waits=[nm1])
                    applies = []
                    for ti in range(4):
                        rs = (idx * 4 + ti) % 8
                        xin = xins[ti]
                        apw = [nm2, stats_steps[ti][0]] + ldcs[ti]
                        if tr_last[ti] is not None:
                            apw.append(tr_last[ti])
                        ap = ph.add("scalar", lambda s, xin=xin, ti=ti:
                                    s.activation(
                                        lnB[:, ti * E:(ti + 1) * E], xin,
                                        AF.Identity,
                                        bias=stv[:, base + 28 + ti:
                                                 base + 29 + ti],
                                        scale=stv[:, base + 12 + ti:
                                                  base + 13 + ti]), waits=apw)
                        ring_cons[rs] = ap
                        applies.append(ap)
                        for ss_ in range(2):
                            bank = state["psT"][ss_]
                            deps = [ap, ld_const[-1]]
                            if bank_ev[ss_] is not None:
                                deps.append(bank_ev[ss_])
                            def tre(tn, ti=ti, ss_=ss_, bank=bank):
                                last = None
                                for q in range(8):
                                    fc = ss_ * 8 + q
                                    last = tn.transpose(
                                        bank[:, q * 128:(q + 1) * 128],
                                        lnB[:, ti * E + fc * 128:
                                            ti * E + (fc + 1) * 128], idB[:])
                                return last
                            st = ph.add("tensor", tre, waits=deps)
                            evw = [st]
                            if ti == 0 and ss_ == 0:
                                evw += state.get("lnT_consumers", [])
                            lnbase = state.get("lnT_base", 0)
                            def eva(v, ss_=ss_, ti=ti, bank=bank, lnbase=lnbase):
                                dstv = lnT[:, lnbase:lnbase + 16 * 512].rearrange(
                                    "p (fc t) -> p fc t", fc=16)
                                return v.tensor_copy(
                                    dstv[:, ss_ * 8:(ss_ + 1) * 8,
                                         ti * 128:(ti + 1) * 128],
                                    bank[:].rearrange("p (q t) -> p q t", q=8))
                            e = ph.add("vector", eva, waits=evw)
                            bank_ev[ss_] = e
                            lnT_evs_chunk.append(e)
                            tr_last[ti] = st
                    slotdone[par] = list(applies)
                    state["lnT_evs"] = lnT_evs_chunk
                    state["ch"] = ch
                    if do_kv:
                        bankK = state["psKs"][par]
                        bankV = state["psKVs"][par]
                        kdeps = list(lnT_evs_chunk) + [ld_wkv]
                        if state["kev_par"][par] is not None:
                            kdeps.append(state["kev_par"][par])
                        def kmm(tn, bankK=bankK):
                            last = None
                            for fc in range(16):
                                last = tn.matmul(
                                    bankK[:],
                                    wkv[:, fc * 256: fc * 256 + 128],
                                    lnT[:, fc * 512:(fc + 1) * 512],
                                    start=(fc == 0), stop=(fc == 15))
                            return last
                        state["kmm"] = ph.add("tensor", kmm, waits=kdeps)
                        vdeps = list(lnT_evs_chunk) + [ld_wkv]
                        if state["vte_par"][par] is not None:
                            vdeps.append(state["vte_par"][par])
                        def vtm(tn, bankV=bankV):
                            last = None
                            for fc in range(16):
                                last = tn.matmul(
                                    bankV[:],
                                    wkv[:, fc * 256 + 128: fc * 256 + 256],
                                    lnT[:, fc * 512:(fc + 1) * 512],
                                    start=(fc == 0), stop=(fc == 15))
                            return last
                        state["vtm"] = ph.add("tensor", vtm, waits=vdeps)

                emit_p1()
                run_phase(p1)

                p2 = Phase("p2")
                tr_last = st1["tr_last"]
                bank_ev = st1["bank_ev"]
                och1_evs = []
                qev = {0: None, 1: None}
                qps = (psKV0, psKV1)
                qwar0 = {0: [st1["vte_par"][0]], 1: [st1["vte_par"][1]]}
                lnT_ready = {0: list(st1["lnT_evs_7"]), 1: och1_evs}
                qm_by_jc_och = {}
                qev_h = {}

                def emit_q(jc, och):
                    deps = list(lnT_ready[och]) + [wqld[jc]]
                    if qev[och] is not None:
                        deps.append(qev[och])
                    else:
                        deps += qwar0[och]
                    def qmm(tn, jc=jc, och=och):
                        last = None
                        base_l = och * 8192
                        for fc in range(16):
                            last = tn.matmul(
                                qps[och][:],
                                wqs[:, (jc % 3) * E + fc * 128:
                                    (jc % 3) * E + (fc + 1) * 128],
                                lnT[:, base_l + fc * 512:
                                    base_l + (fc + 1) * 512],
                                start=(fc == 0), stop=(fc == 15))
                        return last
                    qm = p2.add("tensor", qmm, waits=deps)
                    qm_by_jc_och[(jc, och)] = qm
                    qev[och] = p2.add(
                        "scalar", lambda s, jc=jc, och=och: s.activation(
                            qT[:, jc * 1024 + och * 512:
                               jc * 1024 + (och + 1) * 512],
                            qps[och][:], AF.Identity, bias=bq[:, jc:jc + 1]),
                        waits=[qm, ld_const[-1]])
                    qev_h.setdefault(jc, {})[och] = qev[och]
                    return qm

                def post_jc(jc):
                    wq_slot_rd[jc % 3] = qm_by_jc_och[(jc, 1)]
                    if jc + 3 < H:
                        emit_wq_ld(p2, jc + 3)

                emit_q(0, 0)
                emit_q(1, 0)
                for ti in range(4):
                    apw = [st1["sv_mu"], st1["sv_r"]] + st1["ldc_6"][ti]
                    if tr_last[ti] is not None:
                        apw.append(tr_last[ti])
                    ap = p2.add("vector", lambda v, ti=ti: v.tensor_scalar(
                        lnB[:, ti * E:(ti + 1) * E],
                        xch[:, ti * E:(ti + 1) * E],
                        stv[:, 140 + ti:141 + ti],
                        stv[:, 148 + ti:149 + ti],
                        ALU.subtract, ALU.mult), waits=apw)
                    for ss_ in range(2):
                        bank = (psT0, psT1)[ss_]
                        deps = [ap, ld_const[-1]]
                        if bank_ev[ss_] is not None:
                            deps.append(bank_ev[ss_])
                        def tre2(tn, ti=ti, ss_=ss_, bank=bank):
                            last = None
                            for q in range(8):
                                fc = ss_ * 8 + q
                                last = tn.transpose(
                                    bank[:, q * 128:(q + 1) * 128],
                                    lnB[:, ti * E + fc * 128:
                                        ti * E + (fc + 1) * 128], idB[:])
                            return last
                        st = p2.add("tensor", tre2, waits=deps)
                        def eva2(v, ss_=ss_, ti=ti, bank=bank):
                            dstv = lnT[:, 8192:16384].rearrange(
                                "p (fc t) -> p fc t", fc=16)
                            return v.tensor_copy(
                                dstv[:, ss_ * 8:(ss_ + 1) * 8,
                                     ti * 128:(ti + 1) * 128],
                                bank[:].rearrange("p (q t) -> p q t", q=8))
                        e = p2.add("vector", eva2, waits=[st])
                        bank_ev[ss_] = e
                        och1_evs.append(e)
                        tr_last[ti] = st
                emit_q(0, 1)
                post_jc(0)
                emit_q(1, 1)
                post_jc(1)
                for jc in range(2, H):
                    emit_q(jc, 0)
                    emit_q(jc, 1)
                    post_jc(jc)
                q2mm_prev = None
                red_prev = None
                acc_prev = None
                for h in range(H):
                    sqw = [qev_h[h][0], qev_h[h][1]]
                    if q2mm_prev is not None:
                        sqw.append(q2mm_prev)
                    q2sq = p2.add("scalar", lambda s, h=h: s.activation(
                        k2s[:, 0:1024], qT[:, h * 1024:(h + 1) * 1024], AF.Square),
                        waits=sqw)
                    for g in range(2):
                        w = [q2sq, ld_const[-1]]
                        if red_prev is not None:
                            w.append(red_prev)
                        else:
                            w.append(st1["k2red"])
                        q2mm = p2.add("tensor", lambda tn, g=g: tn.matmul(
                            psk2[:], onesC[:], k2s[:, g * 512:(g + 1) * 512],
                            start=True, stop=True), waits=w)
                        q2mm_prev = q2mm
                        rw = [q2mm]
                        if acc_prev is not None:
                            rw.append(acc_prev)
                        red = p2.add("vector", lambda v: v.reduce_max(
                            s11[0:1, 0:1], psk2[0:1, :], AX.X), waits=rw)
                        if h == 0:
                            acc = p2.add("vector", lambda v, g=g: v.tensor_copy(
                                mq2[0:1, g:g + 1], s11[0:1, 0:1]), waits=[red])
                        else:
                            acc = p2.add("vector", lambda v, g=g: v.tensor_tensor(
                                mq2[0:1, g:g + 1], mq2[0:1, g:g + 1],
                                s11[0:1, 0:1], ALU.max), waits=[red])
                        red_prev = red
                        acc_prev = acc
                if debug:
                    d1 = p2.add("sync", dma(dbg["KT"][:, :], KT[:]), inc=16, sem="ddbg")
                    d2 = p2.add("sync", dma(dbg["V"][:, :], V[:]), inc=16, sem="ddbg")
                    d3 = p2.add("sync", dma(dbg["qT"][:, :], qT[:]), inc=16, sem="ddbg")
                    p2.add("sync", lambda s: None, waits=[d3], inc=0)
                run_phase(p2)

        with ExitStack() as sB:
            sB1 = sB.enter_context(ExitStack())
            attnT = sB1.enter_context(nc.sbuf_tensor("attnT", [128, H * 1024], BF16, side="right"))
            wfcj = sB1.enter_context(nc.sbuf_tensor("wfcj", [128, 2 * 16 * 512], BF16, side="right"))
            m3 = sB1.enter_context(ExitStack())
            pT = m3.enter_context(nc.sbuf_tensor("pT", [128, 4 * 1024], BF16, side="right"))
            pacc = m3.enter_context(nc.sbuf_tensor("pacc", [128, 2 * 1024], BF16, side="right"))
            masks = m3.enter_context(nc.sbuf_tensor("masks_s", [128, 4 * 512], BF16, side="right"))
            rcs = m3.enter_context(nc.sbuf_tensor("rcs", [1, 4 * 512], F32, side="right"))
            rcsum = m3.enter_context(nc.sbuf_tensor("rcsum", [1, 4 * 512], F32, side="right"))
            aTu = m3.enter_context(nc.sbuf_tensor("aTu", [128, 4 * 512], BF16, side="right"))
            rbs = m3.enter_context(nc.sbuf_tensor("rbs", [128, 4 * 512], F32, side="right"))

            wfc_r = wfc_h.rearrange("(fc p) j -> p fc j", p=128)
            wfcj_r = wfcj[:].rearrange("p (b fc j) -> p b fc j", b=2, fc=16)

            with nc.psum_tensor("psS0", [128, 1024], F32) as psS0, \
                 nc.psum_tensor("psS1", [128, 1024], F32) as psS1, \
                 nc.psum_tensor("psA0", [128, 512], F32) as psA0, \
                 nc.psum_tensor("psA1", [128, 512], F32) as psA1, \
                 nc.psum_tensor("psR0", [128, 512], F32) as psR0, \
                 nc.psum_tensor("psR1", [128, 512], F32) as psR1:
                psS = (psS0, psS1)
                psA = (psA0, psA1)
                psR = (psR0, psR1)

                p3 = Phase("p3")
                ld_mask = p3.add("sync", dma(masks[:], mask_h[:, :]), inc=16,
                                 sem="dmask")
                ld_wfc01 = []
                for j in range(2):
                    ld_wfc01.append(p3.add("gpsimd", dma(
                        wfcj_r[:, j], wfc_r[:, :, j * 512:(j + 1) * 512]), inc=16,
                        sem="dwfc%d" % j))

                arms = []
                for c in range(4):
                    arm = Phase(f"p3arm{c}")
                    own = (c, 7 - c)
                    perm = [p for p in range(8) if p not in own] + [7 - c, c]
                    inv = {p: s for s, p in enumerate(perm)}
                    extents = (4 * (c + 1), 4 * (8 - c))
                    kc_lists = []
                    for g in range(2):
                        ephys = c if g == 0 else 7 - c
                        slots_nd = sorted(inv[p] for p in range(ephys))
                        kl = [s * 4 + q for s in slots_nd for q in range(4)]
                        kl += [inv[ephys] * 4 + q for q in range(4)]
                        kc_lists.append(kl)
                    bset = []
                    for g in range(2):
                        ephys = c if g == 0 else 7 - c
                        slots = sorted(inv[p] for p in range(ephys + 1))
                        m0 = arm.add("vector", lambda v, s0=slots[0], g=g:
                                     v.tensor_copy(s11[0:1, 14 + g:15 + g],
                                                   kmax8[0:1, s0:s0 + 1]))
                        for s in slots[1:]:
                            m0 = arm.add("vector", lambda v, s=s, g=g:
                                         v.tensor_tensor(
                                             s11[0:1, 14 + g:15 + g],
                                             s11[0:1, 14 + g:15 + g],
                                             kmax8[0:1, s:s + 1], ALU.max),
                                         waits=[m0])
                        b1 = arm.add("vector", lambda v, g=g:
                                     v.tensor_tensor(
                                         s11[0:1, 8 + g:9 + g], mq2[0:1, g:g + 1],
                                         s11[0:1, 14 + g:15 + g], ALU.mult),
                                     waits=[m0])
                        b2 = arm.add("scalar", lambda s, g=g: s.activation(
                            s11[0:1, 10 + g:11 + g], s11[0:1, 8 + g:9 + g],
                            AF.Sqrt, scale=1.0 / 128.0), waits=[b1])
                        b3 = arm.add("vector", lambda v, g=g: v.tensor_scalar(
                            s11[0:1, 12 + g:13 + g], s11[0:1, 10 + g:11 + g],
                            -1.01, -0.25, ALU.mult, ALU.add), waits=[b2])
                        b4 = arm.add("tensor", lambda tn, g=g: tn.matmul(
                            psR0[:, g:g + 1], onesRf[:],
                            s11[0:1, 12 + g:13 + g], start=True, stop=True),
                            waits=[b3, ld_const[-1]])
                        b5 = arm.add("vector", lambda v, g=g: v.tensor_copy(
                            bB[:, g:g + 1], psR0[:, g:g + 1]), waits=[b4])
                        bset.append(b5)

                    prev_eager = None
                    pending_lazy = None
                    lazy_hist = [None, None]
                    acc_war = [None, None]
                    last_accs = None
                    bseq = 0
                    for hp in range(8):
                        h0, h1 = 2 * hp, 2 * hp + 1
                        for g in range(2):
                            Eg = extents[g]
                            kcl = kc_lists[g]
                            exp_steps = [None] * Eg
                            post_steps = [None] * Eg
                            cons_steps = [None] * Eg
                            acc_steps = [None] * Eg
                            s2_steps = [None] * Eg

                            def mk_s2(kc, it, m=None, h0=h0, h1=h1, g=g):
                                def f(tn):
                                    b = psS[it % 2]
                                    tn.matmul(
                                        b[:, 0:512],
                                        KT[:, kc * 128:(kc + 1) * 128],
                                        qT[:, h0 * 1024 + g * 512: h0 * 1024 + (g + 1) * 512],
                                        start=True, stop=(m is None))
                                    if m is not None:
                                        tn.matmul(
                                            b[:, 0:512], idB[:],
                                            masks[:, m * 512:(m + 1) * 512],
                                            start=False, stop=True)
                                    last = tn.matmul(
                                        b[:, 512:1024],
                                        KT[:, kc * 128:(kc + 1) * 128],
                                        qT[:, h1 * 1024 + g * 512: h1 * 1024 + (g + 1) * 512],
                                        start=True, stop=(m is None))
                                    if m is not None:
                                        last = tn.matmul(
                                            b[:, 512:1024], idB[:],
                                            masks[:, m * 512:(m + 1) * 512],
                                            start=False, stop=True)
                                    return last
                                return f

                            def mk_cons(kc, it, Eg=Eg):
                                def f(tn):
                                    first = (it == 0)
                                    last = (it == Eg - 1)
                                    buf = it % 4
                                    tn.matmul(psA[0][:],
                                              V[:, kc * 128:(kc + 1) * 128],
                                              pT[:, buf * 1024: buf * 1024 + 512],
                                              start=first, stop=last)
                                    return tn.matmul(
                                        psA[1][:], V[:, kc * 128:(kc + 1) * 128],
                                        pT[:, buf * 1024 + 512: buf * 1024 + 1024],
                                        start=first, stop=last)
                                return f

                            def mk_acc(it, bpar):
                                def f(v):
                                    dst = pacc[:, bpar * 1024:
                                               bpar * 1024 + 1024]
                                    src = pT[:, (it % 4) * 1024:
                                             (it % 4) * 1024 + 1024]
                                    if it == 0:
                                        return v.tensor_copy(dst, src)
                                    return v.tensor_tensor(dst, dst, src,
                                                           ALU.add)
                                return f

                            def mk_rowsum(bpar):
                                def f(tn):
                                    tn.matmul(psR[0][0:1, :], onesC[:],
                                              pacc[:, bpar * 1024:
                                                   bpar * 1024 + 512],
                                              start=True, stop=True)
                                    return tn.matmul(
                                        psR[1][0:1, :], onesC[:],
                                        pacc[:, bpar * 1024 + 512:
                                             bpar * 1024 + 1024],
                                        start=True, stop=True)
                                return f

                            def mk_exp(it, g=g):
                                def f(s):
                                    return s.activation(
                                        pT[:, (it % 4) * 1024:(it % 4) * 1024 + 1024],
                                        psS[it % 2][:, 0:1024], AF.Exp,
                                        bias=bB[:, g:g + 1], scale=INVSQ)
                                return f

                            def mk_mask(it, m, half):
                                def f(v):
                                    base = (it % 4) * 1024 + half * 512
                                    return v.tensor_tensor(
                                        pT[:, base:base + 512],
                                        pT[:, base:base + 512],
                                        masks[:, m * 512:(m + 1) * 512], ALU.mult)
                                return f

                            def emit_eager(rs_last, av_last, lazy_prev2, par):
                                ev = []
                                for hh in (0, 1):
                                    w = [rs_last]
                                    if lazy_prev2 is not None:
                                        w.append(lazy_prev2["dm1"][hh])
                                    ev.append(arm.add(
                                        "vector", lambda v, hh=hh, par=par:
                                        v.tensor_copy(
                                            rcsum[0:1, par * 1024 + hh * 512:
                                                  par * 1024 + (hh + 1) * 512],
                                            psR[hh][0:1, :]), waits=w))
                                for hh in (0, 1):
                                    w = [av_last]
                                    if lazy_prev2 is not None:
                                        w.append(lazy_prev2["mul"][hh])
                                    ev.append(arm.add(
                                        "scalar", lambda s, hh=hh, par=par:
                                        s.activation(
                                            aTu[:, par * 1024 + hh * 512:
                                                par * 1024 + (hh + 1) * 512],
                                            psA[hh][:], AF.Identity), waits=w))
                                return ev

                            def lazy_stage1(bspec, out):
                                h0b, h1b, gb, ev, lazy_prev2, par = bspec
                                for hh in (0, 1):
                                    o = par * 1024 + hh * 512
                                    rs_ = rcsum[0:1, o:o + 512]
                                    di = par * 2 + hh
                                    rw = [ev[hh]]
                                    if lazy_prev2 is not None:
                                        rw.append(lazy_prev2["dma"][hh])
                                    dm1 = arm.add("sync",
                                                  lambda g_, rs_=rs_, di=di:
                                                  g_.dma_start(
                                                      out=rbd_h[di:di + 1, :],
                                                      in_=rs_),
                                                  waits=rw, inc=16,
                                                  sem="dbo%d" % hh)
                                    out["dm1"].append(dm1)

                            def lazy_stage2(bspec, out):
                                h0b, h1b, gb, ev, lazy_prev2, par = bspec
                                for hh in (0, 1):
                                    o = par * 1024 + hh * 512
                                    rb = rbs[:, o:o + 512]
                                    di = par * 2 + hh
                                    dw2 = [out["dm1"][hh]]
                                    if lazy_prev2 is not None:
                                        dw2.append(lazy_prev2["mul"][hh])
                                    dmb = arm.add("sync",
                                                  lambda g_, rb=rb, di=di:
                                                  g_.dma_start(
                                                      out=rb,
                                                      in_=rbd_h[di:di + 1, :]
                                                      .broadcast_to((128, 512))),
                                                  waits=dw2, inc=16,
                                                  sem="dbr%d" % hh)
                                    out["dma"].append(dmb)

                            def lazy_stage3(bspec, out):
                                h0b, h1b, gb, ev, lazy_prev2, par = bspec
                                for hh, hid in ((0, h0b), (1, h1b)):
                                    o = par * 1024 + hh * 512
                                    rb = rbs[:, o:o + 512]
                                    au = aTu[:, o:o + 512]
                                    rec = arm.add(
                                        "vector", lambda v, rb=rb:
                                        v.reciprocal(rb, rb),
                                        waits=[out["dma"][hh]])
                                    mul = arm.add(
                                        "vector",
                                        lambda v, rb=rb, au=au, hid=hid, gb=gb:
                                        v.tensor_tensor(
                                            attnT[:, hid * 1024 + gb * 512:
                                                  hid * 1024 + (gb + 1) * 512],
                                            au, rb, ALU.mult),
                                        waits=[rec, ev[2 + hh]])
                                    out["mul"].append(mul)

                            def emit_lazy(bspec):
                                out = {"rec": [], "dm1": [], "dma": [], "mul": []}
                                lazy_stage1(bspec, out)
                                lazy_stage2(bspec, out)
                                lazy_stage3(bspec, out)
                                return out

                            def mk_av_last(kc, it):
                                def f(tn):
                                    buf = it % 4
                                    first = (it == 0)
                                    tn.matmul(psA[0][:],
                                              V[:, kc * 128:(kc + 1) * 128],
                                              pT[:, buf * 1024: buf * 1024 + 512],
                                              start=first, stop=True)
                                    return tn.matmul(
                                        psA[1][:], V[:, kc * 128:(kc + 1) * 128],
                                        pT[:, buf * 1024 + 512: buf * 1024 + 1024],
                                        start=first, stop=True)
                                return f

                            for it in range(Eg):
                                kc = kcl[it]
                                deps = list(bset) if (hp == 0 and g == 0 and it == 0) else []
                                if it >= 2:
                                    deps.append(exp_steps[it - 2])
                                elif hp > 0 or g > 0:
                                    deps.append(last_exps[it - 2])
                                mmask = it - (Eg - 4) if it >= Eg - 4 else None
                                if mmask is not None:
                                    deps.append(ld_mask)
                                s2_steps[it] = arm.add("tensor",
                                                       mk_s2(kc, it, mmask),
                                                       waits=deps)
                                edeps = [s2_steps[it]]
                                if it >= 4:
                                    edeps.append(cons_steps[it - 4])
                                    edeps.append(acc_steps[it - 4])
                                elif hp > 0 or g > 0:
                                    edeps.append(last_cons[it - 4])
                                    edeps.append(last_accs[it - 4])
                                if hp == 0 and g == 0 and it < 2:
                                    edeps += list(bset)
                                exp_steps[it] = arm.add("scalar", mk_exp(it),
                                                        waits=edeps)
                                post_steps[it] = exp_steps[it]
                                aw = [post_steps[it]]
                                if it == 0:
                                    if acc_war[bseq % 2] is not None:
                                        aw.append(acc_war[bseq % 2])
                                else:
                                    aw.append(acc_steps[it - 1])
                                acc_steps[it] = arm.add(
                                    "vector", mk_acc(it, bseq % 2), waits=aw)
                                if pending_lazy is not None:
                                    if it == 1:
                                        lazy_out = {"rec": [], "dm1": [],
                                                    "dma": [], "mul": []}
                                        lazy_stage1(pending_lazy, lazy_out)
                                    elif it == 2:
                                        lazy_stage2(pending_lazy, lazy_out)
                                    elif it == 3:
                                        lazy_stage3(pending_lazy, lazy_out)
                                        lazy_hist[pending_lazy[5]] = lazy_out
                                        pending_lazy = None
                                if it >= 1:
                                    cdeps = [post_steps[it - 1]]
                                    if it == 1 and prev_eager is not None:
                                        cdeps += prev_eager
                                    cons_steps[it - 1] = arm.add(
                                        "tensor", mk_cons(kcl[it - 1], it - 1),
                                        waits=cdeps)
                            av_last = arm.add("tensor", mk_av_last(kcl[Eg - 1],
                                                                   Eg - 1),
                                              waits=[post_steps[Eg - 1]])
                            cons_steps[Eg - 1] = av_last
                            rsw = [acc_steps[Eg - 1], ld_const[-1]]
                            if prev_eager is not None:
                                rsw += prev_eager[0:2]
                            rsm = arm.add("tensor", mk_rowsum(bseq % 2),
                                          waits=rsw)
                            acc_war[bseq % 2] = rsm
                            last_exps = exp_steps
                            last_cons = cons_steps
                            last_accs = acc_steps
                            par = bseq % 2
                            eager_ev = emit_eager(rsm, av_last,
                                                  lazy_hist[par], par)
                            prev_eager = eager_ev
                            pending_lazy = (h0, h1, g, eager_ev,
                                            lazy_hist[par], par)
                            bseq += 1
                    lazy_hist[pending_lazy[5]] = emit_lazy(pending_lazy)
                    pending_lazy = None
                    arms.append(arm)
                run_phase(p3, switch={c: arms[c] for c in range(4)})

                if debug:
                    pd = Phase("p3d")
                    dd = pd.add("sync", dma(dbg["attnT"][:, :], attnT[:]), inc=16,
                                sem="ddbg")
                    pd.add("sync", lambda s: None, waits=[dd], inc=0)
                    run_phase(pd)

            m3.close()
            sKVQ.close()
            x1 = sB.enter_context(nc.sbuf_tensor("x1", [128, 8 * E], F32))
            fst = sB.enter_context(nc.sbuf_tensor("fst", [128, 2 * 512], BF16))
            ost = sB.enter_context(nc.sbuf_tensor("ost", [128, 4 * 512], F32))
            m4 = sB1.enter_context(ExitStack())
            xos = m4.enter_context(nc.sbuf_tensor("xos", [128, 4 * 512], F32, side="right"))
            sqJ4 = m4.enter_context(nc.sbuf_tensor("sqJ4", [128, 512], F32, side="right"))
            wTs = m4.enter_context(nc.sbuf_tensor("wTs", [128, 4 * E], BF16, side="right"))
            xqb = sB.enter_context(nc.sbuf_tensor("xqb", [128, 2 * E], F32))
            with nc.psum_tensor("psF0", [128, 512], F32) as psF0, \
                 nc.psum_tensor("psF1", [128, 512], F32) as psF1, \
                 nc.psum_tensor("psF2", [128, 512], F32) as psF2, \
                 nc.psum_tensor("psF3", [128, 512], F32) as psF3, \
                 nc.psum_tensor("psW", [1, 128], F32) as psW:
                psF = (psF0, psF1, psF2, psF3)
                p4 = Phase("p4")
                ld_wfc = {0: ld_wfc01[0], 1: ld_wfc01[1]}
                evac = {}
                bank_prev = [None, None, None, None]
                xos_prev = [None, None, None, None]
                mm_by_j = {0: [], 1: [], 2: [], 3: []}
                stat_tail = {}
                stwar = [None] * 32
                sq4_prev = [None]
                wT_rc = wT_h.rearrange("(fc p) j -> fc p j", p=128)
                wam = {"buf": [None] * 4, "acc": None}

                def emit_wamax(fc):
                    wl = [wam["buf"][fc % 4]] if wam["buf"][fc % 4] else []
                    ldw = p4.add("sync", dma(
                        wTs[:, (fc % 4) * E:(fc % 4 + 1) * E], wT_rc[fc]),
                        waits=wl, inc=16, sem="dwt%d" % (fc % 4))
                    red = p4.add("vector", lambda v, fc=fc: v.tensor_reduce(
                        stv[:, 64:65], wTs[:, (fc % 4) * E:(fc % 4 + 1) * E],
                        AX.X, ALU.max, apply_absolute_value=True),
                        waits=[ldw] + ([wam["acc"]] if wam["acc"] else []))
                    if fc == 0:
                        acc = p4.add("vector", lambda v: v.tensor_copy(
                            amw[:, 0:1], stv[:, 64:65]), waits=[red])
                    else:
                        acc = p4.add("vector", lambda v: v.tensor_tensor(
                            amw[:, 0:1], amw[:, 0:1], stv[:, 64:65], ALU.max),
                            waits=[red, wam["acc"]])
                    wam["buf"][fc % 4] = red
                    wam["acc"] = acc

                n = 0
                for j in range(4):
                    if j >= 2:
                        ld_wfc[j] = p4.add("gpsimd", dma(
                            wfcj_r[:, j % 2],
                            wfc_r[:, :, j * 512:(j + 1) * 512]),
                            waits=list(mm_by_j[j - 2]), inc=16,
                            sem="dwfc%d" % (j % 2))
                    for tt in range(8):
                        xw = [xos_prev[n % 4]] if xos_prev[n % 4] else []
                        xd = p4.add("sync", dma(
                            xos[:, (n % 4) * 512:(n % 4 + 1) * 512],
                            xo_h[tt * 128:(tt + 1) * 128, j * 512:(j + 1) * 512]),
                            waits=xw, inc=16, sem="dxo%d" % (n % 4))
                        deps = [ld_wfc[j], xd]
                        if bank_prev[n % 4] is not None:
                            deps.append(bank_prev[n % 4])
                        def fcmm(tn, j=j, tt=tt, bank=psF[n % 4]):
                            last = tn.matmul(bank[:], onesR[:],
                                             bfcb[0:1, j * 512:(j + 1) * 512],
                                             start=True, stop=False)
                            for ec in range(16):
                                last = tn.matmul(
                                    bank[:],
                                    attnT[:, ec * 1024 + tt * 128: ec * 1024 + (tt + 1) * 128],
                                    wfcj[:, (j % 2) * 8192 + ec * 512:
                                         (j % 2) * 8192 + (ec + 1) * 512],
                                    start=False, stop=(ec == 15))
                            return last
                        mm = p4.add("tensor", fcmm,
                                    waits=deps + [ld_const[-1], ld_bfcb])
                        mm_by_j[j].append(mm)
                        ev = p4.add("vector",
                                    lambda v, j=j, tt=tt, n=n, bank=psF[n % 4]:
                                    v.scalar_tensor_tensor(
                                        x1[:, tt * E + j * 512: tt * E + (j + 1) * 512],
                                        bank[:], 1.0,
                                        xos[:, (n % 4) * 512:(n % 4 + 1) * 512],
                                        ALU.mult, ALU.add,
                                        accum_out=stv[:, tt * 4 + j:tt * 4 + j + 1]),
                                    waits=[mm, xd] +
                                          ([stwar[tt * 4 + j]] if stwar[tt * 4 + j]
                                           else []))
                        evac[(j, tt)] = ev
                        x1sl = x1[:, tt * E + j * 512: tt * E + (j + 1) * 512]
                        sq_p = p4.add("scalar", lambda s, x1sl=x1sl, j=j, tt=tt:
                                      s.activation(sqJ4[:, :], x1sl, AF.Square,
                                                   accum_out=stv[:, 32 + tt * 4 + j:
                                                                 33 + tt * 4 + j]),
                                      waits=[ev] + ([sq4_prev[0]] if sq4_prev[0]
                                                    else []))
                        sq4_prev[0] = sq_p
                        rmx = p4.add("vector", lambda v, x1sl=x1sl, j=j, tt=tt:
                                     v.tensor_reduce(
                                         stv[:, 136 + tt * 4 + j:137 + tt * 4 + j],
                                         x1sl, AX.X, ALU.max), waits=[ev])
                        rmn = p4.add("vector", lambda v, x1sl=x1sl, j=j, tt=tt:
                                     v.tensor_reduce(
                                         stv[:, 168 + tt * 4 + j:169 + tt * 4 + j],
                                         x1sl, AX.X, ALU.min), waits=[ev])
                        stat_tail[(j, tt)] = (sq_p, rmx, rmn)
                        stwar[tt * 4 + j] = None
                        bank_prev[n % 4] = ev
                        xos_prev[n % 4] = ev
                        if n < 16:
                            emit_wamax(n)
                        n += 1
                trw = p4.add("tensor", lambda tn: tn.transpose(
                    psW[0:1, :], amw[:, 0:1], idT[:]),
                    waits=[wam["acc"], ld_const[-1]])
                redw = p4.add("vector", lambda v: v.reduce_max(
                    s11[0:1, 1:2], psW[0:1, :], AX.X), waits=[trw])
                swst = p4.add("vector", lambda v: v.tensor_scalar(
                    s11[0:1, 2:3], s11[0:1, 1:2], 1.0 / 448.0, 1e-12,
                    ALU.mult, ALU.max), waits=[redw])
                rsw = p4.add("vector", lambda v: v.reciprocal(
                    s11[0:1, 3:4], s11[0:1, 2:3]), waits=[swst])
                i2sw = p4.add("vector", lambda v: v.tensor_scalar(
                    s11[0:1, 4:5], s11[0:1, 3:4], 0.5, None, ALU.mult),
                    waits=[rsw])
                bw = p4.add("tensor", lambda tn: tn.matmul(
                    psF0[:, 0:1], onesRf[:], s11[0:1, 4:5],
                    start=True, stop=True),
                    waits=[i2sw, evac[(0, 0)], bank_prev[0], ld_const[-1]])
                bwc = p4.add("vector", lambda v: v.tensor_copy(
                    colv[:, 1:2], psF0[:, 0:1]), waits=[bw])
                sums8 = stv[:, 72:80]
                sumsq8 = stv[:, 80:88]
                mx8 = stv[:, 88:96]
                mn8 = stv[:, 96:104]
                t8 = stv[:, 104:112]
                u8 = stv[:, 112:120]
                r08 = stv[:, 120:128]
                tmp8 = stv[:, 128:136]
                mu8 = mus[:, 0:8]
                r8 = mus[:, 8:16]
                combs = []
                for tt in range(8):
                    wsum = [evac[(j, tt)] for j in range(4)]
                    wtail = [stat_tail[(j, tt)] for j in range(4)]
                    combs.append(p4.add("vector", lambda v, tt=tt: v.tensor_reduce(
                        sums8[:, tt:tt + 1], stv[:, tt * 4:tt * 4 + 4], AX.X,
                        ALU.add), waits=wsum))
                    combs.append(p4.add("vector", lambda v, tt=tt: v.tensor_reduce(
                        sumsq8[:, tt:tt + 1], stv[:, 32 + tt * 4:36 + tt * 4],
                        AX.X, ALU.add), waits=[w[0] for w in wtail]))
                    combs.append(p4.add("vector", lambda v, tt=tt: v.tensor_reduce(
                        mx8[:, tt:tt + 1], stv[:, 136 + tt * 4:140 + tt * 4],
                        AX.X, ALU.max), waits=[w[1] for w in wtail]))
                    combs.append(p4.add("vector", lambda v, tt=tt: v.tensor_reduce(
                        mn8[:, tt:tt + 1], stv[:, 168 + tt * 4:172 + tt * 4],
                        AX.X, ALU.min), waits=[w[2] for w in wtail]))
                c1 = p4.add("vector", lambda v: v.tensor_scalar(
                    mu8, sums8, 1.0 / 2048.0, None, ALU.mult), waits=combs)
                c2 = p4.add("vector", lambda v: v.tensor_scalar(
                    t8, sumsq8, 1.0 / 2048.0, None, ALU.mult), waits=combs)
                c3 = p4.add("vector", lambda v: v.tensor_tensor(
                    tmp8, mu8, mu8, ALU.mult), waits=[c1])
                c4 = p4.add("vector", lambda v: v.tensor_tensor(
                    t8, t8, tmp8, ALU.subtract), waits=[c2, c3])
                c5 = p4.add("vector", lambda v: v.tensor_scalar(
                    t8, t8, EPS, None, ALU.add), waits=[c4])
                c6 = p4.add("vector", lambda v: v.reciprocal(u8, t8), waits=[c5])
                c7 = p4.add("scalar", lambda s: s.activation(
                    r08, u8, AF.Sqrt), waits=[c6])
                c8 = p4.add("vector", lambda v: v.tensor_tensor(
                    tmp8, r08, r08, ALU.mult), waits=[c7])
                c9 = p4.add("vector", lambda v: v.tensor_tensor(
                    tmp8, tmp8, t8, ALU.mult), waits=[c8])
                c10 = p4.add("vector", lambda v: v.tensor_scalar(
                    tmp8, tmp8, -0.5, 1.5, ALU.mult, ALU.add), waits=[c9])
                c11 = p4.add("vector", lambda v: v.tensor_tensor(
                    r8, r08, tmp8, ALU.mult), waits=[c10])
                a1 = p4.add("vector", lambda v: v.tensor_tensor(
                    mx8, mx8, mu8, ALU.subtract), waits=[c11])
                a2 = p4.add("vector", lambda v: v.tensor_tensor(
                    mn8, mu8, mn8, ALU.subtract), waits=[c11, a1])
                a3 = p4.add("vector", lambda v: v.tensor_tensor(
                    mx8, mx8, mn8, ALU.max), waits=[a2])
                a4 = p4.add("vector", lambda v: v.tensor_tensor(
                    mx8, mx8, r8, ALU.mult), waits=[a3])
                a5 = p4.add("vector", lambda v: v.tensor_reduce(
                    amx[:, 0:1], mx8, AX.X, ALU.max), waits=[a4])
                trx = p4.add("tensor", lambda tn: tn.transpose(
                    psW[0:1, :], amx[:, 0:1], idT[:]),
                    waits=[a5, redw, ld_const[-1]])
                redx = p4.add("vector", lambda v: v.reduce_max(
                    s11[0:1, 5:6], psW[0:1, :], AX.X), waits=[trx])
                nm8a = p4.add("vector", lambda v: v.tensor_tensor(
                    stv[:, 72:80], mu8, r8, ALU.mult), waits=[c11])
                nm8b = p4.add("vector", lambda v: v.tensor_scalar(
                    stv[:, 72:80], stv[:, 72:80], -1.0, None, ALU.mult),
                    waits=[nm8a])
                q1_steps = {}
                for tt in range(2):
                    q1_steps[tt] = p4.add(
                        "scalar", lambda s, tt=tt: s.activation(
                            xqb[:, (tt % 2) * E:(tt % 2 + 1) * E],
                            x1[:, tt * E:(tt + 1) * E], AF.Identity,
                            bias=stv[:, 72 + tt:73 + tt],
                            scale=mus[:, 8 + tt:9 + tt]), waits=[nm8b])
                if debug:
                    dd4 = p4.add("sync", dma(dbg["x1"][:, :], x1[:]),
                                 waits=[evac[(j, tt)] for j in range(4)
                                        for tt in range(8)], inc=16, sem="ddbg")
                    p4.add("sync", lambda s: None, waits=[dd4], inc=0)
                run_phase(p4)

            sB1.close()
            with ExitStack() as sC:
                lnqT = sC.enter_context(nc.sbuf_tensor("lnqT", [128, 16 * 1024], F8, side="right"))
                wq8 = sC.enter_context(nc.sbuf_tensor("wq8", [128, 4 * 16 * 512], F8, side="right"))
                wqj = sC.enter_context(nc.sbuf_tensor("wqj", [128, 2 * 16 * 512], BF16, side="right"))
                with nc.psum_tensor("psQ0", [128, 1024], F32) as psQ0, \
                     nc.psum_tensor("psQ1", [128, 1024], F32) as psQ1, \
                     nc.psum_tensor("psN0", [128, 512], F32) as psN0, \
                     nc.psum_tensor("psN1", [128, 512], F32) as psN1, \
                     nc.psum_tensor("psN2", [128, 512], F32) as psN2, \
                     nc.psum_tensor("psN3", [128, 512], F32) as psN3:
                    psQ = (psQ0, psQ1)
                    psN = (psN0, psN1, psN2, psN3)
                    p5 = Phase("p5")
                    wT_rj = wT_h.rearrange("(fc p) j -> p fc j", p=128)
                    wqj_r = wqj[:].rearrange("p (b fc j) -> p b fc j", b=2, fc=16)
                    ld_wTj = {}
                    wq_steps = {}
                    for j in range(4):
                        wl = [] if j < 2 else [wq_steps[j - 2]]
                        ld_wTj[j] = p5.add("sync", dma(
                            wqj_r[:, j % 2], wT_rj[:, :, j * 512:(j + 1) * 512]),
                            waits=wl, inc=16, sem="dwj%d" % (j % 2))
                        if j % 2 == 0:
                            wq1 = p5.add("scalar", lambda s, j=j: s.activation(
                                wq8[:, j * 8192:(j + 1) * 8192],
                                wqj[:, (j % 2) * 8192:(j % 2) * 8192 + 8192],
                                AF.Identity, scale=colv[:, 1:2]),
                                waits=[ld_wTj[j], bwc])
                        else:
                            wq1 = p5.add("vector", lambda v, j=j: v.tensor_scalar(
                                wq8[:, j * 8192:(j + 1) * 8192],
                                wqj[:, (j % 2) * 8192:(j % 2) * 8192 + 8192],
                                colv[:, 1:2], None, ALU.mult),
                                waits=[ld_wTj[j]])
                        wq_steps[j] = wq1
                    tox = p5.add("gpsimd", dma(cc_in[:, :], s11[0:1, 5:6]),
                                 waits=[redx], inc=16, sem="dcci")
                    cc = p5.add("gpsimd", lambda gp: gp.collective_compute(
                        "AllReduce", ALU.max, replica_groups=[list(range(8))],
                        ins=[cc_in[:, :]], outs=[cc_out[:, :]]), waits=[tox])
                    fromx = p5.add("gpsimd", dma(s11[0:1, 6:7], cc_out[:, :]),
                                   waits=[cc], inc=16, sem="dcco")
                    sxst = p5.add("vector", lambda v: v.tensor_scalar(
                        s11[0:1, 7:8], s11[0:1, 6:7], 1.0 / 448.0, 1e-12,
                        ALU.mult, ALU.max), waits=[fromx])
                    rsx = p5.add("vector", lambda v: v.reciprocal(
                        s11[0:1, 8:9], s11[0:1, 7:8]), waits=[sxst])
                    i2sx = p5.add("vector", lambda v: v.tensor_scalar(
                        s11[0:1, 9:10], s11[0:1, 8:9], 0.5, None, ALU.mult),
                        waits=[rsx])
                    sc4a = p5.add("vector", lambda v: v.tensor_tensor(
                        s11[0:1, 10:11], s11[0:1, 7:8], s11[0:1, 2:3], ALU.mult),
                        waits=[sxst, swst])
                    sc4b = p5.add("vector", lambda v: v.tensor_scalar(
                        s11[0:1, 11:12], s11[0:1, 10:11], 4.0, None, ALU.mult),
                        waits=[sc4a])
                    dmW1 = p5.add("sync", lambda g_: g_.dma_start(
                        out=scb_h[0:1, 0:1], in_=s11[0:1, 9:10]),
                        waits=[i2sx], inc=16, sem="dbo0")
                    dmW2 = p5.add("sync", lambda g_: g_.dma_start(
                        out=scb_h[0:1, 1:2], in_=s11[0:1, 11:12]),
                        waits=[sc4b], inc=16, sem="dbo1")
                    bxc = p5.add("sync", lambda g_: g_.dma_start(
                        out=colv[:, 0:1], in_=scb_h[0:1, 0:1]
                        .broadcast_to((128, 1))), waits=[dmW1], inc=16,
                        sem="dbr0")
                    bxc2 = p5.add("sync", lambda g_: g_.dma_start(
                        out=colv[:, 2:3], in_=scb_h[0:1, 1:2]
                        .broadcast_to((128, 1))), waits=[dmW2], inc=16,
                        sem="dbr1")
                    lnq_done = {}
                    tr_by_tt = {}
                    bank_ev5 = [None, None]
                    for tt in range(8):
                        if tt < 2:
                            q1 = q1_steps[tt]
                        else:
                            q1 = p5.add("scalar", lambda s, tt=tt:
                                        s.activation(
                                            xqb[:, (tt % 2) * E:
                                                (tt % 2 + 1) * E],
                                            x1[:, tt * E:(tt + 1) * E],
                                            AF.Identity,
                                            bias=stv[:, 72 + tt:73 + tt],
                                            scale=mus[:, 8 + tt:9 + tt]),
                                        waits=[nm8b, tr_by_tt[tt - 2]])
                        evs = []
                        for ss_ in range(2):
                            bank = psQ[ss_]
                            deps = [q1, ld_const[-1]]
                            if bank_ev5[ss_] is not None:
                                deps.append(bank_ev5[ss_])
                            def tre5(tn, ss_=ss_, tt=tt, bank=bank):
                                last = None
                                for q in range(8):
                                    fc = ss_ * 8 + q
                                    last = tn.transpose(
                                        bank[:, q * 128:(q + 1) * 128],
                                        xqb[:, (tt % 2) * E + fc * 128:
                                            (tt % 2) * E + (fc + 1) * 128],
                                        idT[:])
                                return last
                            st = p5.add("tensor", tre5, waits=deps)
                            tr_by_tt[tt] = st
                            def eva5(v, ss_=ss_, tt=tt, bank=bank):
                                dstv = lnqT[:].rearrange("p (fc t) -> p fc t",
                                                         fc=16)
                                return v.tensor_scalar(
                                    dstv[:, ss_ * 8:(ss_ + 1) * 8,
                                         tt * 128:(tt + 1) * 128],
                                    bank[:].rearrange("p (q t) -> p q t", q=8),
                                    colv[:, 0:1], None, ALU.mult)
                            e = p5.add("vector", eva5, waits=[st, bxc])
                            bank_ev5[ss_] = e
                            evs.append(e)
                        lnq_done[tt] = evs
                    out_dmas = []
                    ffn_ev = {}
                    mm_lastj = {}
                    nn = 0
                    for j in range(4):
                        for tt in range(8):
                            deps = [wq_steps[j]] + list(lnq_done[tt])
                            if nn >= 4:
                                deps.append(ffn_ev[nn - 4])
                            def fmm(tn, j=j, tt=tt, bank=psN[nn % 4]):
                                last = None
                                lq = lnqT[:].rearrange("p (fc t) -> p fc t",
                                                       fc=16)
                                wv_ = wq8[:].rearrange("p (b fc jj) -> p b fc jj",
                                                       b=4, fc=16)
                                for fp in range(8):
                                    last = tn.matmul(
                                        bank[:],
                                        lq[:, 2 * fp:2 * fp + 2,
                                           tt * 128:(tt + 1) * 128],
                                        wv_[:, j, 2 * fp:2 * fp + 2, :],
                                        start=(fp == 0), stop=(fp == 7),
                                        perf_mode=mybir.MatmulPerfMode.DoubleRow)
                                return last
                            mm = p5.add("tensor", fmm, waits=deps)
                            if tt == 7:
                                mm_lastj[j] = mm
                            f1 = p5.add("vector",
                                        lambda v, nn=nn, bank=psN[nn % 4]:
                                        v.tensor_scalar(
                                            fst[:, (nn % 2) * 512:(nn % 2 + 1) * 512],
                                            bank[:], colv[:, 2:3], None, ALU.mult),
                                        waits=[mm, bxc2] +
                                              ([ffn_ev[nn - 2]] if nn >= 2 else []))
                            f2 = p5.add("vector", lambda v, nn=nn, j=j, tt=tt:
                                        v.tensor_tensor(
                                            ost[:, (nn % 4) * 512:(nn % 4 + 1) * 512],
                                            fst[:, (nn % 2) * 512:(nn % 2 + 1) * 512],
                                            x1[:, tt * E + j * 512: tt * E + (j + 1) * 512],
                                            ALU.add),
                                        waits=[f1] + ([out_dmas[nn - 4]]
                                                      if nn >= 4 else []))
                            ffn_ev[nn] = f2
                            od = p5.add("sync", dma(
                                y_h[tt * 128:(tt + 1) * 128, j * 512:(j + 1) * 512],
                                ost[:, (nn % 4) * 512:(nn % 4 + 1) * 512]),
                                waits=[f2], inc=16, sem="dos%d" % (nn % 4))
                            out_dmas.append(od)
                            nn += 1
                    p5.add("sync", lambda s: None,
                           waits=out_dmas[-4:], inc=0)
                    run_phase(p5)

    return nc


_NC_CACHE = {}


def _get_nc(debug=False):
    key = bool(debug)
    if key not in _NC_CACHE:
        _NC_CACHE[key] = build_nc(debug=key)
    return _NC_CACHE[key]


def make_in_maps(inputs):
    x = np.asarray(inputs["L0_attn_x"])
    w_qkv = np.ascontiguousarray(np.asarray(inputs["L0_attn_w_qkv"]))
    b_qkv = np.asarray(inputs["L0_attn_b_qkv"])
    w_fc = np.ascontiguousarray(np.asarray(inputs["L0_attn_w_fc"]))
    b_fc = np.asarray(inputs["L0_attn_b_fc"])
    w_ffn = np.asarray(inputs["L0_ffn_weight"])
    if w_ffn.dtype == np.uint16:
        w_ffn = w_ffn.view(ml_dtypes.bfloat16)

    ident = np.eye(128, dtype=np.float32)
    identb = np.eye(128, dtype=ml_dtypes.bfloat16)
    ident8 = np.eye(128, dtype=ml_dtypes.float8_e4m3)
    onesR = np.ones((1, 128), dtype=ml_dtypes.bfloat16)
    onesC = np.ones((128, 1), dtype=ml_dtypes.bfloat16)
    onesRf = np.ones((1, 128), dtype=np.float32)
    masks = np.zeros((128, 4, 512), dtype=ml_dtypes.bfloat16)
    ii = np.arange(128)[:, None]
    jj = np.arange(512)[None, :]
    for m in range(4):
        masks[:, m, :] = np.where((m * 128 + ii) <= jj, 0.0, -2000.0)
    masks = np.ascontiguousarray(masks.reshape(128, 2048))

    w_q = np.ascontiguousarray(w_qkv[:, :E])
    w_kv = np.ascontiguousarray(w_qkv[:, E:])
    b_q = np.ascontiguousarray(b_qkv[:E].reshape(H, D).T)
    b_k = np.ascontiguousarray(b_qkv[E:E + D].reshape(D, 1))
    b_v = np.ascontiguousarray(b_qkv[E + D:].reshape(D, 1))
    b_fc2 = np.ascontiguousarray(b_fc.reshape(1, E))
    wT = np.ascontiguousarray(w_ffn.T)

    in_maps = []
    for pid in range(8):
        bi, c = divmod(pid, 4)
        xo = np.concatenate([x[bi, c * 512:(c + 1) * 512],
                             x[bi, (7 - c) * 512:(8 - c) * 512]], axis=0)
        perm = [p for p in range(8) if p not in (c, 7 - c)] + [7 - c, c]
        xp = x[bi].reshape(8, 512, E)[perm].reshape(4096, E)
        in_maps.append({
            "xp": np.ascontiguousarray(xp),
            "xo": np.ascontiguousarray(xo),
            "w_q": w_q, "w_kv": w_kv, "b_q": b_q, "b_k": b_k, "b_v": b_v,
            "w_fc": w_fc, "b_fc": b_fc2, "w_ffnT": wT,
            "ident": ident, "identb": identb, "ident8": ident8,
            "onesR": onesR, "onesC": onesC, "onesRf": onesRf,
            "masks": masks,
        })
    return in_maps


def kernel(**inputs):
    in_maps = make_in_maps(inputs)
    nc = _get_nc(debug=False)
    res = run_bass_kernel_spmd(nc, in_maps, core_ids=list(range(8)))
    out = np.empty((2, 4096, 2048), dtype=np.float32)
    for pid in range(8):
        bi, c = divmod(pid, 4)
        y = res.results[pid]["y"]
        out[bi, c * 512:(c + 1) * 512] = y[:512]
        out[bi, (7 - c) * 512:(8 - c) * 512] = y[512:]
    return out


# version 31
# speedup vs baseline: 1.0560x; 1.0018x over previous
from contextlib import ExitStack

import numpy as np
import ml_dtypes

import concourse.bass as bass
import concourse.mybir as mybir
from concourse.bass_utils import run_bass_kernel_spmd

F32 = mybir.dt.float32
BF16 = mybir.dt.bfloat16
F8 = mybir.dt.float8e4
AF = mybir.ActivationFunctionType
ALU = mybir.AluOpType
AX = mybir.AxisListType

E = 2048
H = 16
D = 128
NSEQ = 4096
OWN = 1024
NCH = 8
INVSQ = float(np.float32(1.0) / np.float32(np.sqrt(np.float32(128.0))))
EPS = 1e-5

ENGS = ("sync", "gpsimd", "vector", "scalar", "tensor")


class Step:
    __slots__ = ("eng", "emit", "waits", "inc_n", "seq", "val", "sem")

    def __init__(self, eng, emit, waits, inc_n, sem=None):
        self.eng = eng
        self.emit = emit
        self.waits = [w for w in waits if w is not None]
        self.inc_n = inc_n
        self.seq = None
        self.val = None
        self.sem = sem if sem is not None else eng


class Phase:
    def __init__(self, name):
        self.name = name
        self.steps = []

    def add(self, eng, emit, waits=(), inc=1, sem=None):
        s = Step(eng, emit, list(waits), inc, sem=sem)
        s.seq = len(self.steps)
        self.steps.append(s)
        return s

    def resolve(self, ctr0):
        ctr = dict(ctr0)
        for s in self.steps:
            for w in s.waits:
                if isinstance(w, Step):
                    assert w.val is not None, f"{self.name}: wait on unresolved step"
                    assert w.inc_n > 0, f"{self.name}: wait on inc-less step"
            ctr[s.sem] = ctr.get(s.sem, 0) + s.inc_n
            s.val = ctr[s.sem]
        return ctr

    def emit_engine(self, engname, eng, sems):
        for s in self.steps:
            if s.eng != engname:
                continue
            needed = {}
            for w in s.waits:
                if isinstance(w, Step):
                    key, val = w.sem, w.val
                else:
                    key, val = w
                if needed.get(key, -1) < val:
                    needed[key] = val
            for key, val in sorted(needed.items()):
                eng.wait_ge(sems[key], val)
            instr = s.emit(eng)
            if s.inc_n:
                assert instr is not None, f"{self.name}: step returned no instr"
                instr.then_inc(sems[s.sem], s.inc_n)


def build_nc(debug=False):
    nc = bass.Bass("TRN2", target_bir_lowering=False, debug=False)

    xp_h = nc.dram_tensor("xp", [NSEQ, E], F32, kind="ExternalInput").ap()
    xo_h = nc.dram_tensor("xo", [OWN, E], F32, kind="ExternalInput").ap()
    wq_h = nc.dram_tensor("w_q", [E, E], F32, kind="ExternalInput").ap()
    wkv_h = nc.dram_tensor("w_kv", [E, 2 * D], F32, kind="ExternalInput").ap()
    bq_h = nc.dram_tensor("b_q", [D, H], F32, kind="ExternalInput").ap()
    bk_h = nc.dram_tensor("b_k", [D, 1], F32, kind="ExternalInput").ap()
    bv_h = nc.dram_tensor("b_v", [D, 1], F32, kind="ExternalInput").ap()
    wfc_h = nc.dram_tensor("w_fc", [E, E], F32, kind="ExternalInput").ap()
    bfc_h = nc.dram_tensor("b_fc", [1, E], F32, kind="ExternalInput").ap()
    wT_h = nc.dram_tensor("w_ffnT", [E, E], BF16, kind="ExternalInput").ap()
    id_h = nc.dram_tensor("ident", [128, 128], F32, kind="ExternalInput").ap()
    idb_h = nc.dram_tensor("identb", [128, 128], BF16, kind="ExternalInput").ap()
    id8_h = nc.dram_tensor("ident8", [128, 128], F8, kind="ExternalInput").ap()
    onesR_h = nc.dram_tensor("onesR", [1, 128], BF16, kind="ExternalInput").ap()
    onesC_h = nc.dram_tensor("onesC", [128, 1], BF16, kind="ExternalInput").ap()
    onesRf_h = nc.dram_tensor("onesRf", [1, 128], F32, kind="ExternalInput").ap()
    mask_h = nc.dram_tensor("masks", [128, 4 * 512], BF16, kind="ExternalInput").ap()
    y_h = nc.dram_tensor("y", [OWN, E], F32, kind="ExternalOutput").ap()
    cc_in = nc.dram_tensor("cc_in", [1, 1], F32, kind="Internal").ap()
    rbd_h = nc.dram_tensor("rb_scratch", [4, 512], F32, kind="Internal").ap()
    scb_h = nc.dram_tensor("sc_scratch", [1, 2], F32, kind="Internal").ap()
    cc_out = nc.dram_tensor("cc_out", [1, 1], F32, kind="Internal").ap()

    dbg = {}
    if debug:
        for nm, shp, dt in (("KT", [128, NSEQ], BF16), ("V", [128, NSEQ], BF16),
                            ("qT", [128, H * 1024], BF16),
                            ("attnT", [128, H * 1024], BF16),
                            ("x1", [128, 8 * E], F32),
                            ("stat", [1, 32], F32),
                            ("lnqT", [128, 16 * 1024], BF16)):
            dbg[nm] = nc.dram_tensor("dbg_" + nm, shp, dt,
                                     kind="ExternalOutput").ap()

    with ExitStack() as outer:
        ent = outer.enter_context
        idT = ent(nc.sbuf_tensor("idT", [128, 128], F32))
        idB = ent(nc.sbuf_tensor("idB", [128, 128], BF16))
        id8 = ent(nc.sbuf_tensor("id8", [128, 128], F8))
        onesR = ent(nc.sbuf_tensor("onesRs", [1, 128], BF16))
        onesC = ent(nc.sbuf_tensor("onesCs", [128, 1], BF16))
        onesRf = ent(nc.sbuf_tensor("onesRfs", [1, 128], F32))
        bq = ent(nc.sbuf_tensor("bqs", [128, H], F32))
        bk = ent(nc.sbuf_tensor("bks", [128, 1], F32))
        bvb = ent(nc.sbuf_tensor("bvbs", [128, 1], F32))
        bfcb = ent(nc.sbuf_tensor("bfcbs", [1, E], BF16))
        kmax8 = ent(nc.sbuf_tensor("kmax8", [1, 8], F32))
        pmax = ent(nc.sbuf_tensor("pmax", [1, 8], F32))
        mq2 = ent(nc.sbuf_tensor("mq2", [1, 2], F32))
        s11 = ent(nc.sbuf_tensor("s11", [1, 32], F32))
        bB = ent(nc.sbuf_tensor("bB", [128, 2], F32))
        colv = ent(nc.sbuf_tensor("colv", [128, 4], F32))
        amw = ent(nc.sbuf_tensor("amw", [128, 1], F32))
        amx = ent(nc.sbuf_tensor("amx", [128, 1], F32))
        mus = ent(nc.sbuf_tensor("mus", [128, 16], F32))
        stv = ent(nc.sbuf_tensor("stv", [128, 200], F32))
        sKVQ = ent(ExitStack())
        KT = sKVQ.enter_context(nc.sbuf_tensor("KT", [128, NSEQ], BF16))
        V = sKVQ.enter_context(nc.sbuf_tensor("V", [128, NSEQ], BF16))
        qT = sKVQ.enter_context(nc.sbuf_tensor("qT", [128, H * 1024], BF16))

        DMA_SEMS = ["dconst", "dxr0", "dxr1", "dxr2", "dxr3", "dxr4", "dxr5", "dxr6", "dxr7", "dwkv", "dwq",
                    "dbvb", "dbfcb", "dmask", "dwfc0", "dwfc1",
                    "dxo0", "dxo1", "dxo2", "dxo3", "dwt0", "dwt1", "dwt2", "dwt3",
                    "dwj0", "dwj1", "dos0", "dos1", "dos2", "dos3",
                    "dcci", "dcco", "ddbg", "dbr0", "dbr1", "dbo0", "dbo1",
                    "dwq0", "dwq1", "dwq2", "dev0", "dev1"]
        sems = {n: ent(nc.semaphore("sem_" + n)) for n in list(ENGS) + DMA_SEMS}
        ctr = {n: 0 for n in sems}

        def run_phase(ph, switch=None):
            nonlocal ctr
            end_ctr = ph.resolve(ctr)
            arm_end = None
            if switch is not None:
                for c in range(4):
                    e = switch[c].resolve(end_ctr)
                    if arm_end is None:
                        arm_end = e
                    assert e == arm_end, f"arm {c}: {e} != {arm_end}"
            with nc.Block(ph.name, no_gpsimd_drain=True) as blk:
                def mk(engname):
                    def emitter(eng):
                        ph.emit_engine(engname, eng, sems)
                        if switch is not None and engname in ("tensor", "scalar", "vector", "gpsimd", "sync"):
                            rc = eng.partition_id() % 4
                            for c in nc.Switch(engines=[eng], index=[rc], n=4):
                                switch[c].emit_engine(engname, eng, sems)
                    return emitter
                blk.sync(mk("sync"))
                blk.gpsimd(mk("gpsimd"))
                blk.vector(mk("vector"))
                blk.scalar(mk("scalar"))
                blk.tensor(mk("tensor"))
            ctr = arm_end if switch is not None else end_ctr

        def dma(dst, src):
            return lambda eng: eng.dma_start(out=dst, in_=src)

        with ExitStack() as sA:
            wkv = sA.enter_context(nc.sbuf_tensor("wkv", [128, 16 * 256], BF16, side="right"))
            wqs = sA.enter_context(nc.sbuf_tensor("wqs", [128, 3 * E], BF16, side="right"))
            xch = sA.enter_context(nc.sbuf_tensor("xch", [128, 8 * E], F32, side="right"))
            lnB = sA.enter_context(nc.sbuf_tensor("lnB", [128, 4 * E], BF16, side="right"))
            sqJ = sA.enter_context(nc.sbuf_tensor("sqJ", [128, E], BF16, side="right"))
            lnT = sA.enter_context(nc.sbuf_tensor("lnT", [128, 2 * 16 * 512], BF16, side="right"))
            k2s = sA.enter_context(nc.sbuf_tensor("k2s", [128, 1024], BF16, side="right"))
            VTs = sA.enter_context(nc.sbuf_tensor("VTs", [128, 512], BF16, side="right"))

            p0 = Phase("p0")
            ld_const = []
            for dst, srcap in ((idT[:], id_h[:, :]), (idB[:], idb_h[:, :]),
                               (id8[:], id8_h[:, :]),
                               (onesR[:], onesR_h[:, :]),
                               (onesC[:], onesC_h[:, :]), (onesRf[:], onesRf_h[:, :]),
                               (bq[:], bq_h[:, :]), (bk[:], bk_h[:, :])):
                ld_const.append(p0.add("sync", dma(dst, srcap), inc=16,
                                        sem="dconst"))
            ld_bvb = p0.add("gpsimd", dma(bvb[:], bv_h[:, :]), inc=16,
                            sem="dbvb")
            ld_bfcb = p0.add("gpsimd", dma(bfcb[:], bfc_h[:, :]), inc=16,
                             sem="dbfcb")
            ld_wkv = p0.add("gpsimd", dma(
                wkv[:].rearrange("p (fc j) -> p fc j", fc=16),
                wkv_h.rearrange("(fc p) j -> p fc j", p=128)), inc=16,
                sem="dwkv")
            wq_rj = wq_h.rearrange("(fc p) j -> p fc j", p=128)
            run_phase(p0)

            with nc.psum_tensor("psT0", [128, 1024], BF16) as psT0, \
                 nc.psum_tensor("psT1", [128, 1024], BF16) as psT1, \
                 nc.psum_tensor("psKV0", [128, 512], F32) as psKV0, \
                 nc.psum_tensor("psKV1", [128, 512], F32) as psKV1, \
                 nc.psum_tensor("psK0", [128, 512], F32) as psK0, \
                 nc.psum_tensor("psK1", [128, 512], F32) as psK1, \
                 nc.psum_tensor("psVT", [128, 512], BF16) as psVT, \
                 nc.psum_tensor("psk2", [1, 512], F32) as psk2:
                p1 = Phase("p1")
                st1 = {"psT": (psT0, psT1), "psKVs": (psKV0, psKV1),
                       "psKs": (psK0, psK1), "psVT": psVT, "psk2": psk2,
                       "kev_par": {0: None, 1: None},
                       "vte_par": {0: None, 1: None}}
                wqld = {}
                wq_slot_rd = {}

                def emit_wq_ld(ph, jc):
                    slot = jc % 3
                    wl = [wq_slot_rd[slot]] if wq_slot_rd.get(slot) else []
                    ld = ph.add("gpsimd", dma(
                        wqs[:, slot * E:(slot + 1) * E]
                        .rearrange("p (fc j) -> p fc j", fc=16),
                        wq_rj[:, :, jc * 128:(jc + 1) * 128]),
                        waits=wl, inc=16, sem="dwq%d" % slot)
                    wqld[jc] = ld
                    return ld

                def emit_tailA(ph, info):
                    state = st1
                    par = info["pos"] % 2
                    ch = info["ch"]
                    kev = ph.add("scalar", lambda s, ch=ch, info=info:
                                 s.activation(
                                     KT[:, ch * 512:(ch + 1) * 512],
                                     info["bankK"][:], AF.Identity,
                                     bias=bk[:, 0:1]),
                                 waits=[info["kmm"], ld_const[-1]])
                    state["kev_par"][par] = kev
                    info["kev"] = kev
                    vte_w = [info["vtm"], ld_bvb]
                    if state.get("vtr") is not None:
                        vte_w.append(state["vtr"])
                    vte = ph.add("scalar", lambda s, info=info: s.activation(
                        VTs[:, :], info["bankV"][:], AF.Identity,
                        bias=bvb[:, 0:1]), waits=vte_w)
                    state["vte_par"][par] = vte
                    info["vte"] = vte

                def emit_tailB(ph, info):
                    state = st1
                    ch = info["ch"]
                    vtr_w = [info["vte"], ld_const[-1]]
                    if state.get("vev") is not None:
                        vtr_w.append(state["vev"])
                    def vtr(tn):
                        last = None
                        for q in range(4):
                            last = tn.transpose(
                                state["psVT"][:, q * 128:(q + 1) * 128],
                                VTs[:, q * 128:(q + 1) * 128], idB[:])
                        return last
                    state["vtr"] = ph.add("tensor", vtr, waits=vtr_w)
                    state["vev"] = ph.add("vector", lambda v, ch=ch: v.tensor_copy(
                        V[:, ch * 512:(ch + 1) * 512], state["psVT"][:]),
                        waits=[state["vtr"]])
                    ksqw = [info["kev"]]
                    if state.get("k2red") is not None:
                        ksqw.append(state["k2red"])
                    ksq = ph.add("scalar", lambda s, ch=ch: s.activation(
                        k2s[:, (ch % 2) * 512:(ch % 2 + 1) * 512],
                        KT[:, ch * 512:(ch + 1) * 512], AF.Square), waits=ksqw)
                    k2w = [ksq, ld_const[-1]]
                    if state.get("k2red") is not None:
                        k2w.append(state["k2red"])
                    k2m = ph.add("tensor", lambda tn, ch=ch: tn.matmul(
                        state["psk2"][0:1, 0:512], onesC[:],
                        k2s[:, (ch % 2) * 512:(ch % 2 + 1) * 512],
                        start=True, stop=True), waits=k2w)
                    state["k2red"] = ph.add("vector", lambda v, ch=ch: v.reduce_max(
                        kmax8[0:1, ch:ch + 1], state["psk2"][0:1, 0:512], AX.X),
                        waits=[k2m])

                def emit_p1():
                    state = st1
                    info = {}
                    for pos in range(NCH):
                        state["lnT_consumers"] = (
                            [state["kmm"], state["vtm"]] if pos > 0 else [])
                        emit_one_chunk(p1, xp_h, pos, state, do_kv=True)
                        info[pos] = {"pos": pos, "ch": state["ch"],
                                     "kmm": state["kmm"], "vtm": state["vtm"],
                                     "bankK": state["psKs"][pos % 2],
                                     "bankV": state["psKVs"][pos % 2]}
                        if pos >= 2:
                            emit_tailB(p1, info[pos - 2])
                        if pos >= 1:
                            emit_tailA(p1, info[pos - 1])
                        if pos == 4:
                            for jc in range(3):
                                emit_wq_ld(p1, jc)
                        if pos == 6:
                            st1["sv_mu"] = p1.add(
                                "vector", lambda v: v.tensor_copy(
                                    stv[:, 140:144], stv[:, 8:12]),
                                waits=[state["c11"]])
                            st1["sv_r"] = p1.add(
                                "vector", lambda v: v.tensor_copy(
                                    stv[:, 148:152], stv[:, 12:16]),
                                waits=[state["c11"]])
                            st1["ldc_6"] = list(state["ldc_chunk"])
                    emit_tailB(p1, info[6])
                    emit_tailA(p1, info[7])
                    emit_tailB(p1, info[7])
                    st1["lnT_evs_7"] = list(state["lnT_evs"])

                def emit_one_chunk(ph, src_h, pos, state, do_kv):
                    tr_last = state.setdefault("tr_last", {0: None, 1: None,
                                                           2: None, 3: None})
                    bank_ev = state.setdefault("bank_ev", [None, None])
                    sq_last = state.setdefault("sq_last", [None, None])
                    ring_cons = state.setdefault("ring_cons", [None] * 8)
                    slotdone = state.setdefault("slotdone", [[], []])
                    ch = idx = pos
                    par = idx % 2
                    base = par * 32
                    sums4 = stv[:, base + 0:base + 4]
                    sumsq4 = stv[:, base + 4:base + 8]
                    mu4 = stv[:, base + 8:base + 12]
                    r4 = stv[:, base + 12:base + 16]
                    t4 = stv[:, base + 16:base + 20]
                    u4 = stv[:, base + 20:base + 24]
                    r04 = stv[:, base + 24:base + 28]
                    tmp4 = stv[:, base + 28:base + 32]
                    war = list(slotdone[par])
                    stats_steps = []
                    lnT_evs_chunk = []
                    vmm_chunk = []
                    xins = []
                    ldcs = []
                    for ti in range(4):
                        rs = (idx * 4 + ti) % 8
                        row0 = idx * 512 + ti * 128
                        wl = [ring_cons[rs]] if ring_cons[rs] else []
                        halves = []
                        for hh in range(2):
                            halves.append(ph.add("sync", dma(
                                xch[:, rs * E + hh * 1024:
                                    rs * E + (hh + 1) * 1024],
                                src_h[row0:row0 + 128,
                                      hh * 1024:(hh + 1) * 1024]),
                                waits=wl, inc=16,
                                sem="dxr%d" % ((idx * 4 + ti + 4 * hh) % 8)))
                        ldcs.append(halves)
                        xin = xch[:, rs * E:(rs + 1) * E]
                        xins.append(xin)
                        st_sq = ph.add("scalar", lambda s, xin=xin, ti=ti:
                                       s.activation(
                                           sqJ[:, :], xin, AF.Square,
                                           accum_out=stv[:, base + 4 + ti:
                                                         base + 5 + ti]),
                                       waits=halves + war +
                                             ([sq_last[0]] if sq_last[0] else []))
                        sq_last[0] = st_sq
                        vs = ph.add("vector", lambda v, xin=xin, ti=ti:
                                    v.tensor_reduce(
                                        stv[:, base + ti:base + 1 + ti],
                                        xin, AX.X, ALU.add),
                                    waits=halves + war)
                        stats_steps.append((st_sq, vs))
                    state["ldc_chunk"] = ldcs
                    c1 = ph.add("vector", lambda v: v.tensor_scalar(
                        mu4, sums4, 1.0 / 2048.0, None, ALU.mult),
                        waits=[s[1] for s in stats_steps] + war)
                    c2 = ph.add("vector", lambda v: v.tensor_scalar(
                        t4, sumsq4, 1.0 / 2048.0, None, ALU.mult),
                        waits=[s[0] for s in stats_steps])
                    c3 = ph.add("vector", lambda v: v.tensor_tensor(
                        tmp4, mu4, mu4, ALU.mult), waits=[c1])
                    c4 = ph.add("vector", lambda v: v.tensor_tensor(
                        t4, t4, tmp4, ALU.subtract), waits=[c2, c3])
                    c5 = ph.add("vector", lambda v: v.tensor_scalar(
                        t4, t4, EPS, None, ALU.add), waits=[c4])
                    c6 = ph.add("vector", lambda v: v.reciprocal(u4, t4),
                                waits=[c5])
                    c7 = ph.add("scalar", lambda s: s.activation(
                        r04, u4, AF.Sqrt), waits=[c6])
                    c8 = ph.add("vector", lambda v: v.tensor_tensor(
                        tmp4, r04, r04, ALU.mult), waits=[c7])
                    c9 = ph.add("vector", lambda v: v.tensor_tensor(
                        tmp4, tmp4, t4, ALU.mult), waits=[c8])
                    c10 = ph.add("vector", lambda v: v.tensor_scalar(
                        tmp4, tmp4, -0.5, 1.5, ALU.mult, ALU.add), waits=[c9])
                    c11 = ph.add("vector", lambda v: v.tensor_tensor(
                        r4, r04, tmp4, ALU.mult), waits=[c10])
                    state["c11"] = c11
                    nm1 = ph.add("vector", lambda v: v.tensor_tensor(
                        tmp4, mu4, r4, ALU.mult), waits=[c11])
                    nm2 = ph.add("vector", lambda v: v.tensor_scalar(
                        tmp4, tmp4, -1.0, None, ALU.mult), waits=[nm1])
                    applies = []
                    for ti in range(4):
                        rs = (idx * 4 + ti) % 8
                        xin = xins[ti]
                        apw = [nm2, stats_steps[ti][0]] + ldcs[ti]
                        if tr_last[ti] is not None:
                            apw.append(tr_last[ti])
                        ap = ph.add("scalar", lambda s, xin=xin, ti=ti:
                                    s.activation(
                                        lnB[:, ti * E:(ti + 1) * E], xin,
                                        AF.Identity,
                                        bias=stv[:, base + 28 + ti:
                                                 base + 29 + ti],
                                        scale=stv[:, base + 12 + ti:
                                                  base + 13 + ti]), waits=apw)
                        ring_cons[rs] = ap
                        applies.append(ap)
                        for ss_ in range(2):
                            bank = state["psT"][ss_]
                            deps = [ap, ld_const[-1]]
                            if bank_ev[ss_] is not None:
                                deps.append(bank_ev[ss_])
                            def tre(tn, ti=ti, ss_=ss_, bank=bank):
                                last = None
                                for q in range(8):
                                    fc = ss_ * 8 + q
                                    last = tn.transpose(
                                        bank[:, q * 128:(q + 1) * 128],
                                        lnB[:, ti * E + fc * 128:
                                            ti * E + (fc + 1) * 128], idB[:])
                                return last
                            st = ph.add("tensor", tre, waits=deps)
                            evw = [st]
                            if ti == 0 and ss_ == 0:
                                evw += state.get("lnT_consumers", [])
                            lnbase = state.get("lnT_base", 0)
                            def eva(v, ss_=ss_, ti=ti, bank=bank, lnbase=lnbase):
                                dstv = lnT[:, lnbase:lnbase + 16 * 512].rearrange(
                                    "p (fc t) -> p fc t", fc=16)
                                return v.tensor_copy(
                                    dstv[:, ss_ * 8:(ss_ + 1) * 8,
                                         ti * 128:(ti + 1) * 128],
                                    bank[:].rearrange("p (q t) -> p q t", q=8))
                            e = ph.add("vector", eva, waits=evw)
                            bank_ev[ss_] = e
                            lnT_evs_chunk.append(e)
                            tr_last[ti] = st
                    slotdone[par] = list(applies)
                    state["lnT_evs"] = lnT_evs_chunk
                    state["ch"] = ch
                    if do_kv:
                        bankK = state["psKs"][par]
                        bankV = state["psKVs"][par]
                        kdeps = list(lnT_evs_chunk) + [ld_wkv]
                        if state["kev_par"][par] is not None:
                            kdeps.append(state["kev_par"][par])
                        def kmm(tn, bankK=bankK):
                            last = None
                            for fc in range(16):
                                last = tn.matmul(
                                    bankK[:],
                                    wkv[:, fc * 256: fc * 256 + 128],
                                    lnT[:, fc * 512:(fc + 1) * 512],
                                    start=(fc == 0), stop=(fc == 15))
                            return last
                        state["kmm"] = ph.add("tensor", kmm, waits=kdeps)
                        vdeps = list(lnT_evs_chunk) + [ld_wkv]
                        if state["vte_par"][par] is not None:
                            vdeps.append(state["vte_par"][par])
                        def vtm(tn, bankV=bankV):
                            last = None
                            for fc in range(16):
                                last = tn.matmul(
                                    bankV[:],
                                    wkv[:, fc * 256 + 128: fc * 256 + 256],
                                    lnT[:, fc * 512:(fc + 1) * 512],
                                    start=(fc == 0), stop=(fc == 15))
                            return last
                        state["vtm"] = ph.add("tensor", vtm, waits=vdeps)

                emit_p1()
                run_phase(p1)

                p2 = Phase("p2")
                tr_last = st1["tr_last"]
                bank_ev = st1["bank_ev"]
                och1_evs = []
                qev = {0: None, 1: None}
                qps = (psKV0, psKV1)
                qwar0 = {0: [st1["vte_par"][0]], 1: [st1["vte_par"][1]]}
                lnT_ready = {0: list(st1["lnT_evs_7"]), 1: och1_evs}
                qm_by_jc_och = {}
                qev_h = {}

                def emit_q(jc, och):
                    deps = list(lnT_ready[och]) + [wqld[jc]]
                    if qev[och] is not None:
                        deps.append(qev[och])
                    else:
                        deps += qwar0[och]
                    def qmm(tn, jc=jc, och=och):
                        last = None
                        base_l = och * 8192
                        for fc in range(16):
                            last = tn.matmul(
                                qps[och][:],
                                wqs[:, (jc % 3) * E + fc * 128:
                                    (jc % 3) * E + (fc + 1) * 128],
                                lnT[:, base_l + fc * 512:
                                    base_l + (fc + 1) * 512],
                                start=(fc == 0), stop=(fc == 15))
                        return last
                    qm = p2.add("tensor", qmm, waits=deps)
                    qm_by_jc_och[(jc, och)] = qm
                    qev[och] = p2.add(
                        "scalar", lambda s, jc=jc, och=och: s.activation(
                            qT[:, jc * 1024 + och * 512:
                               jc * 1024 + (och + 1) * 512],
                            qps[och][:], AF.Identity, bias=bq[:, jc:jc + 1]),
                        waits=[qm, ld_const[-1]])
                    qev_h.setdefault(jc, {})[och] = qev[och]
                    return qm

                def post_jc(jc):
                    wq_slot_rd[jc % 3] = qm_by_jc_och[(jc, 1)]
                    if jc + 3 < H:
                        emit_wq_ld(p2, jc + 3)

                emit_q(0, 0)
                emit_q(1, 0)
                for ti in range(4):
                    apw = [st1["sv_mu"], st1["sv_r"]] + st1["ldc_6"][ti]
                    if tr_last[ti] is not None:
                        apw.append(tr_last[ti])
                    ap = p2.add("vector", lambda v, ti=ti: v.tensor_scalar(
                        lnB[:, ti * E:(ti + 1) * E],
                        xch[:, ti * E:(ti + 1) * E],
                        stv[:, 140 + ti:141 + ti],
                        stv[:, 148 + ti:149 + ti],
                        ALU.subtract, ALU.mult), waits=apw)
                    for ss_ in range(2):
                        bank = (psT0, psT1)[ss_]
                        deps = [ap, ld_const[-1]]
                        if bank_ev[ss_] is not None:
                            deps.append(bank_ev[ss_])
                        def tre2(tn, ti=ti, ss_=ss_, bank=bank):
                            last = None
                            for q in range(8):
                                fc = ss_ * 8 + q
                                last = tn.transpose(
                                    bank[:, q * 128:(q + 1) * 128],
                                    lnB[:, ti * E + fc * 128:
                                        ti * E + (fc + 1) * 128], idB[:])
                            return last
                        st = p2.add("tensor", tre2, waits=deps)
                        def eva2(v, ss_=ss_, ti=ti, bank=bank):
                            dstv = lnT[:, 8192:16384].rearrange(
                                "p (fc t) -> p fc t", fc=16)
                            return v.tensor_copy(
                                dstv[:, ss_ * 8:(ss_ + 1) * 8,
                                     ti * 128:(ti + 1) * 128],
                                bank[:].rearrange("p (q t) -> p q t", q=8))
                        e = p2.add("vector", eva2, waits=[st])
                        bank_ev[ss_] = e
                        och1_evs.append(e)
                        tr_last[ti] = st
                emit_q(0, 1)
                post_jc(0)
                emit_q(1, 1)
                post_jc(1)
                for jc in range(2, H):
                    emit_q(jc, 0)
                    emit_q(jc, 1)
                    post_jc(jc)
                q2mm_prev = None
                red_prev = None
                acc_prev = None
                for h in range(H):
                    sqw = [qev_h[h][0], qev_h[h][1]]
                    if q2mm_prev is not None:
                        sqw.append(q2mm_prev)
                    q2sq = p2.add("scalar", lambda s, h=h: s.activation(
                        k2s[:, 0:1024], qT[:, h * 1024:(h + 1) * 1024], AF.Square),
                        waits=sqw)
                    for g in range(2):
                        w = [q2sq, ld_const[-1]]
                        if red_prev is not None:
                            w.append(red_prev)
                        else:
                            w.append(st1["k2red"])
                        q2mm = p2.add("tensor", lambda tn, g=g: tn.matmul(
                            psk2[:], onesC[:], k2s[:, g * 512:(g + 1) * 512],
                            start=True, stop=True), waits=w)
                        q2mm_prev = q2mm
                        rw = [q2mm]
                        if acc_prev is not None:
                            rw.append(acc_prev)
                        red = p2.add("vector", lambda v: v.reduce_max(
                            s11[0:1, 0:1], psk2[0:1, :], AX.X), waits=rw)
                        if h == 0:
                            acc = p2.add("vector", lambda v, g=g: v.tensor_copy(
                                mq2[0:1, g:g + 1], s11[0:1, 0:1]), waits=[red])
                        else:
                            acc = p2.add("vector", lambda v, g=g: v.tensor_tensor(
                                mq2[0:1, g:g + 1], mq2[0:1, g:g + 1],
                                s11[0:1, 0:1], ALU.max), waits=[red])
                        red_prev = red
                        acc_prev = acc
                if debug:
                    d1 = p2.add("sync", dma(dbg["KT"][:, :], KT[:]), inc=16, sem="ddbg")
                    d2 = p2.add("sync", dma(dbg["V"][:, :], V[:]), inc=16, sem="ddbg")
                    d3 = p2.add("sync", dma(dbg["qT"][:, :], qT[:]), inc=16, sem="ddbg")
                    p2.add("sync", lambda s: None, waits=[d3], inc=0)
                run_phase(p2)

        with ExitStack() as sB:
            sB1 = sB.enter_context(ExitStack())
            attnT = sB1.enter_context(nc.sbuf_tensor("attnT", [128, H * 1024], BF16, side="right"))
            wfcj = sB1.enter_context(nc.sbuf_tensor("wfcj", [128, 2 * 16 * 512], BF16, side="right"))
            m3 = sB1.enter_context(ExitStack())
            pT = m3.enter_context(nc.sbuf_tensor("pT", [128, 4 * 1024], BF16, side="right"))
            pacc = m3.enter_context(nc.sbuf_tensor("pacc", [128, 2 * 1024], BF16, side="right"))
            masks = m3.enter_context(nc.sbuf_tensor("masks_s", [128, 4 * 512], BF16, side="right"))
            rcs = m3.enter_context(nc.sbuf_tensor("rcs", [1, 4 * 512], F32, side="right"))
            rcsum = m3.enter_context(nc.sbuf_tensor("rcsum", [1, 4 * 512], F32, side="right"))
            aTu = m3.enter_context(nc.sbuf_tensor("aTu", [128, 4 * 512], BF16, side="right"))
            rbs = m3.enter_context(nc.sbuf_tensor("rbs", [128, 4 * 512], F32, side="right"))

            wfc_r = wfc_h.rearrange("(fc p) j -> p fc j", p=128)
            wfcj_r = wfcj[:].rearrange("p (b fc j) -> p b fc j", b=2, fc=16)

            with nc.psum_tensor("psS0", [128, 1024], F32) as psS0, \
                 nc.psum_tensor("psS1", [128, 1024], F32) as psS1, \
                 nc.psum_tensor("psA0", [128, 512], F32) as psA0, \
                 nc.psum_tensor("psA1", [128, 512], F32) as psA1, \
                 nc.psum_tensor("psR0", [128, 512], F32) as psR0, \
                 nc.psum_tensor("psR1", [128, 512], F32) as psR1:
                psS = (psS0, psS1)
                psA = (psA0, psA1)
                psR = (psR0, psR1)

                p3 = Phase("p3")
                ld_mask = p3.add("sync", dma(masks[:], mask_h[:, :]), inc=16,
                                 sem="dmask")
                ld_wfc01 = []
                for j in range(2):
                    ld_wfc01.append(p3.add("gpsimd", dma(
                        wfcj_r[:, j], wfc_r[:, :, j * 512:(j + 1) * 512]), inc=16,
                        sem="dwfc%d" % j))

                arms = []
                for c in range(4):
                    arm = Phase(f"p3arm{c}")
                    own = (c, 7 - c)
                    perm = [p for p in range(8) if p not in own] + [7 - c, c]
                    inv = {p: s for s, p in enumerate(perm)}
                    extents = (4 * (c + 1), 4 * (8 - c))
                    kc_lists = []
                    for g in range(2):
                        ephys = c if g == 0 else 7 - c
                        slots_nd = sorted(inv[p] for p in range(ephys))
                        kl = [s * 4 + q for s in slots_nd for q in range(4)]
                        kl += [inv[ephys] * 4 + q for q in range(4)]
                        kc_lists.append(kl)
                    bset = []
                    for g in range(2):
                        ephys = c if g == 0 else 7 - c
                        slots = sorted(inv[p] for p in range(ephys + 1))
                        m0 = arm.add("vector", lambda v, s0=slots[0], g=g:
                                     v.tensor_copy(s11[0:1, 14 + g:15 + g],
                                                   kmax8[0:1, s0:s0 + 1]))
                        for s in slots[1:]:
                            m0 = arm.add("vector", lambda v, s=s, g=g:
                                         v.tensor_tensor(
                                             s11[0:1, 14 + g:15 + g],
                                             s11[0:1, 14 + g:15 + g],
                                             kmax8[0:1, s:s + 1], ALU.max),
                                         waits=[m0])
                        b1 = arm.add("vector", lambda v, g=g:
                                     v.tensor_tensor(
                                         s11[0:1, 8 + g:9 + g], mq2[0:1, g:g + 1],
                                         s11[0:1, 14 + g:15 + g], ALU.mult),
                                     waits=[m0])
                        b2 = arm.add("scalar", lambda s, g=g: s.activation(
                            s11[0:1, 10 + g:11 + g], s11[0:1, 8 + g:9 + g],
                            AF.Sqrt, scale=1.0 / 128.0), waits=[b1])
                        b3 = arm.add("vector", lambda v, g=g: v.tensor_scalar(
                            s11[0:1, 12 + g:13 + g], s11[0:1, 10 + g:11 + g],
                            -1.01, -0.25, ALU.mult, ALU.add), waits=[b2])
                        b4 = arm.add("tensor", lambda tn, g=g: tn.matmul(
                            psR0[:, g:g + 1], onesRf[:],
                            s11[0:1, 12 + g:13 + g], start=True, stop=True),
                            waits=[b3, ld_const[-1]])
                        b5 = arm.add("vector", lambda v, g=g: v.tensor_copy(
                            bB[:, g:g + 1], psR0[:, g:g + 1]), waits=[b4])
                        bset.append(b5)

                    prev_eager = None
                    pending_lazy = None
                    lazy_hist = [None, None]
                    acc_war = [None, None]
                    last_accs = None
                    bseq = 0
                    for hp in range(8):
                        h0, h1 = 2 * hp, 2 * hp + 1
                        for g in range(2):
                            Eg = extents[g]
                            kcl = kc_lists[g]
                            exp_steps = [None] * Eg
                            post_steps = [None] * Eg
                            cons_steps = [None] * Eg
                            acc_steps = [None] * Eg
                            s2_steps = [None] * Eg

                            def mk_s2(kc, it, m=None, h0=h0, h1=h1, g=g):
                                def f(tn):
                                    b = psS[it % 2]
                                    tn.matmul(
                                        b[:, 0:512],
                                        KT[:, kc * 128:(kc + 1) * 128],
                                        qT[:, h0 * 1024 + g * 512: h0 * 1024 + (g + 1) * 512],
                                        start=True, stop=(m is None))
                                    if m is not None:
                                        tn.matmul(
                                            b[:, 0:512], idB[:],
                                            masks[:, m * 512:(m + 1) * 512],
                                            start=False, stop=True)
                                    last = tn.matmul(
                                        b[:, 512:1024],
                                        KT[:, kc * 128:(kc + 1) * 128],
                                        qT[:, h1 * 1024 + g * 512: h1 * 1024 + (g + 1) * 512],
                                        start=True, stop=(m is None))
                                    if m is not None:
                                        last = tn.matmul(
                                            b[:, 512:1024], idB[:],
                                            masks[:, m * 512:(m + 1) * 512],
                                            start=False, stop=True)
                                    return last
                                return f

                            def mk_cons(kc, it, Eg=Eg):
                                def f(tn):
                                    first = (it == 0)
                                    last = (it == Eg - 1)
                                    buf = it % 4
                                    tn.matmul(psA[0][:],
                                              V[:, kc * 128:(kc + 1) * 128],
                                              pT[:, buf * 1024: buf * 1024 + 512],
                                              start=first, stop=last)
                                    return tn.matmul(
                                        psA[1][:], V[:, kc * 128:(kc + 1) * 128],
                                        pT[:, buf * 1024 + 512: buf * 1024 + 1024],
                                        start=first, stop=last)
                                return f

                            def mk_acc(it, bpar):
                                def f(v):
                                    dst = pacc[:, bpar * 1024:
                                               bpar * 1024 + 1024]
                                    src = pT[:, (it % 4) * 1024:
                                             (it % 4) * 1024 + 1024]
                                    if it == 0:
                                        return v.tensor_copy(dst, src)
                                    return v.tensor_tensor(dst, dst, src,
                                                           ALU.add)
                                return f

                            def mk_rowsum(bpar):
                                def f(tn):
                                    tn.matmul(psR[0][0:1, :], onesC[:],
                                              pacc[:, bpar * 1024:
                                                   bpar * 1024 + 512],
                                              start=True, stop=True)
                                    return tn.matmul(
                                        psR[1][0:1, :], onesC[:],
                                        pacc[:, bpar * 1024 + 512:
                                             bpar * 1024 + 1024],
                                        start=True, stop=True)
                                return f

                            def mk_exp(it, g=g):
                                def f(s):
                                    return s.activation(
                                        pT[:, (it % 4) * 1024:(it % 4) * 1024 + 1024],
                                        psS[it % 2][:, 0:1024], AF.Exp,
                                        bias=bB[:, g:g + 1], scale=INVSQ)
                                return f

                            def mk_mask(it, m, half):
                                def f(v):
                                    base = (it % 4) * 1024 + half * 512
                                    return v.tensor_tensor(
                                        pT[:, base:base + 512],
                                        pT[:, base:base + 512],
                                        masks[:, m * 512:(m + 1) * 512], ALU.mult)
                                return f

                            def emit_eager(rs_last, av_last, lazy_prev2, par):
                                ev = []
                                for hh in (0, 1):
                                    w = [rs_last]
                                    if lazy_prev2 is not None:
                                        w.append(lazy_prev2["dm1"][hh])
                                    ev.append(arm.add(
                                        "vector", lambda v, hh=hh, par=par:
                                        v.tensor_copy(
                                            rcsum[0:1, par * 1024 + hh * 512:
                                                  par * 1024 + (hh + 1) * 512],
                                            psR[hh][0:1, :]), waits=w))
                                for hh in (0, 1):
                                    w = [av_last]
                                    if lazy_prev2 is not None:
                                        w.append(lazy_prev2["mul"][hh])
                                    ev.append(arm.add(
                                        "scalar", lambda s, hh=hh, par=par:
                                        s.activation(
                                            aTu[:, par * 1024 + hh * 512:
                                                par * 1024 + (hh + 1) * 512],
                                            psA[hh][:], AF.Identity), waits=w))
                                return ev

                            def lazy_stage1(bspec, out):
                                h0b, h1b, gb, ev, lazy_prev2, par = bspec
                                for hh in (0, 1):
                                    o = par * 1024 + hh * 512
                                    rs_ = rcsum[0:1, o:o + 512]
                                    di = par * 2 + hh
                                    rw = [ev[hh]]
                                    if lazy_prev2 is not None:
                                        rw.append(lazy_prev2["dma"][hh])
                                    dm1 = arm.add("sync",
                                                  lambda g_, rs_=rs_, di=di:
                                                  g_.dma_start(
                                                      out=rbd_h[di:di + 1, :],
                                                      in_=rs_),
                                                  waits=rw, inc=16,
                                                  sem="dbo%d" % hh)
                                    out["dm1"].append(dm1)

                            def lazy_stage2(bspec, out):
                                h0b, h1b, gb, ev, lazy_prev2, par = bspec
                                for hh in (0, 1):
                                    o = par * 1024 + hh * 512
                                    rb = rbs[:, o:o + 512]
                                    di = par * 2 + hh
                                    dw2 = [out["dm1"][hh]]
                                    if lazy_prev2 is not None:
                                        dw2.append(lazy_prev2["mul"][hh])
                                    dmb = arm.add("sync",
                                                  lambda g_, rb=rb, di=di:
                                                  g_.dma_start(
                                                      out=rb,
                                                      in_=rbd_h[di:di + 1, :]
                                                      .broadcast_to((128, 512))),
                                                  waits=dw2, inc=16,
                                                  sem="dbr%d" % hh)
                                    out["dma"].append(dmb)

                            def lazy_stage3(bspec, out):
                                h0b, h1b, gb, ev, lazy_prev2, par = bspec
                                for hh, hid in ((0, h0b), (1, h1b)):
                                    o = par * 1024 + hh * 512
                                    rb = rbs[:, o:o + 512]
                                    au = aTu[:, o:o + 512]
                                    rec = arm.add(
                                        "vector", lambda v, rb=rb:
                                        v.reciprocal(rb, rb),
                                        waits=[out["dma"][hh]])
                                    mul = arm.add(
                                        "vector",
                                        lambda v, rb=rb, au=au, hid=hid, gb=gb:
                                        v.tensor_tensor(
                                            attnT[:, hid * 1024 + gb * 512:
                                                  hid * 1024 + (gb + 1) * 512],
                                            au, rb, ALU.mult),
                                        waits=[rec, ev[2 + hh]])
                                    out["mul"].append(mul)

                            def emit_lazy(bspec):
                                out = {"rec": [], "dm1": [], "dma": [], "mul": []}
                                lazy_stage1(bspec, out)
                                lazy_stage2(bspec, out)
                                lazy_stage3(bspec, out)
                                return out

                            def mk_av_last(kc, it):
                                def f(tn):
                                    buf = it % 4
                                    first = (it == 0)
                                    tn.matmul(psA[0][:],
                                              V[:, kc * 128:(kc + 1) * 128],
                                              pT[:, buf * 1024: buf * 1024 + 512],
                                              start=first, stop=True)
                                    return tn.matmul(
                                        psA[1][:], V[:, kc * 128:(kc + 1) * 128],
                                        pT[:, buf * 1024 + 512: buf * 1024 + 1024],
                                        start=first, stop=True)
                                return f

                            for it in range(Eg):
                                kc = kcl[it]
                                deps = list(bset) if (hp == 0 and g == 0 and it == 0) else []
                                if it >= 2:
                                    deps.append(exp_steps[it - 2])
                                elif hp > 0 or g > 0:
                                    deps.append(last_exps[it - 2])
                                mmask = it - (Eg - 4) if it >= Eg - 4 else None
                                if mmask is not None:
                                    deps.append(ld_mask)
                                s2_steps[it] = arm.add("tensor",
                                                       mk_s2(kc, it, mmask),
                                                       waits=deps)
                                edeps = [s2_steps[it]]
                                if it >= 4:
                                    edeps.append(cons_steps[it - 4])
                                    edeps.append(acc_steps[it - 4])
                                elif hp > 0 or g > 0:
                                    edeps.append(last_cons[it - 4])
                                    edeps.append(last_accs[it - 4])
                                if hp == 0 and g == 0 and it < 2:
                                    edeps += list(bset)
                                exp_steps[it] = arm.add("scalar", mk_exp(it),
                                                        waits=edeps)
                                post_steps[it] = exp_steps[it]
                                aw = [post_steps[it]]
                                if it == 0:
                                    if acc_war[bseq % 2] is not None:
                                        aw.append(acc_war[bseq % 2])
                                else:
                                    aw.append(acc_steps[it - 1])
                                acc_steps[it] = arm.add(
                                    "vector", mk_acc(it, bseq % 2), waits=aw)
                                if pending_lazy is not None:
                                    if it == 1:
                                        lazy_out = {"rec": [], "dm1": [],
                                                    "dma": [], "mul": []}
                                        lazy_stage1(pending_lazy, lazy_out)
                                    elif it == 2:
                                        lazy_stage2(pending_lazy, lazy_out)
                                    elif it == 3:
                                        lazy_stage3(pending_lazy, lazy_out)
                                        lazy_hist[pending_lazy[5]] = lazy_out
                                        pending_lazy = None
                                if it >= 1:
                                    cdeps = [post_steps[it - 1]]
                                    if it == 1 and prev_eager is not None:
                                        cdeps += prev_eager
                                    cons_steps[it - 1] = arm.add(
                                        "tensor", mk_cons(kcl[it - 1], it - 1),
                                        waits=cdeps)
                            av_last = arm.add("tensor", mk_av_last(kcl[Eg - 1],
                                                                   Eg - 1),
                                              waits=[post_steps[Eg - 1]])
                            cons_steps[Eg - 1] = av_last
                            rsw = [acc_steps[Eg - 1], ld_const[-1]]
                            if prev_eager is not None:
                                rsw += prev_eager[0:2]
                            rsm = arm.add("tensor", mk_rowsum(bseq % 2),
                                          waits=rsw)
                            acc_war[bseq % 2] = rsm
                            last_exps = exp_steps
                            last_cons = cons_steps
                            last_accs = acc_steps
                            par = bseq % 2
                            eager_ev = emit_eager(rsm, av_last,
                                                  lazy_hist[par], par)
                            prev_eager = eager_ev
                            pending_lazy = (h0, h1, g, eager_ev,
                                            lazy_hist[par], par)
                            bseq += 1
                    lazy_hist[pending_lazy[5]] = emit_lazy(pending_lazy)
                    pending_lazy = None
                    arms.append(arm)
                run_phase(p3, switch={c: arms[c] for c in range(4)})

                if debug:
                    pd = Phase("p3d")
                    dd = pd.add("sync", dma(dbg["attnT"][:, :], attnT[:]), inc=16,
                                sem="ddbg")
                    pd.add("sync", lambda s: None, waits=[dd], inc=0)
                    run_phase(pd)

            m3.close()
            sKVQ.close()
            x1 = sB.enter_context(nc.sbuf_tensor("x1", [128, 8 * E], F32))
            fst = sB.enter_context(nc.sbuf_tensor("fst", [128, 2 * 512], BF16))
            ost = sB.enter_context(nc.sbuf_tensor("ost", [128, 4 * 512], F32))
            m4 = sB1.enter_context(ExitStack())
            xos = m4.enter_context(nc.sbuf_tensor("xos", [128, 4 * 512], F32, side="right"))
            sqJ4 = m4.enter_context(nc.sbuf_tensor("sqJ4", [128, 512], F32, side="right"))
            wTs = m4.enter_context(nc.sbuf_tensor("wTs", [128, 4 * E], BF16, side="right"))
            xqb = sB.enter_context(nc.sbuf_tensor("xqb", [128, 2 * E], F32))
            with nc.psum_tensor("psF0", [128, 512], F32) as psF0, \
                 nc.psum_tensor("psF1", [128, 512], F32) as psF1, \
                 nc.psum_tensor("psF2", [128, 512], F32) as psF2, \
                 nc.psum_tensor("psF3", [128, 512], F32) as psF3, \
                 nc.psum_tensor("psW", [1, 128], F32) as psW:
                psF = (psF0, psF1, psF2, psF3)
                p4 = Phase("p4")
                ld_wfc = {0: ld_wfc01[0], 1: ld_wfc01[1]}
                evac = {}
                bank_prev = [None, None, None, None]
                xos_prev = [None, None, None, None]
                mm_by_j = {0: [], 1: [], 2: [], 3: []}
                stat_tail = {}
                stwar = [None] * 32
                sq4_prev = [None]
                wT_rc = wT_h.rearrange("(fc p) j -> fc p j", p=128)
                wam = {"buf": [None] * 4, "acc": None}

                def emit_wamax(fc):
                    wl = [wam["buf"][fc % 4]] if wam["buf"][fc % 4] else []
                    ldw = p4.add("sync", dma(
                        wTs[:, (fc % 4) * E:(fc % 4 + 1) * E], wT_rc[fc]),
                        waits=wl, inc=16, sem="dwt%d" % (fc % 4))
                    red = p4.add("vector", lambda v, fc=fc: v.tensor_reduce(
                        stv[:, 64:65], wTs[:, (fc % 4) * E:(fc % 4 + 1) * E],
                        AX.X, ALU.max, apply_absolute_value=True),
                        waits=[ldw] + ([wam["acc"]] if wam["acc"] else []))
                    if fc == 0:
                        acc = p4.add("vector", lambda v: v.tensor_copy(
                            amw[:, 0:1], stv[:, 64:65]), waits=[red])
                    else:
                        acc = p4.add("vector", lambda v: v.tensor_tensor(
                            amw[:, 0:1], amw[:, 0:1], stv[:, 64:65], ALU.max),
                            waits=[red, wam["acc"]])
                    wam["buf"][fc % 4] = red
                    wam["acc"] = acc

                sums8 = stv[:, 72:80]
                sumsq8 = stv[:, 80:88]
                mx8 = stv[:, 88:96]
                mn8 = stv[:, 96:104]
                combs = []

                def emit_comb(tt):
                    wsum = [evac[(j_, tt)] for j_ in range(4)]
                    wtail = [stat_tail[(j_, tt)] for j_ in range(4)]
                    combs.append(p4.add("vector", lambda v, tt=tt: v.tensor_reduce(
                        sums8[:, tt:tt + 1], stv[:, tt * 4:tt * 4 + 4], AX.X,
                        ALU.add), waits=wsum))
                    combs.append(p4.add("vector", lambda v, tt=tt: v.tensor_reduce(
                        sumsq8[:, tt:tt + 1], stv[:, 32 + tt * 4:36 + tt * 4],
                        AX.X, ALU.add), waits=[w[0] for w in wtail]))
                    combs.append(p4.add("vector", lambda v, tt=tt: v.tensor_reduce(
                        mx8[:, tt:tt + 1], stv[:, 136 + tt * 4:140 + tt * 4],
                        AX.X, ALU.max), waits=[w[1] for w in wtail]))
                    combs.append(p4.add("vector", lambda v, tt=tt: v.tensor_reduce(
                        mn8[:, tt:tt + 1], stv[:, 168 + tt * 4:172 + tt * 4],
                        AX.X, ALU.min), waits=[w[2] for w in wtail]))

                n = 0
                for j in range(4):
                    if j >= 2:
                        ld_wfc[j] = p4.add("gpsimd", dma(
                            wfcj_r[:, j % 2],
                            wfc_r[:, :, j * 512:(j + 1) * 512]),
                            waits=list(mm_by_j[j - 2]), inc=16,
                            sem="dwfc%d" % (j % 2))
                    for tt in range(8):
                        xw = [xos_prev[n % 4]] if xos_prev[n % 4] else []
                        xd = p4.add("sync", dma(
                            xos[:, (n % 4) * 512:(n % 4 + 1) * 512],
                            xo_h[tt * 128:(tt + 1) * 128, j * 512:(j + 1) * 512]),
                            waits=xw, inc=16, sem="dxo%d" % (n % 4))
                        deps = [ld_wfc[j], xd]
                        if bank_prev[n % 4] is not None:
                            deps.append(bank_prev[n % 4])
                        def fcmm(tn, j=j, tt=tt, bank=psF[n % 4]):
                            last = tn.matmul(bank[:], onesR[:],
                                             bfcb[0:1, j * 512:(j + 1) * 512],
                                             start=True, stop=False)
                            for ec in range(16):
                                last = tn.matmul(
                                    bank[:],
                                    attnT[:, ec * 1024 + tt * 128: ec * 1024 + (tt + 1) * 128],
                                    wfcj[:, (j % 2) * 8192 + ec * 512:
                                         (j % 2) * 8192 + (ec + 1) * 512],
                                    start=False, stop=(ec == 15))
                            return last
                        mm = p4.add("tensor", fcmm,
                                    waits=deps + [ld_const[-1], ld_bfcb])
                        mm_by_j[j].append(mm)
                        ev = p4.add("vector",
                                    lambda v, j=j, tt=tt, n=n, bank=psF[n % 4]:
                                    v.scalar_tensor_tensor(
                                        x1[:, tt * E + j * 512: tt * E + (j + 1) * 512],
                                        bank[:], 1.0,
                                        xos[:, (n % 4) * 512:(n % 4 + 1) * 512],
                                        ALU.mult, ALU.add,
                                        accum_out=stv[:, tt * 4 + j:tt * 4 + j + 1]),
                                    waits=[mm, xd] +
                                          ([stwar[tt * 4 + j]] if stwar[tt * 4 + j]
                                           else []))
                        evac[(j, tt)] = ev
                        x1sl = x1[:, tt * E + j * 512: tt * E + (j + 1) * 512]
                        sq_p = p4.add("scalar", lambda s, x1sl=x1sl, j=j, tt=tt:
                                      s.activation(sqJ4[:, :], x1sl, AF.Square,
                                                   accum_out=stv[:, 32 + tt * 4 + j:
                                                                 33 + tt * 4 + j]),
                                      waits=[ev] + ([sq4_prev[0]] if sq4_prev[0]
                                                    else []))
                        sq4_prev[0] = sq_p
                        rmx = p4.add("vector", lambda v, x1sl=x1sl, j=j, tt=tt:
                                     v.tensor_reduce(
                                         stv[:, 136 + tt * 4 + j:137 + tt * 4 + j],
                                         x1sl, AX.X, ALU.max), waits=[ev])
                        rmn = p4.add("vector", lambda v, x1sl=x1sl, j=j, tt=tt:
                                     v.tensor_reduce(
                                         stv[:, 168 + tt * 4 + j:169 + tt * 4 + j],
                                         x1sl, AX.X, ALU.min), waits=[ev])
                        stat_tail[(j, tt)] = (sq_p, rmx, rmn)
                        stwar[tt * 4 + j] = None
                        bank_prev[n % 4] = ev
                        xos_prev[n % 4] = ev
                        if n < 16:
                            emit_wamax(n)
                        if j == 3:
                            emit_comb(tt)
                        n += 1
                trw = p4.add("tensor", lambda tn: tn.transpose(
                    psW[0:1, :], amw[:, 0:1], idT[:]),
                    waits=[wam["acc"], ld_const[-1]])
                redw = p4.add("vector", lambda v: v.reduce_max(
                    s11[0:1, 1:2], psW[0:1, :], AX.X), waits=[trw])
                swst = p4.add("vector", lambda v: v.tensor_scalar(
                    s11[0:1, 2:3], s11[0:1, 1:2], 1.0 / 448.0, 1e-12,
                    ALU.mult, ALU.max), waits=[redw])
                rsw = p4.add("vector", lambda v: v.reciprocal(
                    s11[0:1, 3:4], s11[0:1, 2:3]), waits=[swst])
                i2sw = p4.add("vector", lambda v: v.tensor_scalar(
                    s11[0:1, 4:5], s11[0:1, 3:4], 0.5, None, ALU.mult),
                    waits=[rsw])
                bw = p4.add("tensor", lambda tn: tn.matmul(
                    psF0[:, 0:1], onesRf[:], s11[0:1, 4:5],
                    start=True, stop=True),
                    waits=[i2sw, evac[(0, 0)], bank_prev[0], ld_const[-1]])
                bwc = p4.add("vector", lambda v: v.tensor_copy(
                    colv[:, 1:2], psF0[:, 0:1]), waits=[bw])
                t8 = stv[:, 104:112]
                u8 = stv[:, 112:120]
                r08 = stv[:, 120:128]
                tmp8 = stv[:, 128:136]
                mu8 = mus[:, 0:8]
                r8 = mus[:, 8:16]
                c1 = p4.add("vector", lambda v: v.tensor_scalar(
                    mu8, sums8, 1.0 / 2048.0, None, ALU.mult), waits=combs)
                c2 = p4.add("vector", lambda v: v.tensor_scalar(
                    t8, sumsq8, 1.0 / 2048.0, None, ALU.mult), waits=combs)
                c3 = p4.add("vector", lambda v: v.tensor_tensor(
                    tmp8, mu8, mu8, ALU.mult), waits=[c1])
                c4 = p4.add("vector", lambda v: v.tensor_tensor(
                    t8, t8, tmp8, ALU.subtract), waits=[c2, c3])
                c5 = p4.add("vector", lambda v: v.tensor_scalar(
                    t8, t8, EPS, None, ALU.add), waits=[c4])
                c6 = p4.add("vector", lambda v: v.reciprocal(u8, t8), waits=[c5])
                c7 = p4.add("scalar", lambda s: s.activation(
                    r08, u8, AF.Sqrt), waits=[c6])
                c8 = p4.add("vector", lambda v: v.tensor_tensor(
                    tmp8, r08, r08, ALU.mult), waits=[c7])
                c9 = p4.add("vector", lambda v: v.tensor_tensor(
                    tmp8, tmp8, t8, ALU.mult), waits=[c8])
                c10 = p4.add("vector", lambda v: v.tensor_scalar(
                    tmp8, tmp8, -0.5, 1.5, ALU.mult, ALU.add), waits=[c9])
                c11 = p4.add("vector", lambda v: v.tensor_tensor(
                    r8, r08, tmp8, ALU.mult), waits=[c10])
                a1 = p4.add("vector", lambda v: v.tensor_tensor(
                    mx8, mx8, mu8, ALU.subtract), waits=[c11])
                a2 = p4.add("vector", lambda v: v.tensor_tensor(
                    mn8, mu8, mn8, ALU.subtract), waits=[c11, a1])
                a3 = p4.add("vector", lambda v: v.tensor_tensor(
                    mx8, mx8, mn8, ALU.max), waits=[a2])
                a4 = p4.add("vector", lambda v: v.tensor_tensor(
                    mx8, mx8, r8, ALU.mult), waits=[a3])
                a5 = p4.add("vector", lambda v: v.tensor_reduce(
                    amx[:, 0:1], mx8, AX.X, ALU.max), waits=[a4])
                trx = p4.add("tensor", lambda tn: tn.transpose(
                    psW[0:1, :], amx[:, 0:1], idT[:]),
                    waits=[a5, redw, ld_const[-1]])
                redx = p4.add("vector", lambda v: v.reduce_max(
                    s11[0:1, 5:6], psW[0:1, :], AX.X), waits=[trx])
                nm8a = p4.add("vector", lambda v: v.tensor_tensor(
                    stv[:, 72:80], mu8, r8, ALU.mult), waits=[c11])
                nm8b = p4.add("vector", lambda v: v.tensor_scalar(
                    stv[:, 72:80], stv[:, 72:80], -1.0, None, ALU.mult),
                    waits=[nm8a])
                q1_steps = {}
                for tt in range(2):
                    q1_steps[tt] = p4.add(
                        "scalar", lambda s, tt=tt: s.activation(
                            xqb[:, (tt % 2) * E:(tt % 2 + 1) * E],
                            x1[:, tt * E:(tt + 1) * E], AF.Identity,
                            bias=stv[:, 72 + tt:73 + tt],
                            scale=mus[:, 8 + tt:9 + tt]), waits=[nm8b])
                if debug:
                    dd4 = p4.add("sync", dma(dbg["x1"][:, :], x1[:]),
                                 waits=[evac[(j, tt)] for j in range(4)
                                        for tt in range(8)], inc=16, sem="ddbg")
                    p4.add("sync", lambda s: None, waits=[dd4], inc=0)
                run_phase(p4)

            sB1.close()
            with ExitStack() as sC:
                lnqT = sC.enter_context(nc.sbuf_tensor("lnqT", [128, 16 * 1024], F8, side="right"))
                wq8 = sC.enter_context(nc.sbuf_tensor("wq8", [128, 4 * 16 * 512], F8, side="right"))
                wqj = sC.enter_context(nc.sbuf_tensor("wqj", [128, 2 * 16 * 512], BF16, side="right"))
                with nc.psum_tensor("psQ0", [128, 1024], F32) as psQ0, \
                     nc.psum_tensor("psQ1", [128, 1024], F32) as psQ1, \
                     nc.psum_tensor("psN0", [128, 512], F32) as psN0, \
                     nc.psum_tensor("psN1", [128, 512], F32) as psN1, \
                     nc.psum_tensor("psN2", [128, 512], F32) as psN2, \
                     nc.psum_tensor("psN3", [128, 512], F32) as psN3:
                    psQ = (psQ0, psQ1)
                    psN = (psN0, psN1, psN2, psN3)
                    p5 = Phase("p5")
                    wT_rj = wT_h.rearrange("(fc p) j -> p fc j", p=128)
                    wqj_r = wqj[:].rearrange("p (b fc j) -> p b fc j", b=2, fc=16)
                    ld_wTj = {}
                    wq_steps = {}
                    for j in range(4):
                        wl = [] if j < 2 else [wq_steps[j - 2]]
                        ld_wTj[j] = p5.add("sync", dma(
                            wqj_r[:, j % 2], wT_rj[:, :, j * 512:(j + 1) * 512]),
                            waits=wl, inc=16, sem="dwj%d" % (j % 2))
                        if j % 2 == 0:
                            wq1 = p5.add("scalar", lambda s, j=j: s.activation(
                                wq8[:, j * 8192:(j + 1) * 8192],
                                wqj[:, (j % 2) * 8192:(j % 2) * 8192 + 8192],
                                AF.Identity, scale=colv[:, 1:2]),
                                waits=[ld_wTj[j], bwc])
                        else:
                            wq1 = p5.add("vector", lambda v, j=j: v.tensor_scalar(
                                wq8[:, j * 8192:(j + 1) * 8192],
                                wqj[:, (j % 2) * 8192:(j % 2) * 8192 + 8192],
                                colv[:, 1:2], None, ALU.mult),
                                waits=[ld_wTj[j]])
                        wq_steps[j] = wq1
                    tox = p5.add("gpsimd", dma(cc_in[:, :], s11[0:1, 5:6]),
                                 waits=[redx], inc=16, sem="dcci")
                    cc = p5.add("gpsimd", lambda gp: gp.collective_compute(
                        "AllReduce", ALU.max, replica_groups=[list(range(8))],
                        ins=[cc_in[:, :]], outs=[cc_out[:, :]]), waits=[tox])
                    fromx = p5.add("gpsimd", dma(s11[0:1, 6:7], cc_out[:, :]),
                                   waits=[cc], inc=16, sem="dcco")
                    sxst = p5.add("vector", lambda v: v.tensor_scalar(
                        s11[0:1, 7:8], s11[0:1, 6:7], 1.0 / 448.0, 1e-12,
                        ALU.mult, ALU.max), waits=[fromx])
                    rsx = p5.add("vector", lambda v: v.reciprocal(
                        s11[0:1, 8:9], s11[0:1, 7:8]), waits=[sxst])
                    i2sx = p5.add("vector", lambda v: v.tensor_scalar(
                        s11[0:1, 9:10], s11[0:1, 8:9], 0.5, None, ALU.mult),
                        waits=[rsx])
                    sc4a = p5.add("vector", lambda v: v.tensor_tensor(
                        s11[0:1, 10:11], s11[0:1, 7:8], s11[0:1, 2:3], ALU.mult),
                        waits=[sxst, swst])
                    sc4b = p5.add("vector", lambda v: v.tensor_scalar(
                        s11[0:1, 11:12], s11[0:1, 10:11], 4.0, None, ALU.mult),
                        waits=[sc4a])
                    dmW1 = p5.add("sync", lambda g_: g_.dma_start(
                        out=scb_h[0:1, 0:1], in_=s11[0:1, 9:10]),
                        waits=[i2sx], inc=16, sem="dbo0")
                    dmW2 = p5.add("sync", lambda g_: g_.dma_start(
                        out=scb_h[0:1, 1:2], in_=s11[0:1, 11:12]),
                        waits=[sc4b], inc=16, sem="dbo1")
                    bxc = p5.add("sync", lambda g_: g_.dma_start(
                        out=colv[:, 0:1], in_=scb_h[0:1, 0:1]
                        .broadcast_to((128, 1))), waits=[dmW1], inc=16,
                        sem="dbr0")
                    bxc2 = p5.add("sync", lambda g_: g_.dma_start(
                        out=colv[:, 2:3], in_=scb_h[0:1, 1:2]
                        .broadcast_to((128, 1))), waits=[dmW2], inc=16,
                        sem="dbr1")
                    lnq_done = {}
                    tr_by_tt = {}
                    bank_ev5 = [None, None]
                    for tt in range(8):
                        if tt < 2:
                            q1 = q1_steps[tt]
                        else:
                            q1 = p5.add("scalar", lambda s, tt=tt:
                                        s.activation(
                                            xqb[:, (tt % 2) * E:
                                                (tt % 2 + 1) * E],
                                            x1[:, tt * E:(tt + 1) * E],
                                            AF.Identity,
                                            bias=stv[:, 72 + tt:73 + tt],
                                            scale=mus[:, 8 + tt:9 + tt]),
                                        waits=[nm8b, tr_by_tt[tt - 2]])
                        evs = []
                        for ss_ in range(2):
                            bank = psQ[ss_]
                            deps = [q1, ld_const[-1]]
                            if bank_ev5[ss_] is not None:
                                deps.append(bank_ev5[ss_])
                            def tre5(tn, ss_=ss_, tt=tt, bank=bank):
                                last = None
                                for q in range(8):
                                    fc = ss_ * 8 + q
                                    last = tn.transpose(
                                        bank[:, q * 128:(q + 1) * 128],
                                        xqb[:, (tt % 2) * E + fc * 128:
                                            (tt % 2) * E + (fc + 1) * 128],
                                        idT[:])
                                return last
                            st = p5.add("tensor", tre5, waits=deps)
                            tr_by_tt[tt] = st
                            def eva5(v, ss_=ss_, tt=tt, bank=bank):
                                dstv = lnqT[:].rearrange("p (fc t) -> p fc t",
                                                         fc=16)
                                return v.tensor_scalar(
                                    dstv[:, ss_ * 8:(ss_ + 1) * 8,
                                         tt * 128:(tt + 1) * 128],
                                    bank[:].rearrange("p (q t) -> p q t", q=8),
                                    colv[:, 0:1], None, ALU.mult)
                            e = p5.add("vector", eva5, waits=[st, bxc])
                            bank_ev5[ss_] = e
                            evs.append(e)
                        lnq_done[tt] = evs
                    out_dmas = []
                    ffn_ev = {}
                    mm_lastj = {}
                    nn = 0
                    for j in range(4):
                        for tt in range(8):
                            deps = [wq_steps[j]] + list(lnq_done[tt])
                            if nn >= 4:
                                deps.append(ffn_ev[nn - 4])
                            def fmm(tn, j=j, tt=tt, bank=psN[nn % 4]):
                                last = None
                                lq = lnqT[:].rearrange("p (fc t) -> p fc t",
                                                       fc=16)
                                wv_ = wq8[:].rearrange("p (b fc jj) -> p b fc jj",
                                                       b=4, fc=16)
                                for fp in range(8):
                                    last = tn.matmul(
                                        bank[:],
                                        lq[:, 2 * fp:2 * fp + 2,
                                           tt * 128:(tt + 1) * 128],
                                        wv_[:, j, 2 * fp:2 * fp + 2, :],
                                        start=(fp == 0), stop=(fp == 7),
                                        perf_mode=mybir.MatmulPerfMode.DoubleRow)
                                return last
                            mm = p5.add("tensor", fmm, waits=deps)
                            if tt == 7:
                                mm_lastj[j] = mm
                            f1 = p5.add("vector",
                                        lambda v, nn=nn, bank=psN[nn % 4]:
                                        v.tensor_scalar(
                                            fst[:, (nn % 2) * 512:(nn % 2 + 1) * 512],
                                            bank[:], colv[:, 2:3], None, ALU.mult),
                                        waits=[mm, bxc2] +
                                              ([ffn_ev[nn - 2]] if nn >= 2 else []))
                            f2 = p5.add("vector", lambda v, nn=nn, j=j, tt=tt:
                                        v.tensor_tensor(
                                            ost[:, (nn % 4) * 512:(nn % 4 + 1) * 512],
                                            fst[:, (nn % 2) * 512:(nn % 2 + 1) * 512],
                                            x1[:, tt * E + j * 512: tt * E + (j + 1) * 512],
                                            ALU.add),
                                        waits=[f1] + ([out_dmas[nn - 4]]
                                                      if nn >= 4 else []))
                            ffn_ev[nn] = f2
                            od = p5.add("sync", dma(
                                y_h[tt * 128:(tt + 1) * 128, j * 512:(j + 1) * 512],
                                ost[:, (nn % 4) * 512:(nn % 4 + 1) * 512]),
                                waits=[f2], inc=16, sem="dos%d" % (nn % 4))
                            out_dmas.append(od)
                            nn += 1
                    p5.add("sync", lambda s: None,
                           waits=out_dmas[-4:], inc=0)
                    run_phase(p5)

    return nc


_NC_CACHE = {}


def _get_nc(debug=False):
    key = bool(debug)
    if key not in _NC_CACHE:
        _NC_CACHE[key] = build_nc(debug=key)
    return _NC_CACHE[key]


def make_in_maps(inputs):
    x = np.asarray(inputs["L0_attn_x"])
    w_qkv = np.ascontiguousarray(np.asarray(inputs["L0_attn_w_qkv"]))
    b_qkv = np.asarray(inputs["L0_attn_b_qkv"])
    w_fc = np.ascontiguousarray(np.asarray(inputs["L0_attn_w_fc"]))
    b_fc = np.asarray(inputs["L0_attn_b_fc"])
    w_ffn = np.asarray(inputs["L0_ffn_weight"])
    if w_ffn.dtype == np.uint16:
        w_ffn = w_ffn.view(ml_dtypes.bfloat16)

    ident = np.eye(128, dtype=np.float32)
    identb = np.eye(128, dtype=ml_dtypes.bfloat16)
    ident8 = np.eye(128, dtype=ml_dtypes.float8_e4m3)
    onesR = np.ones((1, 128), dtype=ml_dtypes.bfloat16)
    onesC = np.ones((128, 1), dtype=ml_dtypes.bfloat16)
    onesRf = np.ones((1, 128), dtype=np.float32)
    masks = np.zeros((128, 4, 512), dtype=ml_dtypes.bfloat16)
    ii = np.arange(128)[:, None]
    jj = np.arange(512)[None, :]
    for m in range(4):
        masks[:, m, :] = np.where((m * 128 + ii) <= jj, 0.0, -2000.0)
    masks = np.ascontiguousarray(masks.reshape(128, 2048))

    w_q = np.ascontiguousarray(w_qkv[:, :E])
    w_kv = np.ascontiguousarray(w_qkv[:, E:])
    b_q = np.ascontiguousarray(b_qkv[:E].reshape(H, D).T)
    b_k = np.ascontiguousarray(b_qkv[E:E + D].reshape(D, 1))
    b_v = np.ascontiguousarray(b_qkv[E + D:].reshape(D, 1))
    b_fc2 = np.ascontiguousarray(b_fc.reshape(1, E))
    wT = np.ascontiguousarray(w_ffn.T)

    in_maps = []
    for pid in range(8):
        bi, c = divmod(pid, 4)
        xo = np.concatenate([x[bi, c * 512:(c + 1) * 512],
                             x[bi, (7 - c) * 512:(8 - c) * 512]], axis=0)
        perm = [p for p in range(8) if p not in (c, 7 - c)] + [7 - c, c]
        xp = x[bi].reshape(8, 512, E)[perm].reshape(4096, E)
        in_maps.append({
            "xp": np.ascontiguousarray(xp),
            "xo": np.ascontiguousarray(xo),
            "w_q": w_q, "w_kv": w_kv, "b_q": b_q, "b_k": b_k, "b_v": b_v,
            "w_fc": w_fc, "b_fc": b_fc2, "w_ffnT": wT,
            "ident": ident, "identb": identb, "ident8": ident8,
            "onesR": onesR, "onesC": onesC, "onesRf": onesRf,
            "masks": masks,
        })
    return in_maps


def kernel(**inputs):
    in_maps = make_in_maps(inputs)
    nc = _get_nc(debug=False)
    res = run_bass_kernel_spmd(nc, in_maps, core_ids=list(range(8)))
    out = np.empty((2, 4096, 2048), dtype=np.float32)
    for pid in range(8):
        bi, c = divmod(pid, 4)
        y = res.results[pid]["y"]
        out[bi, c * 512:(c + 1) * 512] = y[:512]
        out[bi, (7 - c) * 512:(8 - c) * 512] = y[512:]
    return out
